# version 01
# baseline (speedup 1.0000x reference)
import numpy as np
from contextlib import ExitStack

import concourse.bass as bass
import concourse.mybir as mybir
from concourse.bass_utils import run_bass_kernel_spmd

F32 = mybir.dt.float32
BF16 = mybir.dt.bfloat16
AF = mybir.ActivationFunctionType
ALU = mybir.AluOpType
AX = mybir.AxisListType.X

NCORES = 8
T = 2048
D = 1024
DCH = 8
NTB = 4
NTC = 16
NH = 16

ENGS = ("tensor", "vector", "scalar", "gpsimd", "sync")


class Plan:

    def __init__(self, mk):
        self.mk = mk
        self.steps = {e: [] for e in ENGS}

    SELF_SEM = {"vector": "ve", "scalar": "ae"}

    def add(self, eng, fn, waits=None, inc=None, inc_amt=None):
        waits = list(waits or [])
        s = self.SELF_SEM.get(eng)
        if s is not None and self.mk.c[s] > 0:
            waits.append((s, self.mk.c[s]))
        if inc is not None and inc_amt is None:
            inc_amt = 16 if inc in ("wd", "xd") else 1
        if inc is not None:
            self.mk.c[inc] += inc_amt
        self.steps[eng].append((fn, waits, inc, inc_amt))
        return self.mk.c[inc] if inc else None

    def wait(self, eng, waits):
        self.steps[eng].append((None, list(waits), None, None))

    def emit(self):
        mk = self.mk
        with mk.nc.Block() as block:
            for eng in ENGS:
                if not self.steps[eng]:
                    continue

                def make(eng):
                    def run(proxy):
                        for fn, waits, inc, amt in self.steps[eng]:
                            for (sname, val) in waits:
                                if val and val > 0:
                                    proxy.wait_ge(mk.s[sname], val)
                            if fn is not None:
                                inst = fn(proxy)
                                if inc is not None:
                                    inst.then_inc(mk.s[inc], amt)
                    return run

                getattr(block, eng)(make(eng))


class MK:
    def __init__(self, upto=99):
        self.upto = upto
        nc = bass.Bass("TRN2", target_bir_lowering=False)
        self.nc = nc
        self.es = ExitStack()
        es = self.es

        dr = lambda name, shape: nc.dram_tensor(name, list(shape), F32,
                                                kind="ExternalInput")
        self.d = d = {}
        d["xT"] = dr("xT", (D, T))
        d["c_ident"] = dr("c_ident", (128, 128))
        d["c_sel16"] = dr("c_sel16", (16, 1024))
        for w in ("l0_wq", "l0_wk", "l0_wv", "l0_wo",
                  "l1_wq", "l1_wk", "l1_wv", "l1_wo",
                  "l0_sg", "l0_su", "l0_sd"):
            d[w] = dr(w, (D, D))
        d["l0_bq"] = dr("l0_bq", (128, 8))
        d["l0_bo"] = dr("l0_bo", (128, 8))
        d["l0_bk"] = dr("l0_bk", (1, D))
        d["l0_bv"] = dr("l0_bv", (1, D))
        d["l0_gw"] = dr("l0_gw", (D, 32))
        d["l0_gb"] = dr("l0_gb", (1, 32))
        d["l0_eg"] = dr("l0_eg", (32, D, 512))
        d["l0_eu"] = dr("l0_eu", (32, D, 512))
        d["l0_ed"] = dr("l0_ed", (32, 512, D))
        d["l1_gw"] = dr("l1_gw", (D, 8))
        d["l1_gb"] = dr("l1_gb", (1, 8))
        d["l1_w1"] = dr("l1_w1", (9, D, 4096))
        d["l1_w3"] = dr("l1_w3", (9, D, 4096))
        d["l1_w2"] = dr("l1_w2", (9, 2048, D))
        d["l1_b1"] = dr("l1_b1", (9, 128, 32))
        d["l1_b3"] = dr("l1_b3", (9, 128, 32))
        d["l1_b2"] = dr("l1_b2", (9, 1, D))
        self.out = nc.dram_tensor("out", [D, T], F32, kind="ExternalOutput")
        self.cc_in = [nc.dram_tensor(f"cc_in{i}", [128, 520], F32) for i in range(2)]
        self.cc_out = [nc.dram_tensor(f"cc_out{i}", [128, 520], F32) for i in range(2)]

        sb = lambda name, shape, dt: es.enter_context(
            nc.sbuf_tensor(name, list(shape), dt))
        self.stream = sb("stream", (128, DCH * T), F32)
        self.xb = sb("xb", (128, DCH * T), BF16)
        self.A = sb("A", (128, DCH * T), BF16)
        self.W = sb("W", (128, 16384), BF16)
        self.Bx = sb("Bx", (128, 12288), BF16)
        self.scrF = sb("scrF", (128, 2112), F32)
        self.scrB = sb("scrB", (128, 4416), BF16)
        self.rowB = sb("rowB", (1, 2048), BF16)
        self.ident = sb("ident", (128, 128), BF16)
        self.ones_col = sb("ones_col", (128, 1), BF16)
        self.ones_row = sb("ones_row", (1, 768), BF16)

        self.s = {}
        self.c = {}
        for name in ("pe", "ve", "ae", "ge", "wd", "xd", "cc"):
            self.s[name] = es.enter_context(nc.semaphore(name))
            self.c[name] = 0

        self.w_free_W = 0
        self.w_free_Bx = 0
        self.fence = {"pe": 0, "ve": 0, "ae": 0}
        self.v_stream = ("xd", 0)
        self.v_xb = ("wd", 0)

    def stv(self, buf, c, lo, n):
        return buf[:, c * T + lo: c * T + lo + n]

    def new_plan(self):
        p = Plan(self)
        f = {k: self.c[k] for k in ("pe", "ve", "ae")}
        p.wait("tensor", [("ve", f["ve"]), ("ae", f["ae"])])
        p.wait("vector", [("pe", f["pe"]), ("ae", f["ae"])])
        p.wait("scalar", [("pe", f["pe"]), ("ve", f["ve"])])
        self.fence = f
        return p

    def wload(self, p, dst_ap, src_ap, wait_pe=0, extra_waits=None):
        waits = list(extra_waits or [])
        if wait_pe:
            waits.append(("pe", wait_pe))
        if len(src_ap.ap) == 3 and len(dst_ap.ap) == 2:
            n = src_ap.ap[2][1]
            dst_ap = dst_ap.rearrange("p (r n) -> p r n", n=n)
        p.add("gpsimd", lambda e, dd=dst_ap, ss=src_ap: e.dma_start(out=dd, in_=ss),
              waits=waits, inc="wd")
        return self.c["wd"]

    def chunked(self, ap):
        return ap.rearrange("(r p) n -> p r n", p=128)

    def fap(self, buf, lo, n, dims):
        base = buf[:, lo: lo + n]
        return bass.AP(base.tensor, base.offset, [base.ap[0]] + list(dims))

    def ph_load(self):
        p = Plan(self)
        p.add("vector", lambda e: e.memset(self.ones_col[:, :], 1.0), inc="ve")
        p.add("vector", lambda e: e.memset(self.ones_row[0:1, 0:128], 1.0), inc="ve")
        p.add("vector", lambda e: e.memset(self.ones_row[0:1, 128:256], -1.0), inc="ve")
        p.add("vector", lambda e: e.memset(self.ones_row[0:1, 256:768], 1.0), inc="ve")
        p.add("vector", lambda e: e.memset(self.scrF[0:1, 1552:1680], 1.0), inc="ve")
        self.v_consts = self.c["ve"]
        xap = self.d["xT"].ap().rearrange("(c p) t -> p c t", p=128)
        st3 = self.stream[:, :].rearrange("p (c t) -> p c t", t=T)
        xb3 = self.xb[:, :].rearrange("p (c t) -> p c t", t=T)
        p.add("sync", lambda e: e.dma_start(out=st3, in_=xap), inc="xd")
        self.v_stream = ("xd", self.c["xd"])
        p.add("gpsimd", lambda e: e.dma_start(out=xb3, in_=xap), inc="wd")
        self.v_xb = ("wd", self.c["wd"])
        p.add("gpsimd", lambda e: e.dma_start(out=self.ident[:, :],
                                              in_=self.d["c_ident"].ap()), inc="wd")
        self.v_ident = ("wd", self.c["wd"])
        p.emit()

    def ph_attn(self, L):
        nc = self.nc
        has_bias = (L == 0)
        es2 = ExitStack()
        psK0 = es2.enter_context(nc.psum_tensor(f"psK0_{L}", [128, 512], F32))
        psK1 = es2.enter_context(nc.psum_tensor(f"psK1_{L}", [128, 512], F32))
        psV0 = es2.enter_context(nc.psum_tensor(f"psV0_{L}", [128, 512], F32))
        psV1 = es2.enter_context(nc.psum_tensor(f"psV1_{L}", [128, 512], F32))
        es_kv = ExitStack()
        kvA = [es_kv.enter_context(nc.psum_tensor(f"kvA{i}_{L}", [128, 260], F32))
               for i in range(2)]
        kvB = [es_kv.enter_context(nc.psum_tensor(f"kvB{i}_{L}", [128, 260], F32))
               for i in range(2)]

        pre = "l0_" if L == 0 else "l1_"
        KFM, VAUG, BDG, KVB = 0, 2048, 1024, 1152

        p = self.new_plan()
        fence_pe = self.fence["pe"]
        v_wk = self.wload(p, self.W[:, 0:8192],
                          self.chunked(self.d[pre + "wk"].ap()), wait_pe=self.w_free_W)
        v_wv = self.wload(p, self.W[:, 8192:16384],
                          self.chunked(self.d[pre + "wv"].ap()))
        v_wq = self.wload(p, self.Bx[:, 0:8192],
                          self.chunked(self.d[pre + "wq"].ap()), wait_pe=self.w_free_Bx)
        v_bk = v_bv = v_bq = v_bo = 0
        if has_bias:
            v_bk = self.wload(p, self.rowB[0:1, 0:1024], self.d["l0_bk"].ap())
            v_bv = self.wload(p, self.rowB[0:1, 1024:2048], self.d["l0_bv"].ap())
            p.add("sync", lambda e: e.dma_start(out=self.scrF[:, 1712:1720],
                                                in_=self.d["l0_bq"].ap()), inc="xd")
            v_bq = self.c["xd"]
            p.add("sync", lambda e: e.dma_start(out=self.scrF[:, 1720:1728],
                                                in_=self.d["l0_bo"].ap()), inc="xd")
            v_bo = self.c["xd"]
        sel_ap = self.scrF[0:16, 0:1024]

        p.add("vector", lambda e: e.memset(self.scrB[:, VAUG:VAUG + 1040], 1.0),
              inc="ve")
        v_vaug_ones = self.c["ve"]
        xb_sem, xb_val = self.v_xb

        fm_done = {}
        vaug_done = {}
        kvacc_done = {}
        for tc in range(NTC):
            kb = KFM + (tc % 2) * 1024
            kbuf = self.scrB[:, kb:kb + 1024]

            def mk_k(e, tc=tc, half=0, ps=psK0):
                for r in range(DCH):
                    inst = e.matmul(ps[:, :],
                                    self.stv(self.xb, r, tc * 128, 128),
                                    self.W[:, r * 1024 + half * 512:
                                           r * 1024 + half * 512 + 512],
                                    start=(r == 0),
                                    stop=(r == DCH - 1 and not has_bias))
                if has_bias:
                    inst = e.matmul(ps[:, :], self.ones_row[0:1, 0:128],
                                    self.rowB[0:1, half * 512: half * 512 + 512],
                                    start=False, stop=True)
                return inst

            waits = [("wd", max(v_wk, xb_val if xb_sem == "wd" else 0, v_bk)),
                     ("ve", self.v_consts)]
            if xb_sem == "ae":
                waits.append(("ae", xb_val))
            if tc >= 2:
                waits.append(("ve", fm_done[tc - 2][0]))
            p.add("tensor", lambda e, tc=tc: mk_k(e, tc, 0, psK0), waits=waits,
                  inc="pe")
            p.add("tensor", lambda e, tc=tc: mk_k(e, tc, 1, psK1),
                  waits=[("ve", fm_done[tc - 2][1])] if tc >= 2 else [], inc="pe")
            v_k_mm = self.c["pe"]

            def mk_v(e, tc=tc, half=0, ps=psV0):
                for r in range(DCH):
                    inst = e.matmul(ps[:, :],
                                    self.stv(self.xb, r, tc * 128, 128),
                                    self.W[:, 8192 + r * 1024 + half * 512:
                                           8192 + r * 1024 + half * 512 + 512],
                                    start=(r == 0),
                                    stop=(r == DCH - 1 and not has_bias))
                if has_bias:
                    inst = e.matmul(ps[:, :], self.ones_row[0:1, 0:128],
                                    self.rowB[0:1, 1024 + half * 512:
                                              1024 + half * 512 + 512],
                                    start=False, stop=True)
                return inst

            vwaits = [("wd", max(v_wv, v_bv))]
            if tc >= 1:
                vwaits.append(("ae", vaug_done[tc - 1]))
            p.add("tensor", lambda e, tc=tc: mk_v(e, tc, 0, psV0), waits=vwaits,
                  inc="pe")
            p.add("tensor", lambda e, tc=tc: mk_v(e, tc, 1, psV1), inc="pe")
            v_v_mm = self.c["pe"]

            fd = []
            for half, ps in ((0, psK0), (1, psK1)):
                s0 = self.scrF[:, 0:512]
                s1 = self.scrF[:, 512:1024]
                kout = kbuf[:, half * 512: half * 512 + 512]
                p.add("vector", lambda e, ps=ps, s0=s0: e.tensor_scalar(
                    s0, ps[:, :], 0.0, None, ALU.min),
                    waits=[("pe", v_k_mm)], inc="ve")
                p.add("scalar", lambda e, s0=s0, s1=s1: e.activation(s1, s0, AF.Exp),
                      waits=[("ve", self.c["ve"])], inc="ae")
                p.add("vector", lambda e, ps=ps, s1=s1, kout=kout:
                      e.scalar_tensor_tensor(kout, ps[:, :], 0.0, s1,
                                             ALU.max, ALU.add),
                      waits=[("ae", self.c["ae"])], inc="ve")
                fd.append(self.c["ve"])
            fm_done[tc] = fd

            for half, ps in ((0, psV0), (1, psV1)):
                def mk_vc(e, half=half, ps=ps):
                    src = bass.AP(ps.ap().tensor, ps.ap().offset,
                                  [ps.ap().ap[0], [64, 8], [1, 64]])
                    dst = self.fap(self.scrB, VAUG + half * 520, 520,
                                   [[65, 8], [1, 64]])
                    return e.activation(dst, src, AF.Identity)
                p.add("scalar", mk_vc,
                      waits=[("pe", v_v_mm), ("ve", v_vaug_ones)], inc="ae")
            vaug_done[tc] = self.c["ae"]

            def mk_kv(e, tc=tc, kbuf=kbuf):
                for h in range(NH):
                    pair, base = h // 2, (h % 2) * 64
                    ps = (kvA if pair < 4 else kvB)[tc % 2]
                    col = (pair % 4) * 65
                    inst = e.matmul(ps[base:base + 64, col:col + 65],
                                    kbuf[:, h * 64: h * 64 + 64],
                                    self.scrB[:, VAUG + h * 65: VAUG + h * 65 + 65],
                                    start=(h in (0, 1, 8, 9)),
                                    stop=(h in (6, 7, 14, 15)),
                                    skip_group_check=True)
                return inst
            kvw = [("ve", fd[1]), ("ae", vaug_done[tc])]
            if tc >= 2:
                kvw.append(("ve", kvacc_done[tc - 2]))
            p.add("tensor", mk_kv, waits=kvw, inc="pe")
            v_kvmm_tc = self.c["pe"]
            kvsb_t = self.scrF[:, 1024:1544]
            if tc == 0:
                p.add("vector", lambda e, tc=tc: e.tensor_copy(
                    kvsb_t[:, 0:260], kvA[tc % 2][:, :]),
                    waits=[("pe", v_kvmm_tc)], inc="ve")
                p.add("vector", lambda e, tc=tc: e.tensor_copy(
                    kvsb_t[:, 260:520], kvB[tc % 2][:, :]), inc="ve")
            else:
                p.add("vector", lambda e, tc=tc: e.tensor_tensor(
                    kvsb_t[:, 0:260], kvA[tc % 2][:, :], kvsb_t[:, 0:260],
                    ALU.add), waits=[("pe", v_kvmm_tc)], inc="ve")
                p.add("vector", lambda e, tc=tc: e.tensor_tensor(
                    kvsb_t[:, 260:520], kvB[tc % 2][:, :], kvsb_t[:, 260:520],
                    ALU.add), inc="ve")
            kvacc_done[tc] = self.c["ve"]
        v_kv_mm = self.c["pe"]

        kvsb = self.scrF[:, 1024:1544]
        v_kv_sb = kvacc_done[NTC - 1]
        if getattr(self, "dbg_attn", 0) == 2:
            p.add("vector", lambda e: e.tensor_copy(self.stream[:, 0:520], kvsb),
                  waits=[("ve", v_kv_sb)], inc="ve")
            v_dloc = self.c["ve"]
        p.add("sync", lambda e: e.dma_start(out=self.cc_in[L][:, :], in_=kvsb),
              waits=[("ve", v_kv_sb)], inc="xd")
        v_bounce = self.c["xd"]
        p.add("gpsimd", lambda e: e.collective_compute(
            "AllReduce", ALU.add, replica_groups=[[0, 1], [2, 3], [4, 5], [6, 7]],
            ins=[self.cc_in[L].ap().opt()], outs=[self.cc_out[L].ap().opt()]),
            waits=[("xd", v_bounce)], inc="cc")
        v_cc = self.c["cc"]
        p.add("sync", lambda e: e.dma_start(out=kvsb, in_=self.cc_out[L][:, :]),
              waits=[("cc", v_cc)], inc="xd")
        v_kv_back = self.c["xd"]

        es_kv.close()
        psQ0 = es2.enter_context(nc.psum_tensor(f"psQ0_{L}", [128, 512], F32))
        psQ1 = es2.enter_context(nc.psum_tensor(f"psQ1_{L}", [128, 512], F32))
        q_evac = {}
        for i, (c, tb) in enumerate((c, tb) for c in range(DCH) for tb in range(NTB)):
            ps = psQ0 if i % 2 == 0 else psQ1

            def mk_q(e, c=c, tb=tb, ps=ps):
                for r in range(DCH):
                    inst = e.matmul(ps[:, :],
                                    self.Bx[:, r * 1024 + c * 128:
                                            r * 1024 + c * 128 + 128],
                                    self.stv(self.xb, r, tb * 512, 512),
                                    start=(r == 0), stop=(r == DCH - 1))
                return inst
            waits = [("wd", v_wq)]
            if i < 2:
                waits.append(("ve", v_kv_sb))
            if i >= 2:
                waits.append(("ve", q_evac[i - 2]))
            p.add("tensor", mk_q, waits=waits, inc="pe")
            v_q_mm = self.c["pe"]
            s0, s1 = self.scrF[:, 0:512], self.scrF[:, 512:1024]
            qout = self.stv(self.A, c, tb * 512, 512)
            if has_bias:
                bq = self.scrF[:, 1712 + c: 1713 + c]
                p.add("vector", lambda e, ps=ps, s0=s0, bq=bq: e.tensor_scalar(
                    s0, ps[:, :], bq, 0.0, ALU.add, ALU.min),
                    waits=[("pe", v_q_mm), ("xd", v_bq)], inc="ve")
                p.add("scalar", lambda e, s0=s0, s1=s1: e.activation(s1, s0, AF.Exp),
                      waits=[("ve", self.c["ve"])], inc="ae")
                p.add("scalar", lambda e, ps=ps, s0=s0, bq=bq: e.activation(
                    s0, ps[:, :], AF.Relu, bias=bq), inc="ae")
                p.add("vector", lambda e, s1=s1, s0=s0, qout=qout: e.tensor_tensor(
                    qout, s0, s1, ALU.add), waits=[("ae", self.c["ae"])], inc="ve")
            else:
                p.add("vector", lambda e, ps=ps, s0=s0: e.tensor_scalar(
                    s0, ps[:, :], 0.0, None, ALU.min),
                    waits=[("pe", v_q_mm)], inc="ve")
                p.add("scalar", lambda e, s0=s0, s1=s1: e.activation(s1, s0, AF.Exp),
                      waits=[("ve", self.c["ve"])], inc="ae")
                p.add("vector", lambda e, ps=ps, s1=s1, qout=qout:
                      e.scalar_tensor_tensor(qout, ps[:, :], 0.0, s1,
                                             ALU.max, ALU.add),
                      waits=[("ae", self.c["ae"])], inc="ve")
            q_evac[i] = self.c["ve"]
        v_q_done = self.c["ve"]
        p.add("sync", lambda e: e.dma_start(out=sel_ap, in_=self.d["c_sel16"].ap()),
              waits=[("ve", v_q_done)], inc="xd")
        v_sel = self.c["xd"]

        kvbf = self.scrB[:, KVB:KVB + 520]
        p.add("scalar", lambda e: e.activation(kvbf, kvsb, AF.Identity),
              waits=[("xd", v_kv_back)], inc="ae")
        v_kvbf = self.c["ae"]
        if getattr(self, "dbg_attn", 0) == 2:
            p.add("vector", lambda e: e.tensor_copy(self.stream[:, 2048:2568], kvsb),
                  waits=[("xd", v_kv_back)], inc="ve")
            self.v_stream = ("ve", self.c["ve"])
            self.w_free_W = self.c["pe"]
            self.w_free_Bx = self.c["pe"]
            p.emit()
            es2.close()
            return
        p.add("vector", lambda e: e.memset(self.scrB[:, BDG:BDG + 128], 0.0),
              waits=[("pe", v_kv_mm)], inc="ve")
        v_bz = self.c["ve"]
        for h in range(NH):
            pair, base = h // 2, (h % 2) * 64
            src = self.scrB[base:base + 64,
                            KVB + pair * 65 + 64: KVB + pair * 65 + 65]
            dst = self.scrB[base:base + 64,
                            BDG + (h // 2) * 16 + h: BDG + (h // 2) * 16 + h + 1]
            p.add("scalar", lambda e, src=src, dst=dst: e.activation(
                dst, src, AF.Identity),
                waits=[("ae", v_kvbf), ("ve", v_bz)], inc="ae")
        v_bdiag = self.c["ae"]

        psN = es2.enter_context(nc.psum_tensor(f"psN_{L}", [16, 512], F32))
        normF = self.scrF[0:16, 1024:1536]
        op = ALU.max if L == 0 else ALU.add
        qs_done = {}
        v_prev_bc = 0
        for tb in range(NTB):
            def mk_n(e, tb=tb):
                for r in range(DCH):
                    inst = e.matmul(psN[:, :],
                                    self.scrB[:, BDG + r * 16: BDG + r * 16 + 16],
                                    self.stv(self.A, r, tb * 512, 512),
                                    start=(r == 0), stop=(r == DCH - 1))
                return inst
            waits = [("ae", v_bdiag), ("ve", v_q_done)]
            p.add("tensor", mk_n, waits=waits, inc="pe")
            p.add("scalar", lambda e: e.activation(normF, psN[:, :], AF.Identity),
                  waits=[("pe", self.c["pe"]), ("pe", v_prev_bc)], inc="ae")
            v_ne = self.c["ae"]
            p.add("vector", lambda e: e.tensor_scalar(normF, normF, 1e-6, None, op),
                  waits=[("ae", v_ne)], inc="ve")
            p.add("vector", lambda e: e.reciprocal(normF, normF), inc="ve")
            v_recf = self.c["ve"]
            for c in range(DCH):
                i = c * NTB + tb
                ps = psK0 if c % 2 == 0 else psK1
                def mk_b(e, c=c, ps=ps):
                    return e.matmul(ps[:, :], sel_ap[:, c * 128:(c + 1) * 128],
                                    normF, start=True, stop=True)
                waits = [("ve", v_recf), ("xd", v_sel)]
                if c >= 2:
                    waits.append(("ve", qs_done[(c - 2) * NTB + tb]))
                p.add("tensor", mk_b, waits=waits, inc="pe")
                v_b_mm = self.c["pe"]
                p.add("vector", lambda e, c=c, tb=tb, ps=ps: e.tensor_tensor(
                    self.stv(self.xb, c, tb * 512, 512),
                    self.stv(self.A, c, tb * 512, 512), ps[:, :], ALU.mult),
                    waits=[("pe", v_b_mm)], inc="ve")
                qs_done[i] = self.c["ve"]
            v_prev_bc = self.c["pe"]

        at_done = {}
        for i, (pair, tb) in enumerate((pr, tb) for pr in range(DCH)
                                       for tb in range(NTB)):
            ps = psV0 if i % 2 == 0 else psV1
            def mk_o(e, pair=pair, tb=tb, ps=ps):
                for hh in range(2):
                    base = hh * 64
                    inst = e.matmul(
                        ps[base:base + 64, :],
                        self.scrB[base:base + 64,
                                  KVB + pair * 65: KVB + pair * 65 + 64],
                        self.xb[base:base + 64,
                                pair * T + tb * 512: pair * T + tb * 512 + 512],
                        start=True, stop=True)
                return inst
            waits = [("ae", v_kvbf), ("ve", qs_done[pair * NTB + tb])]
            if i >= 2:
                waits.append(("ae", at_done[i - 2]))
            p.add("tensor", mk_o, waits=waits, inc="pe")
            v_o_mm = self.c["pe"]
            p.add("scalar", lambda e, pair=pair, tb=tb, ps=ps: e.activation(
                self.stv(self.A, pair, tb * 512, 512), ps[:, :], AF.Identity),
                waits=[("pe", v_o_mm)], inc="ae")
            at_done[i] = self.c["ae"]
        v_at_done = self.c["ae"]
        if getattr(self, "dbg_attn", 0) == 1:
            p.add("scalar", lambda e: e.activation(self.stream[:, :],
                                                   self.A[:, :], AF.Identity),
                  waits=[("ae", v_at_done)], inc="ae")
            self.v_stream = ("ae", self.c["ae"])
            self.w_free_W = self.c["pe"]
            self.w_free_Bx = self.c["pe"]
            p.emit()
            es2.close()
            return

        v_wo = self.wload(p, self.W[:, 0:8192],
                          self.chunked(self.d[pre + "wo"].ap()), wait_pe=v_kv_mm)
        st_sem, st_val = self.v_stream
        res_done = {}
        for i, (c, tb) in enumerate((c, tb) for c in range(DCH) for tb in range(NTB)):
            ps = psQ0 if i % 2 == 0 else psQ1
            def mk_p(e, c=c, tb=tb, ps=ps):
                for r in range(DCH):
                    inst = e.matmul(ps[:, :],
                                    self.W[:, r * 1024 + c * 128:
                                           r * 1024 + c * 128 + 128],
                                    self.stv(self.A, r, tb * 512, 512),
                                    start=(r == 0), stop=(r == DCH - 1))
                return inst
            waits = [("wd", v_wo), ("ae", v_at_done)]
            if i >= 2:
                waits.append(("ve", res_done[i - 2]))
            p.add("tensor", mk_p, waits=waits, inc="pe")
            v_p_mm = self.c["pe"]
            ewaits = [("pe", v_p_mm), (st_sem, st_val)]
            if has_bias:
                bo = self.scrF[:, 1720 + c: 1721 + c]
                ewaits.append(("xd", v_bo))
                p.add("vector", lambda e, c=c, tb=tb, ps=ps, bo=bo:
                      e.scalar_tensor_tensor(
                          self.stv(self.stream, c, tb * 512, 512), ps[:, :], bo,
                          self.stv(self.stream, c, tb * 512, 512),
                          ALU.add, ALU.add), waits=ewaits, inc="ve")
            else:
                p.add("vector", lambda e, c=c, tb=tb, ps=ps: e.tensor_tensor(
                    self.stv(self.stream, c, tb * 512, 512), ps[:, :],
                    self.stv(self.stream, c, tb * 512, 512), ALU.add),
                    waits=ewaits, inc="ve")
            res_done[i] = self.c["ve"]
        self.v_stream = ("ve", self.c["ve"])
        self.w_free_W = self.c["pe"]
        self.w_free_Bx = self.c["pe"]
        p.emit()
        es2.close()

    def ph_ln(self, idx):
        nc = self.nc
        es2 = ExitStack()
        psS = es2.enter_context(nc.psum_tensor(f"psS_{idx}", [1, 512], F32))
        psQ = es2.enter_context(nc.psum_tensor(f"psQg_{idx}", [1, 512], F32))
        psA = es2.enter_context(nc.psum_tensor(f"psA_{idx}", [128, 512], F32))
        psB = es2.enter_context(nc.psum_tensor(f"psB_{idx}", [128, 512], F32))
        p = self.new_plan()
        st_sem, st_val = self.v_stream
        SQ = 3072
        v_apply_prev = 0
        for tb in range(NTB):
            sq_done = {}
            for c in range(DCH):
                p.add("scalar", lambda e, c=c, tb=tb: e.activation(
                    self.stv(self.A, c, tb * 512, 512),
                    self.stv(self.stream, c, tb * 512, 512), AF.Identity),
                    waits=[(st_sem, st_val)], inc="ae")
                v_cast = self.c["ae"]
                sq = self.scrB[:, SQ + (c % 2) * 512: SQ + (c % 2) * 512 + 512]
                waits = [("ae", v_cast)]
                if c >= 2:
                    waits.append(("pe", sq_done[c - 2]))
                p.add("scalar", lambda e, c=c, tb=tb, sq=sq: e.activation(
                    sq, self.stv(self.A, c, tb * 512, 512), AF.Square),
                    waits=waits, inc="ae")
                v_sq = self.c["ae"]
                p.add("tensor", lambda e, c=c, tb=tb: e.matmul(
                    psS[:, :], self.ones_col[:, :],
                    self.stv(self.A, c, tb * 512, 512),
                    start=(c == 0), stop=(c == DCH - 1)),
                    waits=[("ae", v_cast)] + ([("ve", v_apply_prev)] if c == 0 else []),
                    inc="pe")
                p.add("tensor", lambda e, c=c, sq=sq: e.matmul(
                    psQ[:, :], self.ones_col[:, :], sq,
                    start=(c == 0), stop=(c == DCH - 1)),
                    waits=[("ae", v_sq)], inc="pe")
                sq_done[c] = self.c["pe"]
            v_stats = self.c["pe"]
            m_row = self.scrF[0:1, 1024:1536]
            e2_row = self.scrF[0:1, 0:512]
            r_row = self.scrF[0:1, 512:1024]
            p.add("scalar", lambda e: e.activation(m_row, psS[0:1, :], AF.Identity,
                                                   scale=1.0 / D),
                  waits=[("pe", v_stats)], inc="ae")
            p.add("scalar", lambda e: e.activation(e2_row, psQ[0:1, :], AF.Identity,
                                                   scale=1.0 / D), inc="ae")
            v_rows = self.c["ae"]
            p.add("vector", lambda e: e.tensor_tensor(psS[0:1, :], m_row, m_row,
                                                      ALU.mult),
                  waits=[("ae", v_rows)], inc="ve")
            p.add("vector", lambda e: e.scalar_tensor_tensor(
                psQ[0:1, :], e2_row, 1e-5, psS[0:1, :], ALU.add, ALU.subtract),
                inc="ve")
            v_var = self.c["ve"]
            p.add("scalar", lambda e: e.activation(e2_row, psQ[0:1, :], AF.Ln),
                  waits=[("ve", v_var)], inc="ae")
            p.add("scalar", lambda e: e.activation(r_row, e2_row, AF.Exp,
                                                   scale=-0.5), inc="ae")
            v_r = self.c["ae"]
            p.add("vector", lambda e: e.scalar_tensor_tensor(
                m_row, m_row, -1.0, r_row, ALU.mult, ALU.mult),
                waits=[("ae", v_r)], inc="ve")
            v_b = self.c["ve"]
            onesF = self.scrF[0:1, 1552:1680]
            p.add("tensor", lambda e: e.matmul(psA[:, :], onesF[0:1, 0:128],
                                               r_row, start=True, stop=True),
                  waits=[("ae", v_r), ("ve", v_apply_prev),
                         ("ve", self.v_consts)], inc="pe")
            p.add("tensor", lambda e: e.matmul(psB[:, :], onesF[0:1, 0:128],
                                               m_row, start=True, stop=True),
                  waits=[("ve", v_b)], inc="pe")
            v_bc = self.c["pe"]
            for c in range(DCH):
                t0 = self.scrF[:, (c % 2) * 512: (c % 2) * 512 + 512]
                p.add("vector", lambda e, c=c, tb=tb, t0=t0: e.tensor_tensor(
                    t0, self.stv(self.stream, c, tb * 512, 512), psA[:, :],
                    ALU.mult), waits=[("pe", v_bc)], inc="ve")
                p.add("vector", lambda e, c=c, tb=tb, t0=t0: e.tensor_tensor(
                    self.stv(self.stream, c, tb * 512, 512), t0, psB[:, :],
                    ALU.add), inc="ve")
                v_app = self.c["ve"]
                p.add("scalar", lambda e, c=c, tb=tb: e.activation(
                    self.stv(self.xb, c, tb * 512, 512),
                    self.stv(self.stream, c, tb * 512, 512), AF.Identity),
                    waits=[("ve", v_app)], inc="ae")
            v_apply_prev = self.c["ve"]
        self.v_stream = ("ve", self.c["ve"])
        self.v_xb = ("ae", self.c["ae"])
        self.ln_pe_done = self.c["pe"]
        p.emit()
        es2.close()

    def ph_l0_moe(self):
        nc = self.nc
        es2 = ExitStack()
        p = self.new_plan()
        fence_pe = self.fence["pe"]
        xb_sem, xb_val = self.v_xb
        st_sem, st_val = self.v_stream

        es_gate = ExitStack()
        psL = es_gate.enter_context(nc.psum_tensor("psL0g", [128, 512], F32))
        psGB = es_gate.enter_context(nc.psum_tensor("psGB0", [128, 32], F32))
        gw_sb = self.scrB[:, 0:256]
        v_gw = self.wload(p, gw_sb, self.chunked(self.d["l0_gw"].ap()),
                          wait_pe=fence_pe)
        p.add("sync", lambda e: e.dma_start(out=self.scrF[0:1, 1680:1712],
                                            in_=self.d["l0_gb"].ap()), inc="xd")
        v_gbr = self.c["xd"]
        p.add("tensor", lambda e: e.matmul(psGB[:, :], self.scrF[0:1, 1552:1680],
                                           self.scrF[0:1, 1680:1712],
                                           start=True, stop=True),
              waits=[("xd", v_gbr), ("ve", self.v_consts)], inc="pe")
        gbb = self.scrF[:, 1728:1760]
        p.add("scalar", lambda e: e.activation(gbb, psGB[:, :], AF.Identity),
              waits=[("pe", self.c["pe"])], inc="ae")
        v_gbb = self.c["ae"]
        for tc in range(NTC):
            def mk_lg(e, tc=tc):
                for r in range(DCH):
                    inst = e.matmul(psL[:, tc * 32:(tc + 1) * 32],
                                    self.stv(self.xb, r, tc * 128, 128),
                                    gw_sb[:, r * 32:(r + 1) * 32],
                                    start=(r == 0), stop=(r == DCH - 1))
                return inst
            p.add("tensor", mk_lg,
                  waits=[("wd", v_gw)] + ([(xb_sem, xb_val)] if tc == 0 else []),
                  inc="pe")
        v_lg = self.c["pe"]

        scores = self.scrF[:, 0:512]
        sc = self.scrF[:, 512:1024]
        msk = self.scrF[:, 1024:1536]
        sel = self.scrB[:, 768:1280]
        eq = self.scrB[:, 256:768]
        p.add("scalar", lambda e: e.activation(scores, psL[:, :], AF.Sigmoid),
              waits=[("pe", v_lg)], inc="ae")
        v_scores = self.c["ae"]

        seg32 = lambda ap: bass.AP(ap.tensor, ap.offset,
                                   [ap.ap[0], [32, NTC], [1, 32]])
        g48 = lambda ap: bass.AP(ap.tensor, ap.offset,
                                 [ap.ap[0], [32, NTC], [8, 4], [1, 8]])
        sm = lambda lo, n, dims: self.fap(self.scrF, lo, n, dims)
        gm1 = sm(1760, 64, [[4, NTC], [1, 4]])
        gm1b = sm(1760, 64, [[4, NTC], [1, 4], [0, 8]])
        gm2 = sm(1824, 64, [[4, NTC], [1, 4]])
        gs = gm1
        gh = sm(1888, 16, [[1, NTC]])
        ghb = sm(1888, 16, [[1, NTC], [0, 4]])
        gmask = sm(1904, 64, [[4, NTC], [1, 4]])
        gmaskb = sm(1904, 64, [[4, NTC], [1, 4], [0, 8]])
        geq = self.fap(self.scrB, 1280, 64, [[4, NTC], [1, 4]])
        mx = sm(1968, 16, [[1, NTC]])
        mxb = sm(1968, 16, [[1, NTC], [0, 32]])

        def gb_b(ap):
            return bass.AP(self.scrF.ap().tensor, gbb.offset,
                           [self.scrF.ap().ap[0], [0, NTC], [1, 32]])
        p.add("vector", lambda e: e.tensor_tensor(seg32(sc), seg32(scores),
                                                  gb_b(None), ALU.add),
              waits=[("ae", max(v_scores, v_gbb))], inc="ve")
        p.add("vector", lambda e: e.reduce_max(gm1, g48(sc), axis=AX), inc="ve")
        p.add("vector", lambda e: e.tensor_tensor(g48(eq), g48(sc), gm1b, ALU.is_ge),
              inc="ve")
        p.add("vector", lambda e: e.scalar_tensor_tensor(msk, eq, -1e9, sc,
                                                         ALU.mult, ALU.add), inc="ve")
        p.add("vector", lambda e: e.reduce_max(gm2, g48(msk), axis=AX), inc="ve")
        p.add("vector", lambda e: e.tensor_tensor(gs, gm1, gm2, ALU.add), inc="ve")
        p.add("vector", lambda e: e.reduce_max(gh, gs, axis=AX), inc="ve")
        p.add("vector", lambda e: e.tensor_tensor(gmask, gs, ghb, ALU.is_ge),
              inc="ve")
        p.add("vector", lambda e: e.scalar_tensor_tensor(gs, gmask, -1e9, gs,
                                                         ALU.mult, ALU.add), inc="ve")
        p.add("vector", lambda e: e.reduce_max(gh, gs, axis=AX), inc="ve")
        p.add("vector", lambda e: e.tensor_tensor(geq, gs, ghb, ALU.is_ge), inc="ve")
        p.add("vector", lambda e: e.tensor_tensor(gmask, gmask, geq, ALU.add),
              inc="ve")
        p.add("vector", lambda e: e.tensor_tensor(g48(msk), g48(sc), gmaskb,
                                                  ALU.mult), inc="ve")
        for it in range(8):
            p.add("vector", lambda e: e.reduce_max(mx, seg32(msk), axis=AX),
                  inc="ve")
            p.add("vector", lambda e: e.tensor_tensor(seg32(eq), seg32(msk), mxb,
                                                      ALU.is_ge), inc="ve")
            if it == 0:
                p.add("vector", lambda e: e.tensor_copy(sel, eq), inc="ve")
            else:
                p.add("vector", lambda e: e.tensor_tensor(sel, sel, eq, ALU.add),
                      inc="ve")
            if it < 7:
                p.add("vector", lambda e: e.scalar_tensor_tensor(
                    msk, eq, -1e9, msk, ALU.mult, ALU.add), inc="ve")
        p.add("vector", lambda e: e.tensor_tensor(msk, sel, scores, ALU.mult),
              inc="ve")
        p.add("vector", lambda e: e.reduce_sum(mx, seg32(msk), axis=AX), inc="ve")
        p.add("vector", lambda e: e.tensor_scalar(mx, mx, 1e-20, None, ALU.add),
              inc="ve")
        p.add("vector", lambda e: e.reciprocal(mx, mx), inc="ve")
        p.add("vector", lambda e: e.tensor_scalar(mx, mx, 2.5, None, ALU.mult),
              inc="ve")
        p.add("vector", lambda e: e.tensor_tensor(seg32(msk), seg32(msk), mxb,
                                                  ALU.mult), inc="ve")
        comb = self.scrB[:, 256:768]
        p.add("vector", lambda e: e.tensor_copy(comb, msk), inc="ve")
        v_comb = self.c["ve"]
        es_gate.close()
        if getattr(self, "dbg_moe", 0) == 1:
            p.add("scalar", lambda e: e.activation(self.stream[:, 0:512], comb,
                                                   AF.Identity),
                  waits=[("ve", v_comb)], inc="ae")
            p.add("scalar", lambda e: e.activation(self.stream[:, 512:1024],
                                                   scores, AF.Identity), inc="ae")
            p.add("scalar", lambda e: e.activation(self.stream[:, 1024:1536],
                                                   sel, AF.Identity), inc="ae")
            v_dump = self.c["ae"]
            self.v_stream = ("ae", v_dump)
            self.w_free_W = self.c["pe"]
            self.w_free_Bx = self.c["pe"]
            p.emit()
            es2.close()
            return

        psG0 = es2.enter_context(nc.psum_tensor("l0psG0", [128, 512], F32))
        psG1 = es2.enter_context(nc.psum_tensor("l0psG1", [128, 512], F32))
        psU0 = es2.enter_context(nc.psum_tensor("l0psU0", [128, 512], F32))
        psU1 = es2.enter_context(nc.psum_tensor("l0psU1", [128, 512], F32))
        psW = es2.enter_context(nc.psum_tensor("l0psW", [128, 512], F32))
        v_sg = self.wload(p, self.W[:, 0:8192],
                          self.chunked(self.d["l0_sg"].ap()), wait_pe=self.w_free_W)
        v_su = self.wload(p, self.W[:, 8192:16384],
                          self.chunked(self.d["l0_su"].ap()))
        v_sd = self.wload(p, self.Bx[:, 0:8192],
                          self.chunked(self.d["l0_sd"].ap()), wait_pe=self.w_free_Bx)
        SIL = 1344
        ZP = 832
        WREP = 2368
        z_done = {}
        for i, (zc, tb) in enumerate((zc, tb) for zc in range(DCH)
                                     for tb in range(NTB)):
            psG = psG0 if i % 2 == 0 else psG1
            psU = psU0 if i % 2 == 0 else psU1
            def mk_g(e, zc=zc, tb=tb, ps=psG, base=0, vv=v_sg):
                for r in range(DCH):
                    inst = e.matmul(ps[:, :],
                                    self.W[:, base + r * 1024 + zc * 128:
                                           base + r * 1024 + zc * 128 + 128],
                                    self.stv(self.xb, r, tb * 512, 512),
                                    start=(r == 0), stop=(r == DCH - 1))
                return inst
            waits = [("wd", max(v_sg, v_su)), ("ve", v_comb)]
            if i >= 2:
                waits.append(("ve", z_done[i - 2]))
            p.add("tensor", lambda e, f=mk_g, ps=psG: f(e, ps=ps, base=0),
                  waits=waits, inc="pe")
            p.add("tensor", lambda e, f=mk_g, ps=psU: f(e, ps=ps, base=8192),
                  inc="pe")
            v_gu = self.c["pe"]
            sil = self.scrB[:, SIL + (i % 2) * 512: SIL + (i % 2) * 512 + 512]
            p.add("scalar", lambda e, ps=psG, sil=sil: e.activation(
                sil, ps[:, :], AF.Silu), waits=[("pe", v_gu)], inc="ae")
            p.add("vector", lambda e, zc=zc, tb=tb, ps=psU, sil=sil: e.tensor_tensor(
                self.stv(self.A, zc, tb * 512, 512), sil, ps[:, :], ALU.mult),
                waits=[("ae", self.c["ae"])], inc="ve")
            z_done[i] = self.c["ve"]
        v_zsh = self.c["ve"]
        v_zsh_pe = self.c["pe"]

        dstream_done = 0
        for i, (co, tb) in enumerate((co, tb) for co in range(DCH)
                                     for tb in range(NTB)):
            ps = psG0 if i % 2 == 0 else psG1
            def mk_d(e, co=co, tb=tb, ps=ps):
                for r in range(DCH):
                    inst = e.matmul(ps[:, :],
                                    self.Bx[:, r * 1024 + co * 128:
                                            r * 1024 + co * 128 + 128],
                                    self.stv(self.A, r, tb * 512, 512),
                                    start=(r == 0), stop=(r == DCH - 1))
                return inst
            waits = [("wd", v_sd), ("ve", v_zsh)]
            if i >= 2:
                waits.append(("ve", dstream_done))
            p.add("tensor", mk_d, waits=waits, inc="pe")
            v_d = self.c["pe"]
            p.add("vector", lambda e, co=co, tb=tb, ps=ps: e.tensor_tensor(
                self.stv(self.stream, co, tb * 512, 512), ps[:, :],
                self.stv(self.stream, co, tb * 512, 512), ALU.add),
                waits=[("pe", v_d), (st_sem, st_val)], inc="ve")
            dstream_done = self.c["ve"]
        v_shared_pe = self.c["pe"]

        arena = {0: self.W, 1: self.Bx}
        egu_free = {0: v_zsh_pe, 1: v_zsh_pe}
        ed_free = {0: v_shared_pe, 1: v_shared_pe}
        v_wegu = {}
        v_wed = {}

        def load_egu(e_i):
            buf = arena[e_i % 2]
            v1 = self.wload(p, buf[:, 0:4096],
                            self.chunked(self.d["l0_eg"].ap()[e_i]),
                            wait_pe=egu_free[e_i % 2])
            v2 = self.wload(p, buf[:, 4096:8192],
                            self.chunked(self.d["l0_eu"].ap()[e_i]))
            v_wegu[e_i] = max(v1, v2)

        def load_ed(e_i):
            buf = arena[e_i % 2]
            v_wed[e_i] = self.wload(p, buf[:, 8192:12288],
                                    self.chunked(self.d["l0_ed"].ap()[e_i]),
                                    wait_pe=ed_free[e_i % 2])

        load_egu(0)
        load_ed(0)
        load_egu(1)
        load_ed(1)
        wex = lambda e_i, base, r, lo, n, cw: (
            arena[e_i % 2][:, base + r * cw + lo: base + r * cw + lo + n])

        v_wrep_prev = self.c["ae"]
        for pair_i in range(16):
            e0, e1 = 2 * pair_i, 2 * pair_i + 1
            for e_idx, e_i in enumerate((e0, e1)):
                for tb in range(NTB):
                    def mk_w(e, e_i=e_i, tb=tb):
                        for q in range(4):
                            tc = tb * 4 + q
                            col = comb[:, tc * 32 + e_i: tc * 32 + e_i + 1]
                            lhsT = bass.AP(col.tensor, col.offset,
                                           [col.ap[0], [0, 128]])
                            inst = e.matmul(psW[:, q * 128:(q + 1) * 128], lhsT,
                                            self.ident[:, :], start=True, stop=True)
                        return inst
                    p.add("tensor", mk_w,
                          waits=[("ve", self.c["ve"]), ("ae", self.c["ae"])],
                          inc="pe")
                    v_wmm = self.c["pe"]
                    p.add("scalar", lambda e, tb=tb: e.activation(
                        self.scrB[:, WREP + tb * 512: WREP + tb * 512 + 512],
                        psW[:, :], AF.Identity), waits=[("pe", v_wmm)], inc="ae")
                v_wrep = self.c["ae"]
                for i, (zc, tb) in enumerate((zc, tb) for zc in range(4)
                                             for tb in range(NTB)):
                    psG = psG0 if i % 2 == 0 else psG1
                    psU = psU0 if i % 2 == 0 else psU1
                    def mk_g2(e, e_i=e_i, zc=zc, tb=tb, ps=psG, base=0):
                        for r in range(DCH):
                            inst = e.matmul(ps[:, :],
                                            wex(e_i, base, r, zc * 128, 128, 512),
                                            self.stv(self.xb, r, tb * 512, 512),
                                            start=(r == 0), stop=(r == DCH - 1))
                        return inst
                    waits = [("wd", v_wegu[e_i])]
                    if i >= 2:
                        waits.append(("ve", z_done[i - 2]))
                    else:
                        waits.append(("ve", self.c["ve"]))
                    p.add("tensor", lambda e, f=mk_g2, ps=psG: f(e, ps=ps, base=0),
                          waits=waits, inc="pe")
                    p.add("tensor", lambda e, f=mk_g2, ps=psU: f(e, ps=ps, base=4096),
                          inc="pe")
                    v_gu = self.c["pe"]
                    sil = self.scrB[:, SIL + (i % 2) * 512: SIL + (i % 2) * 512 + 512]
                    p.add("scalar", lambda e, ps=psG, sil=sil: e.activation(
                        sil, ps[:, :], AF.Silu), waits=[("pe", v_gu)], inc="ae")
                    v_sil = self.c["ae"]
                    zp = self.scrB[:, ZP:ZP + 512]
                    p.add("vector", lambda e, ps=psU, sil=sil, zp=zp: e.tensor_tensor(
                        zp, sil, ps[:, :], ALU.mult), waits=[("ae", v_sil)], inc="ve")
                    p.add("vector", lambda e, e_idx=e_idx, zc=zc, tb=tb, zp=zp:
                          e.tensor_tensor(
                              self.stv(self.A, e_idx * 4 + zc, tb * 512, 512), zp,
                              self.scrB[:, WREP + tb * 512: WREP + tb * 512 + 512],
                              ALU.mult), waits=[("ae", v_wrep)], inc="ve")
                    z_done[i] = self.c["ve"]
                if getattr(self, "dbg_moe", 0) == 2 and e_i == 0:
                    p.add("scalar", lambda e: e.activation(
                        self.stream[:, 0:2048],
                        self.scrB[:, WREP:WREP + 2048], AF.Identity),
                        waits=[("ve", self.c["ve"])], inc="ae")
                    for zc in range(4):
                        p.add("scalar", lambda e, zc=zc: e.activation(
                            self.stream[:, (zc + 1) * 2048:(zc + 2) * 2048],
                            self.A[:, zc * 2048:(zc + 1) * 2048], AF.Identity),
                            inc="ae")
                    self.v_stream = ("ae", self.c["ae"])
                    self.w_free_W = self.c["pe"]
                    self.w_free_Bx = self.c["pe"]
                    p.emit()
                    es2.close()
                    return
                egu_free[e_i % 2] = self.c["pe"]
                if e_i + 2 < 32:
                    load_egu(e_i + 2)
            v_zpair = self.c["ve"]
            for i, (co, tb) in enumerate((co, tb) for co in range(DCH)
                                         for tb in range(NTB)):
                ps = psG0 if i % 2 == 0 else psG1
                def mk_d2(e, e0=e0, e1=e1, co=co, tb=tb, ps=ps):
                    k = 0
                    for e_i in (e0, e1):
                        for r in range(4):
                            inst = e.matmul(ps[:, :],
                                            wex(e_i, 8192, r, co * 128, 128, 1024),
                                            self.stv(self.A, (e_i % 2) * 4 + r,
                                                     tb * 512, 512),
                                            start=(k == 0), stop=(k == 7))
                            k += 1
                    return inst
                waits = [("ve", v_zpair), ("wd", max(v_wed[e0], v_wed[e1]))]
                if i >= 2:
                    waits.append(("ve", dstream_done))
                p.add("tensor", mk_d2, waits=waits, inc="pe")
                v_d = self.c["pe"]
                p.add("vector", lambda e, co=co, tb=tb, ps=ps: e.tensor_tensor(
                    self.stv(self.stream, co, tb * 512, 512), ps[:, :],
                    self.stv(self.stream, co, tb * 512, 512), ALU.add),
                    waits=[("pe", v_d)], inc="ve")
                dstream_done = self.c["ve"]
            ed_free[0] = self.c["pe"]
            ed_free[1] = self.c["pe"]
            if e0 + 2 < 32:
                load_ed(e0 + 2)
                load_ed(e1 + 2)
        self.v_stream = ("ve", self.c["ve"])
        self.w_free_W = self.c["pe"]
        self.w_free_Bx = self.c["pe"]
        p.emit()
        es2.close()

    def ph_l1_moe(self):
        nc = self.nc
        es2 = ExitStack()
        p = self.new_plan()
        fence_pe = self.fence["pe"]
        xb_sem, xb_val = self.v_xb
        st_sem, st_val = self.v_stream

        es_gate = ExitStack()
        psL = es_gate.enter_context(nc.psum_tensor("psL1g", [128, 128], F32))
        psGB = es_gate.enter_context(nc.psum_tensor("psGB1", [128, 8], F32))
        gw_sb = self.scrB[:, 0:64]
        v_gw = self.wload(p, gw_sb, self.chunked(self.d["l1_gw"].ap()),
                          wait_pe=fence_pe)
        p.add("sync", lambda e: e.dma_start(out=self.scrF[0:1, 1680:1688],
                                            in_=self.d["l1_gb"].ap()), inc="xd")
        v_gbr = self.c["xd"]
        p.add("tensor", lambda e: e.matmul(psGB[:, :], self.scrF[0:1, 1552:1680],
                                           self.scrF[0:1, 1680:1688],
                                           start=True, stop=True),
              waits=[("xd", v_gbr)], inc="pe")
        gbb = self.scrF[:, 1728:1736]
        p.add("scalar", lambda e: e.activation(gbb, psGB[:, :], AF.Identity),
              waits=[("pe", self.c["pe"])], inc="ae")
        v_gbb = self.c["ae"]
        for tc in range(NTC):
            def mk_lg(e, tc=tc):
                for r in range(DCH):
                    inst = e.matmul(psL[:, tc * 8:(tc + 1) * 8],
                                    self.stv(self.xb, r, tc * 128, 128),
                                    gw_sb[:, r * 8:(r + 1) * 8],
                                    start=(r == 0), stop=(r == DCH - 1))
                return inst
            p.add("tensor", mk_lg,
                  waits=[("wd", v_gw)] + ([(xb_sem, xb_val)] if tc == 0 else []),
                  inc="pe")
        v_lg = self.c["pe"]
        lg = self.scrF[:, 0:128]
        smx = self.scrF[:, 128:256]
        sel = self.scrF[:, 256:384]
        msk = self.scrF[:, 384:512]
        eq = self.scrB[:, 64:192]
        seg8 = lambda ap: bass.AP(ap.tensor, ap.offset, [ap.ap[0], [8, NTC], [1, 8]])
        mx = self.fap(self.scrF, 1968, 16, [[1, NTC]])
        mxb = self.fap(self.scrF, 1968, 16, [[1, NTC], [0, 8]])
        p.add("scalar", lambda e: e.activation(lg, psL[:, :], AF.Identity),
              waits=[("pe", v_lg)], inc="ae")
        v_lgs = self.c["ae"]
        p.add("vector", lambda e: e.reduce_max(mx, seg8(lg), axis=AX),
              waits=[("ae", v_lgs)], inc="ve")
        p.add("vector", lambda e: e.tensor_tensor(seg8(smx), seg8(lg), mxb,
                                                  ALU.subtract), inc="ve")
        v_sub = self.c["ve"]
        p.add("scalar", lambda e: e.activation(smx, smx, AF.Exp),
              waits=[("ve", v_sub)], inc="ae")
        v_exp = self.c["ae"]
        p.add("vector", lambda e: e.reduce_sum(mx, seg8(smx), axis=AX),
              waits=[("ae", v_exp)], inc="ve")
        p.add("vector", lambda e: e.reciprocal(mx, mx), inc="ve")
        p.add("vector", lambda e: e.tensor_tensor(seg8(smx), seg8(smx), mxb,
                                                  ALU.mult), inc="ve")
        def gb_b(_):
            return bass.AP(self.scrF.ap().tensor, gbb.offset,
                           [self.scrF.ap().ap[0], [0, NTC], [1, 8]])
        p.add("vector", lambda e: e.tensor_tensor(seg8(msk), seg8(smx), gb_b(None),
                                                  ALU.add),
              waits=[("ae", v_gbb)], inc="ve")
        for it in range(2):
            p.add("vector", lambda e: e.reduce_max(mx, seg8(msk), axis=AX), inc="ve")
            p.add("vector", lambda e: e.tensor_tensor(seg8(eq), seg8(msk), mxb,
                                                      ALU.is_ge), inc="ve")
            if it == 0:
                p.add("vector", lambda e: e.tensor_copy(sel, eq), inc="ve")
                p.add("vector", lambda e: e.scalar_tensor_tensor(
                    msk, eq, -1e9, msk, ALU.mult, ALU.add), inc="ve")
            else:
                p.add("vector", lambda e: e.tensor_tensor(sel, sel, eq, ALU.add),
                      inc="ve")
        p.add("vector", lambda e: e.tensor_tensor(msk, sel, smx, ALU.mult), inc="ve")
        comb = self.scrB[:, 64:192]
        p.add("vector", lambda e: e.tensor_copy(comb, msk), inc="ve")
        v_comb = self.c["ve"]
        es_gate.close()
        if getattr(self, "dbg_l1", 0) == 1:
            p.add("scalar", lambda e: e.activation(self.stream[:, 0:128], comb,
                                                   AF.Identity),
                  waits=[("ve", v_comb)], inc="ae")
            p.add("scalar", lambda e: e.activation(self.stream[:, 2048:2176],
                                                   smx, AF.Identity), inc="ae")
            p.add("scalar", lambda e: e.activation(self.stream[:, 4096:4224],
                                                   sel, AF.Identity), inc="ae")
            p.add("scalar", lambda e: e.activation(self.stream[:, 6144:6272],
                                                   lg, AF.Identity), inc="ae")
            self.v_stream = ("ae", self.c["ae"])
            self.w_free_W = self.c["pe"]
            self.w_free_Bx = self.c["pe"]
            p.emit()
            es2.close()
            return

        psY10 = es2.enter_context(nc.psum_tensor("psY10", [128, 512], F32))
        psY11 = es2.enter_context(nc.psum_tensor("psY11", [128, 512], F32))
        psY30 = es2.enter_context(nc.psum_tensor("psY30", [128, 512], F32))
        psY31 = es2.enter_context(nc.psum_tensor("psY31", [128, 512], F32))
        psD0 = es2.enter_context(nc.psum_tensor("psD0", [128, 512], F32))
        psD1 = es2.enter_context(nc.psum_tensor("psD1", [128, 512], F32))
        psW = es2.enter_context(nc.psum_tensor("psWr1", [128, 512], F32))
        E1, G, GM, LC, WREP, LS = 192, 704, 1216, 1728, 2240, 1216
        w1b = {0: 0, 1: 8192}
        w3b = {0: 4096, 1: 12288}
        slot_free = {0: self.w_free_W, 1: self.w_free_W}
        w2_free = self.w_free_Bx
        b2_free = 0
        B1C = {0: 1984, 1: 2016}
        B3C = {0: 2048, 1: 2080}
        dstream_done = 0
        yz_done = {0: v_comb, 1: v_comb}
        v_wrep_guard = self.c["ae"]
        for e_i in range(9):
            sl = e_i % 2
            p.add("sync", lambda e, e_i=e_i, sl=sl: e.dma_start(
                out=self.scrF[:, B1C[sl]:B1C[sl] + 32],
                in_=self.d["l1_b1"].ap()[e_i]),
                waits=[("pe", fence_pe if e_i < 2 else b2_free)], inc="xd")
            p.add("sync", lambda e, e_i=e_i, sl=sl: e.dma_start(
                out=self.scrF[:, B3C[sl]:B3C[sl] + 32],
                in_=self.d["l1_b3"].ap()[e_i]), inc="xd")
            v_bias = self.c["xd"]
            v_b2 = self.wload(p, self.rowB[0:1, 1024:2048],
                              self.d["l1_b2"].ap()[e_i], wait_pe=b2_free)
            if e_i < 8:
                for tb in range(NTB):
                    def mk_w(e, e_i=e_i, tb=tb):
                        for q in range(4):
                            tc = tb * 4 + q
                            col = comb[:, tc * 8 + e_i: tc * 8 + e_i + 1]
                            lhsT = bass.AP(col.tensor, col.offset,
                                           [col.ap[0], [0, 128]])
                            inst = e.matmul(psW[:, q * 128:(q + 1) * 128], lhsT,
                                            self.ident[:, :], start=True, stop=True)
                        return inst
                    p.add("tensor", mk_w,
                          waits=[("ve", self.c["ve"]), ("ae", self.c["ae"])],
                          inc="pe")
                    v_wmm = self.c["pe"]
                    p.add("scalar", lambda e, tb=tb: e.activation(
                        self.scrB[:, WREP + tb * 512: WREP + tb * 512 + 512],
                        psW[:, :], AF.Identity), waits=[("pe", v_wmm)], inc="ae")
                v_wrep = self.c["ae"]
            for half in range(2):
                v_w2h = self.wload(
                    p, self.Bx[:, 0:8192],
                    self.chunked(self.d["l1_w2"].ap()
                                 [e_i, half * 1024:(half + 1) * 1024, :]),
                    wait_pe=w2_free)
                h_done = 0
                for cb in range(4):
                    B = half * 4 + cb
                    sb2 = B % 2
                    v_w1 = self.wload(
                        p, self.W[:, w1b[sb2]:w1b[sb2] + 4096],
                        self.chunked(self.d["l1_w1"].ap()
                                     [e_i][:, B * 512:(B + 1) * 512]),
                        wait_pe=slot_free[sb2])
                    v_w3 = self.wload(
                        p, self.W[:, w3b[sb2]:w3b[sb2] + 4096],
                        self.chunked(self.d["l1_w3"].ap()
                                     [e_i][:, B * 512:(B + 1) * 512]))
                    v_wblk = max(v_w1, v_w3)
                    for tb in range(NTB):
                        for j in range(2):
                            prod_v = {}
                            for gl in range(2):
                                fc = gl * 2 + j
                                i2 = fc % 2
                                psY1 = psY10 if i2 == 0 else psY11
                                psY3 = psY30 if i2 == 0 else psY31
                                def mk_y(e, sb2=sb2, fc=fc, tb=tb, ps=None, w13=0):
                                    base = (w1b if w13 == 0 else w3b)[sb2]
                                    for r in range(DCH):
                                        inst = e.matmul(
                                            ps[:, :],
                                            self.W[:, base + r * 512 + fc * 128:
                                                   base + r * 512 + fc * 128 + 128],
                                            self.stv(self.xb, r, tb * 512, 512),
                                            start=(r == 0), stop=(r == DCH - 1))
                                    return inst
                                waits = [("wd", v_wblk), ("ve", yz_done[i2])]
                                if xb_sem == "ae":
                                    waits.append((xb_sem, xb_val))
                                p.add("tensor",
                                      lambda e, f=mk_y, ps=psY1: f(e, ps=ps, w13=0),
                                      waits=waits, inc="pe")
                                p.add("tensor",
                                      lambda e, f=mk_y, ps=psY3: f(e, ps=ps, w13=1),
                                      inc="pe")
                                v_y = self.c["pe"]
                                bcol = B * 4 + fc
                                b1c = self.scrF[:, B1C[sl] + bcol: B1C[sl] + bcol + 1]
                                b3c = self.scrF[:, B3C[sl] + bcol: B3C[sl] + bcol + 1]
                                e1 = self.scrB[:, E1 + i2 * 256: E1 + i2 * 256 + 256] \
                                    if False else self.scrB[:, E1:E1 + 512]
                                p.add("scalar", lambda e, ps=psY1, e1=e1, b1c=b1c:
                                      e.activation(e1, ps[:, :], AF.Identity,
                                                   bias=b1c),
                                      waits=[("pe", v_y), ("xd", v_bias),
                                             ("ve", h_done)], inc="ae")
                                v_e1 = self.c["ae"]
                                dst = self.scrB[:, (G if gl == 0 else LC):
                                                (G if gl == 0 else LC) + 512]
                                p.add("vector", lambda e, ps=psY3, e1=e1, dst=dst,
                                      b3c=b3c: e.scalar_tensor_tensor(
                                          dst, ps[:, :], b3c, e1, ALU.add, ALU.mult),
                                      waits=[("ae", v_e1)], inc="ve")
                                prod_v[gl] = self.c["ve"]
                                yz_done[i2] = self.c["ve"]
                            if (getattr(self, "dbg_l1", 0) == 3 and e_i == 0
                                    and half == 0 and cb == 0 and tb == 0
                                    and j == 0):
                                p.add("scalar", lambda e: e.activation(
                                    self.stream[:, 0:512],
                                    self.scrB[:, G:G + 512], AF.Identity),
                                    waits=[("ve", prod_v[1])], inc="ae")
                                p.add("scalar", lambda e: e.activation(
                                    self.stream[:, 2048:2560],
                                    self.scrB[:, LC:LC + 512], AF.Identity),
                                    inc="ae")
                                p.add("scalar", lambda e: e.activation(
                                    self.stream[:, 4096:4608],
                                    self.scrB[:, WREP:WREP + 512], AF.Identity),
                                    inc="ae")
                                self.v_stream = ("ae", self.c["ae"])
                                self.w_free_W = self.c["pe"]
                                self.w_free_Bx = self.c["pe"]
                                p.emit()
                                es2.close()
                                return
                            gm = self.scrB[:, GM:GM + 512]
                            p.add("vector", lambda e, gm=gm: e.tensor_scalar(
                                gm, self.scrB[:, G:G + 512], 7.0, None, ALU.min),
                                waits=[("ve", prod_v[0])], inc="ve")
                            v_gm = self.c["ve"]
                            fg = self.scrB[:, E1:E1 + 512]
                            p.add("scalar", lambda e, gm=gm, fg=fg: e.activation(
                                fg, gm, AF.Silu, scale=1.702),
                                waits=[("ve", v_gm)], inc="ae")
                            v_fg = self.c["ae"]
                            ls = self.scrB[:, LS:LS + 512]
                            p.add("vector", lambda e, ls=ls: e.tensor_scalar(
                                ls, self.scrB[:, LC:LC + 512], 7.0, -7.0,
                                ALU.min, ALU.max),
                                waits=[("ve", prod_v[1]), ("ae", v_fg)],
                                inc="ve")
                            p.add("vector", lambda e, ls=ls: e.tensor_scalar(
                                ls, ls, 1.0, 1.0 / 1.702, ALU.add, ALU.mult),
                                inc="ve")
                            v_ls = self.c["ve"]
                            if e_i < 8:
                                p.add("vector", lambda e, ls=ls, tb=tb:
                                      e.tensor_tensor(
                                          ls, ls,
                                          self.scrB[:, WREP + tb * 512:
                                                    WREP + tb * 512 + 512],
                                          ALU.mult),
                                      waits=[("ae", v_wrep)], inc="ve")
                                v_ls = self.c["ve"]
                            hc = cb * 2 + j
                            p.add("vector", lambda e, hc=hc, tb=tb, fg=fg, ls=ls:
                                  e.tensor_tensor(
                                      self.stv(self.A, hc, tb * 512, 512), fg, ls,
                                      ALU.mult),
                                  waits=[("ae", v_fg)], inc="ve")
                            h_done = self.c["ve"]
                            if (getattr(self, "dbg_l1", 0) == 4 and e_i == 0
                                    and half == 0 and cb == 0 and tb == 0
                                    and j == 0):
                                p.add("scalar", lambda e, fg=fg: e.activation(
                                    self.stream[:, 0:512], fg, AF.Identity),
                                    waits=[("ae", v_fg), ("ve", h_done)], inc="ae")
                                p.add("scalar", lambda e, ls=ls: e.activation(
                                    self.stream[:, 2048:2560], ls, AF.Identity),
                                    inc="ae")
                                p.add("scalar", lambda e: e.activation(
                                    self.stream[:, 4096:4608],
                                    self.stv(self.A, 0, 0, 512), AF.Identity),
                                    inc="ae")
                                self.v_stream = ("ae", self.c["ae"])
                                self.w_free_W = self.c["pe"]
                                self.w_free_Bx = self.c["pe"]
                                p.emit()
                                es2.close()
                                return
                    slot_free[sb2] = self.c["pe"]
                v_h_half = h_done
                if getattr(self, "dbg_l1", 0) == 2 and e_i == 0 and half == 0:
                    p.add("scalar", lambda e: e.activation(
                        self.stream[:, 0:16384], self.A[:, 0:16384], AF.Identity),
                        waits=[("ve", v_h_half)], inc="ae")
                    self.v_stream = ("ae", self.c["ae"])
                    self.w_free_W = self.c["pe"]
                    self.w_free_Bx = self.c["pe"]
                    p.emit()
                    es2.close()
                    return
                for i, (co, tb) in enumerate((co, tb) for co in range(DCH)
                                             for tb in range(NTB)):
                    ps = psD0 if i % 2 == 0 else psD1
                    def mk_d(e, e_i=e_i, co=co, tb=tb, ps=ps, half=half):
                        for r in range(8):
                            inst = e.matmul(
                                ps[:, :],
                                self.Bx[:, r * 1024 + co * 128:
                                        r * 1024 + co * 128 + 128],
                                self.stv(self.A, r, tb * 512, 512),
                                start=(r == 0),
                                stop=(r == 7 and half == 1))
                        if half == 0:
                            rhs = (self.scrB[0:1, WREP + tb * 512:
                                             WREP + tb * 512 + 512]
                                   if e_i < 8 else self.ones_row[0:1, 256:768])
                            inst = e.matmul(ps[:, :],
                                            self.rowB[0:1, 1024 + co * 128:
                                                      1024 + co * 128 + 128],
                                            rhs, start=False, stop=True)
                        return inst
                    waits = [("ve", v_h_half), ("wd", max(v_w2h, v_b2))]
                    if i >= 2:
                        waits.append(("ve", dstream_done))
                    p.add("tensor", mk_d, waits=waits, inc="pe")
                    v_d = self.c["pe"]
                    p.add("vector", lambda e, co=co, tb=tb, ps=ps: e.tensor_tensor(
                        self.stv(self.stream, co, tb * 512, 512), ps[:, :],
                        self.stv(self.stream, co, tb * 512, 512), ALU.add),
                        waits=[("pe", v_d), (st_sem, st_val)], inc="ve")
                    dstream_done = self.c["ve"]
                w2_free = self.c["pe"]
            b2_free = self.c["pe"]
        self.v_stream = ("ve", self.c["ve"])
        self.w_free_W = self.c["pe"]
        self.w_free_Bx = self.c["pe"]
        p.emit()
        es2.close()

    def ph_store(self):
        p = Plan(self)
        sem, val = self.v_stream
        p.add("sync", lambda e: e.dma_start(
            out=self.out.ap().rearrange("(c p) t -> p c t", p=128),
            in_=self.stream[:, :].rearrange("p (c t) -> p c t", t=T)),
            waits=[(sem, val)], inc="xd")
        p.wait("sync", [("xd", self.c["xd"])])
        p.emit()

    def build(self):
        self.ph_load()
        phases = [
            lambda: self.ph_attn(0),
            lambda: self.ph_ln(0),
            lambda: self.ph_l0_moe(),
            lambda: self.ph_ln(1),
            lambda: self.ph_attn(1),
            lambda: self.ph_ln(2),
            lambda: self.ph_l1_moe(),
            lambda: self.ph_ln(3),
        ]
        for i, ph in enumerate(phases):
            if i >= self.upto:
                break
            ph()
        self.ph_store()
        return self.nc


def host_prep(inputs):
    f = np.ascontiguousarray
    H = {}
    H["c_ident"] = np.eye(128, dtype=np.float32)
    sel = np.zeros((16, 1024), dtype=np.float32)
    for c in range(8):
        sel[2 * c, c * 128: c * 128 + 64] = 1.0
        sel[2 * c + 1, c * 128 + 64: c * 128 + 128] = 1.0
    H["c_sel16"] = sel
    H["l0_wq"] = f(inputs["L0_attn_Wq"].T)
    H["l0_wk"] = f(inputs["L0_attn_Wk"].T)
    H["l0_wv"] = f(inputs["L0_attn_Wv"].T)
    H["l0_wo"] = f(inputs["L0_attn_Wo"].T)
    H["l0_bq"] = f(inputs["L0_attn_bq"].reshape(8, 128).T)
    H["l0_bo"] = f(inputs["L0_attn_bo"].reshape(8, 128).T)
    H["l0_bk"] = f(inputs["L0_attn_bk"].reshape(1, 1024))
    H["l0_bv"] = f(inputs["L0_attn_bv"].reshape(1, 1024))
    H["l0_gw"] = f(inputs["L0_ffn_gate_w"].T)
    H["l0_gb"] = f(inputs["L0_ffn_gate_bias"].reshape(1, 32))
    H["l0_eg"] = f(inputs["L0_ffn_expert_gate"].transpose(0, 2, 1))
    H["l0_eu"] = f(inputs["L0_ffn_expert_up"].transpose(0, 2, 1))
    H["l0_ed"] = f(inputs["L0_ffn_expert_down"].transpose(0, 2, 1))
    H["l0_sg"] = f(inputs["L0_ffn_shared_gate"].T)
    H["l0_su"] = f(inputs["L0_ffn_shared_up"].T)
    H["l0_sd"] = f(inputs["L0_ffn_shared_down"].T)
    H["l1_wq"] = f(inputs["L1_attn_Wq"])
    H["l1_wk"] = f(inputs["L1_attn_Wk"])
    H["l1_wv"] = f(inputs["L1_attn_Wv"])
    H["l1_wo"] = f(inputs["L1_attn_Wo"])
    H["l1_gw"] = f(inputs["L1_ffn_gate_w"].T)
    H["l1_gb"] = f(inputs["L1_ffn_gate_b"].reshape(1, 8))
    perm = np.concatenate([512 * B + np.concatenate(
        [np.arange(0, 512, 2), np.arange(1, 512, 2)]) for B in range(8)])
    w1 = inputs["L1_ffn_w1"][:, :, perm]
    w3 = inputs["L1_ffn_w3"][:, :, perm]
    b1 = inputs["L1_ffn_b1"][:, perm]
    b3 = inputs["L1_ffn_b3"][:, perm]
    H["l1_w1"] = f(np.concatenate([w1, inputs["L1_ffn_sw1"][:, perm][None]], axis=0))
    H["l1_w3"] = f(np.concatenate([w3, inputs["L1_ffn_sw3"][:, perm][None]], axis=0))
    H["l1_w2"] = f(np.concatenate([inputs["L1_ffn_w2"],
                                   inputs["L1_ffn_sw2"][None]], axis=0))
    b1a = np.concatenate([b1, inputs["L1_ffn_sb1"][perm][None]], axis=0)
    b3a = np.concatenate([b3, inputs["L1_ffn_sb3"][perm][None]], axis=0)
    H["l1_b1"] = f(b1a.reshape(9, 32, 128).transpose(0, 2, 1))
    H["l1_b3"] = f(b3a.reshape(9, 32, 128).transpose(0, 2, 1))
    H["l1_b2"] = f(np.concatenate([inputs["L1_ffn_b2"],
                                   inputs["L1_ffn_sb2"][None]], axis=0)
                   .reshape(9, 1, 1024))
    return H


def build_kernel(upto=99):
    return MK(upto=upto).build()


def kernel(**inputs):
    H = host_prep(inputs)
    x = inputs["L0_attn_x"].reshape(16384, 1024)
    xT = np.ascontiguousarray(x.T)
    nc = build_kernel()
    in_maps = []
    for c in range(NCORES):
        m = dict(H)
        m["xT"] = np.ascontiguousarray(xT[:, c * T:(c + 1) * T])
        in_maps.append(m)
    res = run_bass_kernel_spmd(nc, in_maps, core_ids=list(range(NCORES)))
    outs = [np.asarray(r["out"]).T for r in res.results]
    return np.concatenate(outs, axis=0).reshape(4, 4096, 1024)


# version 02
# speedup vs baseline: 1.3514x; 1.3514x over previous
import numpy as np
from contextlib import ExitStack

import concourse.bass as bass
import concourse.mybir as mybir
from concourse.bass_utils import run_bass_kernel_spmd

F32 = mybir.dt.float32
BF16 = mybir.dt.bfloat16
AF = mybir.ActivationFunctionType
ALU = mybir.AluOpType
AX = mybir.AxisListType.X

NCORES = 8
T = 2048
D = 1024
DCH = 8
NTB = 4
NTC = 16
NH = 16

ENGS = ("tensor", "vector", "scalar", "gpsimd", "sync")


class Plan:

    def __init__(self, mk):
        self.mk = mk
        self.steps = {e: [] for e in ENGS}

    SELF_SEM = {"vector": "ve", "scalar": "ae"}

    def add(self, eng, fn, waits=None, inc=None, inc_amt=None):
        waits = list(waits or [])
        s = self.SELF_SEM.get(eng)
        if s is not None and self.mk.c[s] > 0:
            waits.append((s, self.mk.c[s]))
        if inc is not None and inc_amt is None:
            inc_amt = 16 if inc in ("wd", "xd") else 1
        if inc is not None:
            self.mk.c[inc] += inc_amt
        self.steps[eng].append((fn, waits, inc, inc_amt))
        return self.mk.c[inc] if inc else None

    def wait(self, eng, waits):
        self.steps[eng].append((None, list(waits), None, None))

    def emit(self):
        mk = self.mk
        with mk.nc.Block() as block:
            for eng in ENGS:
                if not self.steps[eng]:
                    continue

                def make(eng):
                    def run(proxy):
                        for fn, waits, inc, amt in self.steps[eng]:
                            for (sname, val) in waits:
                                if val and val > 0:
                                    proxy.wait_ge(mk.s[sname], val)
                            if fn is not None:
                                inst = fn(proxy)
                                if inc is not None:
                                    inst.then_inc(mk.s[inc], amt)
                    return run

                getattr(block, eng)(make(eng))


class MK:
    def __init__(self, upto=99):
        self.upto = upto
        nc = bass.Bass("TRN2", target_bir_lowering=False)
        self.nc = nc
        self.es = ExitStack()
        es = self.es

        dr = lambda name, shape: nc.dram_tensor(name, list(shape), F32,
                                                kind="ExternalInput")
        self.d = d = {}
        d["xT"] = dr("xT", (D, T))
        d["c_ident"] = dr("c_ident", (128, 128))
        d["c_sel16"] = dr("c_sel16", (16, 1024))
        for w in ("l0_wq", "l0_wk", "l0_wv", "l0_wo",
                  "l1_wq", "l1_wk", "l1_wv", "l1_wo",
                  "l0_sg", "l0_su", "l0_sd"):
            d[w] = dr(w, (D, D))
        d["l0_bq"] = dr("l0_bq", (128, 8))
        d["l0_bo"] = dr("l0_bo", (128, 8))
        d["l0_bk"] = dr("l0_bk", (1, D))
        d["l0_bv"] = dr("l0_bv", (1, D))
        d["l0_gw"] = dr("l0_gw", (D, 32))
        d["l0_gb"] = dr("l0_gb", (1, 32))
        d["l0_eg"] = dr("l0_eg", (32, D, 512))
        d["l0_eu"] = dr("l0_eu", (32, D, 512))
        d["l0_ed"] = dr("l0_ed", (32, 512, D))
        d["l1_gw"] = dr("l1_gw", (D, 8))
        d["l1_gb"] = dr("l1_gb", (1, 8))
        d["l1_w1"] = dr("l1_w1", (9, D, 4096))
        d["l1_w3"] = dr("l1_w3", (9, D, 4096))
        d["l1_w2"] = dr("l1_w2", (9, 2048, D))
        d["l1_b1"] = dr("l1_b1", (9, 128, 32))
        d["l1_b3"] = dr("l1_b3", (9, 128, 32))
        d["l1_b2"] = dr("l1_b2", (9, 1, D))
        self.out = nc.dram_tensor("out", [D, T], F32, kind="ExternalOutput")
        self.cc_in = [nc.dram_tensor(f"cc_in{i}", [128, 520], F32) for i in range(2)]
        self.cc_out = [nc.dram_tensor(f"cc_out{i}", [128, 520], F32) for i in range(2)]

        sb = lambda name, shape, dt: es.enter_context(
            nc.sbuf_tensor(name, list(shape), dt))
        self.stream = sb("stream", (128, DCH * T), F32)
        self.xb = sb("xb", (128, DCH * T), BF16)
        self.A = sb("A", (128, DCH * T), BF16)
        self.W = sb("W", (128, 16384), BF16)
        self.Bx = sb("Bx", (128, 12288), BF16)
        self.scrF = sb("scrF", (128, 2112), F32)
        self.scrB = sb("scrB", (128, 4416), BF16)
        self.rowB = sb("rowB", (1, 2048), BF16)
        self.ident = sb("ident", (128, 128), BF16)
        self.ones_col = sb("ones_col", (128, 1), BF16)
        self.ones_row = sb("ones_row", (1, 768), BF16)

        self.s = {}
        self.c = {}
        for name in ("pe", "ve", "ae", "ge", "wd", "xd", "cc"):
            self.s[name] = es.enter_context(nc.semaphore(name))
            self.c[name] = 0

        self.w_free_W = 0
        self.w_free_Bx = 0
        self.fence = {"pe": 0, "ve": 0, "ae": 0}
        self.v_stream = ("xd", 0)
        self.v_xb = ("wd", 0)

    def stv(self, buf, c, lo, n):
        return buf[:, c * T + lo: c * T + lo + n]

    def new_plan(self):
        p = Plan(self)
        f = {k: self.c[k] for k in ("pe", "ve", "ae")}
        p.wait("tensor", [("ve", f["ve"]), ("ae", f["ae"])])
        p.wait("vector", [("pe", f["pe"]), ("ae", f["ae"])])
        p.wait("scalar", [("pe", f["pe"]), ("ve", f["ve"])])
        self.fence = f
        return p

    def wload(self, p, dst_ap, src_ap, wait_pe=0, extra_waits=None):
        waits = list(extra_waits or [])
        if wait_pe:
            waits.append(("pe", wait_pe))
        if len(src_ap.ap) == 3 and len(dst_ap.ap) == 2:
            n = src_ap.ap[2][1]
            dst_ap = dst_ap.rearrange("p (r n) -> p r n", n=n)
        p.add("gpsimd", lambda e, dd=dst_ap, ss=src_ap: e.dma_start(out=dd, in_=ss),
              waits=waits, inc="wd")
        return self.c["wd"]

    def chunked(self, ap):
        return ap.rearrange("(r p) n -> p r n", p=128)

    def fap(self, buf, lo, n, dims):
        base = buf[:, lo: lo + n]
        return bass.AP(base.tensor, base.offset, [base.ap[0]] + list(dims))

    def ph_load(self):
        p = Plan(self)
        p.add("vector", lambda e: e.memset(self.ones_col[:, :], 1.0), inc="ve")
        p.add("vector", lambda e: e.memset(self.ones_row[0:1, 0:128], 1.0), inc="ve")
        p.add("vector", lambda e: e.memset(self.ones_row[0:1, 128:256], -1.0), inc="ve")
        p.add("vector", lambda e: e.memset(self.ones_row[0:1, 256:768], 1.0), inc="ve")
        p.add("vector", lambda e: e.memset(self.scrF[0:1, 1552:1680], 1.0), inc="ve")
        self.v_consts = self.c["ve"]
        xap = self.d["xT"].ap().rearrange("(c p) t -> p c t", p=128)
        st3 = self.stream[:, :].rearrange("p (c t) -> p c t", t=T)
        xb3 = self.xb[:, :].rearrange("p (c t) -> p c t", t=T)
        p.add("sync", lambda e: e.dma_start(out=st3, in_=xap), inc="xd")
        self.v_stream = ("xd", self.c["xd"])
        p.add("gpsimd", lambda e: e.dma_start(out=xb3, in_=xap), inc="wd")
        self.v_xb = ("wd", self.c["wd"])
        p.add("gpsimd", lambda e: e.dma_start(out=self.ident[:, :],
                                              in_=self.d["c_ident"].ap()), inc="wd")
        self.v_ident = ("wd", self.c["wd"])
        p.emit()

    def ph_attn(self, L):
        nc = self.nc
        has_bias = (L == 0)
        es2 = ExitStack()
        psK0 = es2.enter_context(nc.psum_tensor(f"psK0_{L}", [128, 512], F32))
        psK1 = es2.enter_context(nc.psum_tensor(f"psK1_{L}", [128, 512], F32))
        psV0 = es2.enter_context(nc.psum_tensor(f"psV0_{L}", [128, 512], F32))
        psV1 = es2.enter_context(nc.psum_tensor(f"psV1_{L}", [128, 512], F32))
        es_kv = ExitStack()
        kvA = [es_kv.enter_context(nc.psum_tensor(f"kvA{i}_{L}", [128, 260], F32))
               for i in range(2)]
        kvB = [es_kv.enter_context(nc.psum_tensor(f"kvB{i}_{L}", [128, 260], F32))
               for i in range(2)]

        pre = "l0_" if L == 0 else "l1_"
        KFM, VAUG, BDG, KVB = 0, 2048, 1024, 1152

        p = self.new_plan()
        fence_pe = self.fence["pe"]
        v_wk = self.wload(p, self.W[:, 0:8192],
                          self.chunked(self.d[pre + "wk"].ap()), wait_pe=self.w_free_W)
        v_wv = self.wload(p, self.W[:, 8192:16384],
                          self.chunked(self.d[pre + "wv"].ap()))
        v_wq = self.wload(p, self.Bx[:, 0:8192],
                          self.chunked(self.d[pre + "wq"].ap()), wait_pe=self.w_free_Bx)
        v_bk = v_bv = v_bq = v_bo = 0
        if has_bias:
            v_bk = self.wload(p, self.rowB[0:1, 0:1024], self.d["l0_bk"].ap())
            v_bv = self.wload(p, self.rowB[0:1, 1024:2048], self.d["l0_bv"].ap())
            p.add("sync", lambda e: e.dma_start(out=self.scrF[:, 1712:1720],
                                                in_=self.d["l0_bq"].ap()), inc="xd")
            v_bq = self.c["xd"]
            p.add("sync", lambda e: e.dma_start(out=self.scrF[:, 1720:1728],
                                                in_=self.d["l0_bo"].ap()), inc="xd")
            v_bo = self.c["xd"]
        sel_ap = self.scrF[0:16, 0:1024]

        p.add("vector", lambda e: e.memset(self.scrB[:, VAUG:VAUG + 1040], 1.0),
              inc="ve")
        v_vaug_ones = self.c["ve"]
        xb_sem, xb_val = self.v_xb

        fm_done = {}
        vaug_done = {}
        kvacc_done = {}
        for tc in range(NTC):
            kb = KFM + (tc % 2) * 1024
            kbuf = self.scrB[:, kb:kb + 1024]

            def mk_k(e, tc=tc, half=0, ps=psK0):
                for r in range(DCH):
                    inst = e.matmul(ps[:, :],
                                    self.stv(self.xb, r, tc * 128, 128),
                                    self.W[:, r * 1024 + half * 512:
                                           r * 1024 + half * 512 + 512],
                                    start=(r == 0),
                                    stop=(r == DCH - 1 and not has_bias))
                if has_bias:
                    inst = e.matmul(ps[:, :], self.ones_row[0:1, 0:128],
                                    self.rowB[0:1, half * 512: half * 512 + 512],
                                    start=False, stop=True)
                return inst

            waits = [("wd", max(v_wk, xb_val if xb_sem == "wd" else 0, v_bk)),
                     ("ve", self.v_consts)]
            if xb_sem == "ae":
                waits.append(("ae", xb_val))
            if tc >= 2:
                waits.append(("ve", fm_done[tc - 2][0]))
            p.add("tensor", lambda e, tc=tc: mk_k(e, tc, 0, psK0), waits=waits,
                  inc="pe")
            p.add("tensor", lambda e, tc=tc: mk_k(e, tc, 1, psK1),
                  waits=[("ve", fm_done[tc - 2][1])] if tc >= 2 else [], inc="pe")
            v_k_mm = self.c["pe"]

            def mk_v(e, tc=tc, half=0, ps=psV0):
                for r in range(DCH):
                    inst = e.matmul(ps[:, :],
                                    self.stv(self.xb, r, tc * 128, 128),
                                    self.W[:, 8192 + r * 1024 + half * 512:
                                           8192 + r * 1024 + half * 512 + 512],
                                    start=(r == 0),
                                    stop=(r == DCH - 1 and not has_bias))
                if has_bias:
                    inst = e.matmul(ps[:, :], self.ones_row[0:1, 0:128],
                                    self.rowB[0:1, 1024 + half * 512:
                                              1024 + half * 512 + 512],
                                    start=False, stop=True)
                return inst

            vwaits = [("wd", max(v_wv, v_bv))]
            if tc >= 1:
                vwaits.append(("ae", vaug_done[tc - 1]))
            p.add("tensor", lambda e, tc=tc: mk_v(e, tc, 0, psV0), waits=vwaits,
                  inc="pe")
            p.add("tensor", lambda e, tc=tc: mk_v(e, tc, 1, psV1), inc="pe")
            v_v_mm = self.c["pe"]

            fd = []
            for half, ps in ((0, psK0), (1, psK1)):
                s0 = self.scrF[:, 0:512]
                s1 = self.scrF[:, 512:1024]
                kout = kbuf[:, half * 512: half * 512 + 512]
                p.add("vector", lambda e, ps=ps, s0=s0: e.tensor_scalar(
                    s0, ps[:, :], 0.0, None, ALU.min),
                    waits=[("pe", v_k_mm)], inc="ve")
                p.add("scalar", lambda e, s0=s0, s1=s1: e.activation(s1, s0, AF.Exp),
                      waits=[("ve", self.c["ve"])], inc="ae")
                p.add("vector", lambda e, ps=ps, s1=s1, kout=kout:
                      e.scalar_tensor_tensor(kout, ps[:, :], 0.0, s1,
                                             ALU.max, ALU.add),
                      waits=[("ae", self.c["ae"])], inc="ve")
                fd.append(self.c["ve"])
            fm_done[tc] = fd

            for half, ps in ((0, psV0), (1, psV1)):
                def mk_vc(e, half=half, ps=ps):
                    src = bass.AP(ps.ap().tensor, ps.ap().offset,
                                  [ps.ap().ap[0], [64, 8], [1, 64]])
                    dst = self.fap(self.scrB, VAUG + half * 520, 520,
                                   [[65, 8], [1, 64]])
                    return e.activation(dst, src, AF.Identity)
                p.add("scalar", mk_vc,
                      waits=[("pe", v_v_mm), ("ve", v_vaug_ones)], inc="ae")
            vaug_done[tc] = self.c["ae"]

            def mk_kv(e, tc=tc, kbuf=kbuf):
                for h in range(NH):
                    pair, base = h // 2, (h % 2) * 64
                    ps = (kvA if pair < 4 else kvB)[tc % 2]
                    col = (pair % 4) * 65
                    inst = e.matmul(ps[base:base + 64, col:col + 65],
                                    kbuf[:, h * 64: h * 64 + 64],
                                    self.scrB[:, VAUG + h * 65: VAUG + h * 65 + 65],
                                    start=(h in (0, 1, 8, 9)),
                                    stop=(h in (6, 7, 14, 15)),
                                    skip_group_check=True)
                return inst
            kvw = [("ve", fd[1]), ("ae", vaug_done[tc])]
            if tc >= 2:
                kvw.append(("ve", kvacc_done[tc - 2]))
            p.add("tensor", mk_kv, waits=kvw, inc="pe")
            v_kvmm_tc = self.c["pe"]
            kvsb_t = self.scrF[:, 1024:1544]
            if tc == 0:
                p.add("vector", lambda e, tc=tc: e.tensor_copy(
                    kvsb_t[:, 0:260], kvA[tc % 2][:, :]),
                    waits=[("pe", v_kvmm_tc)], inc="ve")
                p.add("vector", lambda e, tc=tc: e.tensor_copy(
                    kvsb_t[:, 260:520], kvB[tc % 2][:, :]), inc="ve")
            else:
                p.add("vector", lambda e, tc=tc: e.tensor_tensor(
                    kvsb_t[:, 0:260], kvA[tc % 2][:, :], kvsb_t[:, 0:260],
                    ALU.add), waits=[("pe", v_kvmm_tc)], inc="ve")
                p.add("vector", lambda e, tc=tc: e.tensor_tensor(
                    kvsb_t[:, 260:520], kvB[tc % 2][:, :], kvsb_t[:, 260:520],
                    ALU.add), inc="ve")
            kvacc_done[tc] = self.c["ve"]
        v_kv_mm = self.c["pe"]

        kvsb = self.scrF[:, 1024:1544]
        v_kv_sb = kvacc_done[NTC - 1]
        if getattr(self, "dbg_attn", 0) == 2:
            p.add("vector", lambda e: e.tensor_copy(self.stream[:, 0:520], kvsb),
                  waits=[("ve", v_kv_sb)], inc="ve")
            v_dloc = self.c["ve"]
        p.add("sync", lambda e: e.dma_start(out=self.cc_in[L][:, :], in_=kvsb),
              waits=[("ve", v_kv_sb)], inc="xd")
        v_bounce = self.c["xd"]
        p.add("gpsimd", lambda e: e.collective_compute(
            "AllReduce", ALU.add, replica_groups=[[0, 1], [2, 3], [4, 5], [6, 7]],
            ins=[self.cc_in[L].ap().opt()], outs=[self.cc_out[L].ap().opt()]),
            waits=[("xd", v_bounce)], inc="cc")
        v_cc = self.c["cc"]
        p.add("sync", lambda e: e.dma_start(out=kvsb, in_=self.cc_out[L][:, :]),
              waits=[("cc", v_cc)], inc="xd")
        v_kv_back = self.c["xd"]

        es_kv.close()
        psQ0 = es2.enter_context(nc.psum_tensor(f"psQ0_{L}", [128, 512], F32))
        psQ1 = es2.enter_context(nc.psum_tensor(f"psQ1_{L}", [128, 512], F32))
        q_evac = {}
        for i, (c, tb) in enumerate((c, tb) for c in range(DCH) for tb in range(NTB)):
            ps = psQ0 if i % 2 == 0 else psQ1

            def mk_q(e, c=c, tb=tb, ps=ps):
                for r in range(DCH):
                    inst = e.matmul(ps[:, :],
                                    self.Bx[:, r * 1024 + c * 128:
                                            r * 1024 + c * 128 + 128],
                                    self.stv(self.xb, r, tb * 512, 512),
                                    start=(r == 0), stop=(r == DCH - 1))
                return inst
            waits = [("wd", v_wq)]
            if i < 2:
                waits.append(("ve", v_kv_sb))
            if i >= 2:
                waits.append(("ve", q_evac[i - 2]))
            p.add("tensor", mk_q, waits=waits, inc="pe")
            v_q_mm = self.c["pe"]
            s0, s1 = self.scrF[:, 0:512], self.scrF[:, 512:1024]
            qout = self.stv(self.A, c, tb * 512, 512)
            if has_bias:
                bq = self.scrF[:, 1712 + c: 1713 + c]
                p.add("vector", lambda e, ps=ps, s0=s0, bq=bq: e.tensor_scalar(
                    s0, ps[:, :], bq, 0.0, ALU.add, ALU.min),
                    waits=[("pe", v_q_mm), ("xd", v_bq)], inc="ve")
                p.add("scalar", lambda e, s0=s0, s1=s1: e.activation(s1, s0, AF.Exp),
                      waits=[("ve", self.c["ve"])], inc="ae")
                p.add("scalar", lambda e, ps=ps, s0=s0, bq=bq: e.activation(
                    s0, ps[:, :], AF.Relu, bias=bq), inc="ae")
                p.add("vector", lambda e, s1=s1, s0=s0, qout=qout: e.tensor_tensor(
                    qout, s0, s1, ALU.add), waits=[("ae", self.c["ae"])], inc="ve")
            else:
                p.add("vector", lambda e, ps=ps, s0=s0: e.tensor_scalar(
                    s0, ps[:, :], 0.0, None, ALU.min),
                    waits=[("pe", v_q_mm)], inc="ve")
                p.add("scalar", lambda e, s0=s0, s1=s1: e.activation(s1, s0, AF.Exp),
                      waits=[("ve", self.c["ve"])], inc="ae")
                p.add("vector", lambda e, ps=ps, s1=s1, qout=qout:
                      e.scalar_tensor_tensor(qout, ps[:, :], 0.0, s1,
                                             ALU.max, ALU.add),
                      waits=[("ae", self.c["ae"])], inc="ve")
            q_evac[i] = self.c["ve"]
        v_q_done = self.c["ve"]
        p.add("sync", lambda e: e.dma_start(out=sel_ap, in_=self.d["c_sel16"].ap()),
              waits=[("ve", v_q_done)], inc="xd")
        v_sel = self.c["xd"]

        kvbf = self.scrB[:, KVB:KVB + 520]
        p.add("scalar", lambda e: e.activation(kvbf, kvsb, AF.Identity),
              waits=[("xd", v_kv_back)], inc="ae")
        v_kvbf = self.c["ae"]
        if getattr(self, "dbg_attn", 0) == 2:
            p.add("vector", lambda e: e.tensor_copy(self.stream[:, 2048:2568], kvsb),
                  waits=[("xd", v_kv_back)], inc="ve")
            self.v_stream = ("ve", self.c["ve"])
            self.w_free_W = self.c["pe"]
            self.w_free_Bx = self.c["pe"]
            p.emit()
            es2.close()
            return
        p.add("vector", lambda e: e.memset(self.scrB[:, BDG:BDG + 128], 0.0),
              waits=[("pe", v_kv_mm)], inc="ve")
        v_bz = self.c["ve"]
        for h in range(NH):
            pair, base = h // 2, (h % 2) * 64
            src = self.scrB[base:base + 64,
                            KVB + pair * 65 + 64: KVB + pair * 65 + 65]
            dst = self.scrB[base:base + 64,
                            BDG + (h // 2) * 16 + h: BDG + (h // 2) * 16 + h + 1]
            p.add("scalar", lambda e, src=src, dst=dst: e.activation(
                dst, src, AF.Identity),
                waits=[("ae", v_kvbf), ("ve", v_bz)], inc="ae")
        v_bdiag = self.c["ae"]

        psN = es2.enter_context(nc.psum_tensor(f"psN_{L}", [16, 512], F32))
        normF = self.scrF[0:16, 1024:1536]
        op = ALU.max if L == 0 else ALU.add
        qs_done = {}
        v_prev_bc = 0
        for tb in range(NTB):
            def mk_n(e, tb=tb):
                for r in range(DCH):
                    inst = e.matmul(psN[:, :],
                                    self.scrB[:, BDG + r * 16: BDG + r * 16 + 16],
                                    self.stv(self.A, r, tb * 512, 512),
                                    start=(r == 0), stop=(r == DCH - 1))
                return inst
            waits = [("ae", v_bdiag), ("ve", v_q_done)]
            p.add("tensor", mk_n, waits=waits, inc="pe")
            p.add("scalar", lambda e: e.activation(normF, psN[:, :], AF.Identity),
                  waits=[("pe", self.c["pe"]), ("pe", v_prev_bc)], inc="ae")
            v_ne = self.c["ae"]
            p.add("vector", lambda e: e.tensor_scalar(normF, normF, 1e-6, None, op),
                  waits=[("ae", v_ne)], inc="ve")
            p.add("vector", lambda e: e.reciprocal(normF, normF), inc="ve")
            v_recf = self.c["ve"]
            for c in range(DCH):
                i = c * NTB + tb
                ps = psK0 if c % 2 == 0 else psK1
                def mk_b(e, c=c, ps=ps):
                    return e.matmul(ps[:, :], sel_ap[:, c * 128:(c + 1) * 128],
                                    normF, start=True, stop=True)
                waits = [("ve", v_recf), ("xd", v_sel)]
                if c >= 2:
                    waits.append(("ve", qs_done[(c - 2) * NTB + tb]))
                p.add("tensor", mk_b, waits=waits, inc="pe")
                v_b_mm = self.c["pe"]
                p.add("vector", lambda e, c=c, tb=tb, ps=ps: e.tensor_tensor(
                    self.stv(self.xb, c, tb * 512, 512),
                    self.stv(self.A, c, tb * 512, 512), ps[:, :], ALU.mult),
                    waits=[("pe", v_b_mm)], inc="ve")
                qs_done[i] = self.c["ve"]
            v_prev_bc = self.c["pe"]

        at_done = {}
        for i, (pair, tb) in enumerate((pr, tb) for pr in range(DCH)
                                       for tb in range(NTB)):
            ps = psV0 if i % 2 == 0 else psV1
            def mk_o(e, pair=pair, tb=tb, ps=ps):
                for hh in range(2):
                    base = hh * 64
                    inst = e.matmul(
                        ps[base:base + 64, :],
                        self.scrB[base:base + 64,
                                  KVB + pair * 65: KVB + pair * 65 + 64],
                        self.xb[base:base + 64,
                                pair * T + tb * 512: pair * T + tb * 512 + 512],
                        start=True, stop=True)
                return inst
            waits = [("ae", v_kvbf), ("ve", qs_done[pair * NTB + tb])]
            if i >= 2:
                waits.append(("ae", at_done[i - 2]))
            p.add("tensor", mk_o, waits=waits, inc="pe")
            v_o_mm = self.c["pe"]
            p.add("scalar", lambda e, pair=pair, tb=tb, ps=ps: e.activation(
                self.stv(self.A, pair, tb * 512, 512), ps[:, :], AF.Identity),
                waits=[("pe", v_o_mm)], inc="ae")
            at_done[i] = self.c["ae"]
        v_at_done = self.c["ae"]
        if getattr(self, "dbg_attn", 0) == 1:
            p.add("scalar", lambda e: e.activation(self.stream[:, :],
                                                   self.A[:, :], AF.Identity),
                  waits=[("ae", v_at_done)], inc="ae")
            self.v_stream = ("ae", self.c["ae"])
            self.w_free_W = self.c["pe"]
            self.w_free_Bx = self.c["pe"]
            p.emit()
            es2.close()
            return

        v_wo = self.wload(p, self.W[:, 0:8192],
                          self.chunked(self.d[pre + "wo"].ap()), wait_pe=v_kv_mm)
        st_sem, st_val = self.v_stream
        res_done = {}
        for i, (c, tb) in enumerate((c, tb) for c in range(DCH) for tb in range(NTB)):
            ps = psQ0 if i % 2 == 0 else psQ1
            def mk_p(e, c=c, tb=tb, ps=ps):
                for r in range(DCH):
                    inst = e.matmul(ps[:, :],
                                    self.W[:, r * 1024 + c * 128:
                                           r * 1024 + c * 128 + 128],
                                    self.stv(self.A, r, tb * 512, 512),
                                    start=(r == 0), stop=(r == DCH - 1))
                return inst
            waits = [("wd", v_wo), ("ae", v_at_done)]
            if i >= 2:
                waits.append(("ve", res_done[i - 2]))
            p.add("tensor", mk_p, waits=waits, inc="pe")
            v_p_mm = self.c["pe"]
            ewaits = [("pe", v_p_mm), (st_sem, st_val)]
            if has_bias:
                bo = self.scrF[:, 1720 + c: 1721 + c]
                ewaits.append(("xd", v_bo))
                p.add("vector", lambda e, c=c, tb=tb, ps=ps, bo=bo:
                      e.scalar_tensor_tensor(
                          self.stv(self.stream, c, tb * 512, 512), ps[:, :], bo,
                          self.stv(self.stream, c, tb * 512, 512),
                          ALU.add, ALU.add), waits=ewaits, inc="ve")
            else:
                p.add("vector", lambda e, c=c, tb=tb, ps=ps: e.tensor_tensor(
                    self.stv(self.stream, c, tb * 512, 512), ps[:, :],
                    self.stv(self.stream, c, tb * 512, 512), ALU.add),
                    waits=ewaits, inc="ve")
            res_done[i] = self.c["ve"]
        self.v_stream = ("ve", self.c["ve"])
        self.w_free_W = self.c["pe"]
        self.w_free_Bx = self.c["pe"]
        p.emit()
        es2.close()

    def ph_ln(self, idx):
        nc = self.nc
        es2 = ExitStack()
        psS = es2.enter_context(nc.psum_tensor(f"psS_{idx}", [1, 512], F32))
        psQ = es2.enter_context(nc.psum_tensor(f"psQg_{idx}", [1, 512], F32))
        psA = es2.enter_context(nc.psum_tensor(f"psA_{idx}", [128, 512], F32))
        psB = es2.enter_context(nc.psum_tensor(f"psB_{idx}", [128, 512], F32))
        p = self.new_plan()
        st_sem, st_val = self.v_stream
        SQ = 3072
        v_apply_prev = 0
        for tb in range(NTB):
            sq_done = {}
            for c in range(DCH):
                p.add("scalar", lambda e, c=c, tb=tb: e.activation(
                    self.stv(self.A, c, tb * 512, 512),
                    self.stv(self.stream, c, tb * 512, 512), AF.Identity),
                    waits=[(st_sem, st_val)], inc="ae")
                v_cast = self.c["ae"]
                sq = self.scrB[:, SQ + (c % 2) * 512: SQ + (c % 2) * 512 + 512]
                waits = [("ae", v_cast)]
                if c >= 2:
                    waits.append(("pe", sq_done[c - 2]))
                p.add("scalar", lambda e, c=c, tb=tb, sq=sq: e.activation(
                    sq, self.stv(self.A, c, tb * 512, 512), AF.Square),
                    waits=waits, inc="ae")
                v_sq = self.c["ae"]
                p.add("tensor", lambda e, c=c, tb=tb: e.matmul(
                    psS[:, :], self.ones_col[:, :],
                    self.stv(self.A, c, tb * 512, 512),
                    start=(c == 0), stop=(c == DCH - 1)),
                    waits=[("ae", v_cast)] + ([("ve", v_apply_prev)] if c == 0 else []),
                    inc="pe")
                p.add("tensor", lambda e, c=c, sq=sq: e.matmul(
                    psQ[:, :], self.ones_col[:, :], sq,
                    start=(c == 0), stop=(c == DCH - 1)),
                    waits=[("ae", v_sq)], inc="pe")
                sq_done[c] = self.c["pe"]
            v_stats = self.c["pe"]
            m_row = self.scrF[0:1, 1024:1536]
            e2_row = self.scrF[0:1, 0:512]
            r_row = self.scrF[0:1, 512:1024]
            p.add("scalar", lambda e: e.activation(m_row, psS[0:1, :], AF.Identity,
                                                   scale=1.0 / D),
                  waits=[("pe", v_stats)], inc="ae")
            p.add("scalar", lambda e: e.activation(e2_row, psQ[0:1, :], AF.Identity,
                                                   scale=1.0 / D), inc="ae")
            v_rows = self.c["ae"]
            p.add("vector", lambda e: e.tensor_tensor(psS[0:1, :], m_row, m_row,
                                                      ALU.mult),
                  waits=[("ae", v_rows)], inc="ve")
            p.add("vector", lambda e: e.scalar_tensor_tensor(
                psQ[0:1, :], e2_row, 1e-5, psS[0:1, :], ALU.add, ALU.subtract),
                inc="ve")
            v_var = self.c["ve"]
            p.add("scalar", lambda e: e.activation(e2_row, psQ[0:1, :], AF.Ln),
                  waits=[("ve", v_var)], inc="ae")
            p.add("scalar", lambda e: e.activation(r_row, e2_row, AF.Exp,
                                                   scale=-0.5), inc="ae")
            v_r = self.c["ae"]
            p.add("vector", lambda e: e.scalar_tensor_tensor(
                m_row, m_row, -1.0, r_row, ALU.mult, ALU.mult),
                waits=[("ae", v_r)], inc="ve")
            v_b = self.c["ve"]
            onesF = self.scrF[0:1, 1552:1680]
            p.add("tensor", lambda e: e.matmul(psA[:, :], onesF[0:1, 0:128],
                                               r_row, start=True, stop=True),
                  waits=[("ae", v_r), ("ve", v_apply_prev),
                         ("ve", self.v_consts)], inc="pe")
            p.add("tensor", lambda e: e.matmul(psB[:, :], onesF[0:1, 0:128],
                                               m_row, start=True, stop=True),
                  waits=[("ve", v_b)], inc="pe")
            v_bc = self.c["pe"]
            for c in range(DCH):
                t0 = self.scrF[:, (c % 2) * 512: (c % 2) * 512 + 512]
                p.add("vector", lambda e, c=c, tb=tb, t0=t0: e.tensor_tensor(
                    t0, self.stv(self.stream, c, tb * 512, 512), psA[:, :],
                    ALU.mult), waits=[("pe", v_bc)], inc="ve")
                p.add("vector", lambda e, c=c, tb=tb, t0=t0: e.tensor_tensor(
                    self.stv(self.stream, c, tb * 512, 512), t0, psB[:, :],
                    ALU.add), inc="ve")
                v_app = self.c["ve"]
                p.add("scalar", lambda e, c=c, tb=tb: e.activation(
                    self.stv(self.xb, c, tb * 512, 512),
                    self.stv(self.stream, c, tb * 512, 512), AF.Identity),
                    waits=[("ve", v_app)], inc="ae")
            v_apply_prev = self.c["ve"]
        self.v_stream = ("ve", self.c["ve"])
        self.v_xb = ("ae", self.c["ae"])
        self.ln_pe_done = self.c["pe"]
        p.emit()
        es2.close()

    def ph_l0_moe(self):
        nc = self.nc
        es2 = ExitStack()
        p = self.new_plan()
        fence_pe = self.fence["pe"]
        xb_sem, xb_val = self.v_xb
        st_sem, st_val = self.v_stream

        es_gate = ExitStack()
        psL = es_gate.enter_context(nc.psum_tensor("psL0g", [128, 512], F32))
        psGB = es_gate.enter_context(nc.psum_tensor("psGB0", [128, 32], F32))
        gw_sb = self.scrB[:, 0:256]
        v_gw = self.wload(p, gw_sb, self.chunked(self.d["l0_gw"].ap()),
                          wait_pe=fence_pe)
        p.add("sync", lambda e: e.dma_start(out=self.scrF[0:1, 1680:1712],
                                            in_=self.d["l0_gb"].ap()), inc="xd")
        v_gbr = self.c["xd"]
        p.add("tensor", lambda e: e.matmul(psGB[:, :], self.scrF[0:1, 1552:1680],
                                           self.scrF[0:1, 1680:1712],
                                           start=True, stop=True),
              waits=[("xd", v_gbr), ("ve", self.v_consts)], inc="pe")
        gbb = self.scrF[:, 1728:1760]
        p.add("scalar", lambda e: e.activation(gbb, psGB[:, :], AF.Identity),
              waits=[("pe", self.c["pe"])], inc="ae")
        v_gbb = self.c["ae"]
        for tc in range(NTC):
            def mk_lg(e, tc=tc):
                for r in range(DCH):
                    inst = e.matmul(psL[:, tc * 32:(tc + 1) * 32],
                                    self.stv(self.xb, r, tc * 128, 128),
                                    gw_sb[:, r * 32:(r + 1) * 32],
                                    start=(r == 0), stop=(r == DCH - 1))
                return inst
            p.add("tensor", mk_lg,
                  waits=[("wd", v_gw)] + ([(xb_sem, xb_val)] if tc == 0 else []),
                  inc="pe")
        v_lg = self.c["pe"]

        scores = self.scrF[:, 0:512]
        sc = self.scrF[:, 512:1024]
        msk = self.scrF[:, 1024:1536]
        sel = self.scrB[:, 768:1280]
        eq = self.scrB[:, 256:768]
        p.add("scalar", lambda e: e.activation(scores, psL[:, :], AF.Sigmoid),
              waits=[("pe", v_lg)], inc="ae")
        v_scores = self.c["ae"]

        seg32 = lambda ap: bass.AP(ap.tensor, ap.offset,
                                   [ap.ap[0], [32, NTC], [1, 32]])
        g48 = lambda ap: bass.AP(ap.tensor, ap.offset,
                                 [ap.ap[0], [32, NTC], [8, 4], [1, 8]])
        sm = lambda lo, n, dims: self.fap(self.scrF, lo, n, dims)
        gm1 = sm(1760, 64, [[4, NTC], [1, 4]])
        gm1b = sm(1760, 64, [[4, NTC], [1, 4], [0, 8]])
        gm2 = sm(1824, 64, [[4, NTC], [1, 4]])
        gs = gm1
        gh = sm(1888, 16, [[1, NTC]])
        ghb = sm(1888, 16, [[1, NTC], [0, 4]])
        gmask = sm(1904, 64, [[4, NTC], [1, 4]])
        gmaskb = sm(1904, 64, [[4, NTC], [1, 4], [0, 8]])
        geq = self.fap(self.scrB, 1280, 64, [[4, NTC], [1, 4]])
        mx = sm(1968, 16, [[1, NTC]])
        mxb = sm(1968, 16, [[1, NTC], [0, 32]])

        def gb_b(ap):
            return bass.AP(self.scrF.ap().tensor, gbb.offset,
                           [self.scrF.ap().ap[0], [0, NTC], [1, 32]])
        p.add("vector", lambda e: e.tensor_tensor(seg32(sc), seg32(scores),
                                                  gb_b(None), ALU.add),
              waits=[("ae", max(v_scores, v_gbb))], inc="ve")
        p.add("vector", lambda e: e.reduce_max(gm1, g48(sc), axis=AX), inc="ve")
        p.add("vector", lambda e: e.tensor_tensor(g48(eq), g48(sc), gm1b, ALU.is_ge),
              inc="ve")
        p.add("vector", lambda e: e.scalar_tensor_tensor(msk, eq, -1e9, sc,
                                                         ALU.mult, ALU.add), inc="ve")
        p.add("vector", lambda e: e.reduce_max(gm2, g48(msk), axis=AX), inc="ve")
        p.add("vector", lambda e: e.tensor_tensor(gs, gm1, gm2, ALU.add), inc="ve")
        p.add("vector", lambda e: e.reduce_max(gh, gs, axis=AX), inc="ve")
        p.add("vector", lambda e: e.tensor_tensor(gmask, gs, ghb, ALU.is_ge),
              inc="ve")
        p.add("vector", lambda e: e.scalar_tensor_tensor(gs, gmask, -1e9, gs,
                                                         ALU.mult, ALU.add), inc="ve")
        p.add("vector", lambda e: e.reduce_max(gh, gs, axis=AX), inc="ve")
        p.add("vector", lambda e: e.tensor_tensor(geq, gs, ghb, ALU.is_ge), inc="ve")
        p.add("vector", lambda e: e.tensor_tensor(gmask, gmask, geq, ALU.add),
              inc="ve")
        p.add("vector", lambda e: e.tensor_tensor(g48(msk), g48(sc), gmaskb,
                                                  ALU.mult), inc="ve")
        for it in range(8):
            p.add("vector", lambda e: e.reduce_max(mx, seg32(msk), axis=AX),
                  inc="ve")
            p.add("vector", lambda e: e.tensor_tensor(seg32(eq), seg32(msk), mxb,
                                                      ALU.is_ge), inc="ve")
            if it == 0:
                p.add("vector", lambda e: e.tensor_copy(sel, eq), inc="ve")
            else:
                p.add("vector", lambda e: e.tensor_tensor(sel, sel, eq, ALU.add),
                      inc="ve")
            if it < 7:
                p.add("vector", lambda e: e.scalar_tensor_tensor(
                    msk, eq, -1e9, msk, ALU.mult, ALU.add), inc="ve")
        p.add("vector", lambda e: e.tensor_tensor(msk, sel, scores, ALU.mult),
              inc="ve")
        p.add("vector", lambda e: e.reduce_sum(mx, seg32(msk), axis=AX), inc="ve")
        p.add("vector", lambda e: e.tensor_scalar(mx, mx, 1e-20, None, ALU.add),
              inc="ve")
        p.add("vector", lambda e: e.reciprocal(mx, mx), inc="ve")
        p.add("vector", lambda e: e.tensor_scalar(mx, mx, 2.5, None, ALU.mult),
              inc="ve")
        p.add("vector", lambda e: e.tensor_tensor(seg32(msk), seg32(msk), mxb,
                                                  ALU.mult), inc="ve")
        comb = self.scrB[:, 256:768]
        p.add("vector", lambda e: e.tensor_copy(comb, msk), inc="ve")
        v_comb = self.c["ve"]
        es_gate.close()
        if getattr(self, "dbg_moe", 0) == 1:
            p.add("scalar", lambda e: e.activation(self.stream[:, 0:512], comb,
                                                   AF.Identity),
                  waits=[("ve", v_comb)], inc="ae")
            p.add("scalar", lambda e: e.activation(self.stream[:, 512:1024],
                                                   scores, AF.Identity), inc="ae")
            p.add("scalar", lambda e: e.activation(self.stream[:, 1024:1536],
                                                   sel, AF.Identity), inc="ae")
            v_dump = self.c["ae"]
            self.v_stream = ("ae", v_dump)
            self.w_free_W = self.c["pe"]
            self.w_free_Bx = self.c["pe"]
            p.emit()
            es2.close()
            return

        psG0 = es2.enter_context(nc.psum_tensor("l0psG0", [128, 512], F32))
        psG1 = es2.enter_context(nc.psum_tensor("l0psG1", [128, 512], F32))
        psU0 = es2.enter_context(nc.psum_tensor("l0psU0", [128, 512], F32))
        psU1 = es2.enter_context(nc.psum_tensor("l0psU1", [128, 512], F32))
        psW = es2.enter_context(nc.psum_tensor("l0psW", [128, 512], F32))
        v_sg = self.wload(p, self.W[:, 0:8192],
                          self.chunked(self.d["l0_sg"].ap()), wait_pe=self.w_free_W)
        v_su = self.wload(p, self.W[:, 8192:16384],
                          self.chunked(self.d["l0_su"].ap()))
        v_sd = self.wload(p, self.Bx[:, 0:8192],
                          self.chunked(self.d["l0_sd"].ap()), wait_pe=self.w_free_Bx)
        SIL = 1344
        ZP = 832
        WREP = 2368
        z_done = {}
        for i, (zc, tb) in enumerate((zc, tb) for zc in range(DCH)
                                     for tb in range(NTB)):
            psG = psG0 if i % 2 == 0 else psG1
            psU = psU0 if i % 2 == 0 else psU1
            def mk_g(e, zc=zc, tb=tb, ps=psG, base=0, vv=v_sg):
                for r in range(DCH):
                    inst = e.matmul(ps[:, :],
                                    self.W[:, base + r * 1024 + zc * 128:
                                           base + r * 1024 + zc * 128 + 128],
                                    self.stv(self.xb, r, tb * 512, 512),
                                    start=(r == 0), stop=(r == DCH - 1))
                return inst
            waits = [("wd", max(v_sg, v_su)), ("ve", v_comb)]
            if i >= 2:
                waits.append(("ve", z_done[i - 2]))
            p.add("tensor", lambda e, f=mk_g, ps=psG: f(e, ps=ps, base=0),
                  waits=waits, inc="pe")
            p.add("tensor", lambda e, f=mk_g, ps=psU: f(e, ps=ps, base=8192),
                  inc="pe")
            v_gu = self.c["pe"]
            sil = self.scrB[:, SIL + (i % 2) * 512: SIL + (i % 2) * 512 + 512]
            p.add("scalar", lambda e, ps=psG, sil=sil: e.activation(
                sil, ps[:, :], AF.Silu), waits=[("pe", v_gu)], inc="ae")
            p.add("vector", lambda e, zc=zc, tb=tb, ps=psU, sil=sil: e.tensor_tensor(
                self.stv(self.A, zc, tb * 512, 512), sil, ps[:, :], ALU.mult),
                waits=[("ae", self.c["ae"])], inc="ve")
            z_done[i] = self.c["ve"]
        v_zsh = self.c["ve"]
        v_zsh_pe = self.c["pe"]

        dsd = {}
        for i, (co, tb) in enumerate((co, tb) for co in range(DCH)
                                     for tb in range(NTB)):
            ps = psG0 if i % 2 == 0 else psG1
            def mk_d(e, co=co, tb=tb, ps=ps):
                for r in range(DCH):
                    inst = e.matmul(ps[:, :],
                                    self.Bx[:, r * 1024 + co * 128:
                                            r * 1024 + co * 128 + 128],
                                    self.stv(self.A, r, tb * 512, 512),
                                    start=(r == 0), stop=(r == DCH - 1))
                return inst
            waits = [("wd", v_sd), ("ve", v_zsh)]
            if i >= 2:
                waits.append(("ve", dsd[i - 2]))
            p.add("tensor", mk_d, waits=waits, inc="pe")
            v_d = self.c["pe"]
            p.add("vector", lambda e, co=co, tb=tb, ps=ps: e.tensor_tensor(
                self.stv(self.stream, co, tb * 512, 512), ps[:, :],
                self.stv(self.stream, co, tb * 512, 512), ALU.add),
                waits=[("pe", v_d), (st_sem, st_val)], inc="ve")
            dsd[i] = self.c["ve"]
        dstream_done = self.c["ve"]
        v_shared_pe = self.c["pe"]

        arena = {0: self.W, 1: self.Bx}
        egu_free = {0: v_zsh_pe, 1: v_zsh_pe}
        ed_free = {0: v_shared_pe, 1: v_shared_pe}
        v_wegu = {}
        v_wed = {}

        def load_egu(e_i):
            buf = arena[e_i % 2]
            v1 = self.wload(p, buf[:, 0:4096],
                            self.chunked(self.d["l0_eg"].ap()[e_i]),
                            wait_pe=egu_free[e_i % 2])
            v2 = self.wload(p, buf[:, 4096:8192],
                            self.chunked(self.d["l0_eu"].ap()[e_i]))
            v_wegu[e_i] = max(v1, v2)

        def load_ed(e_i):
            buf = arena[e_i % 2]
            v_wed[e_i] = self.wload(p, buf[:, 8192:12288],
                                    self.chunked(self.d["l0_ed"].ap()[e_i]),
                                    wait_pe=ed_free[e_i % 2])

        load_egu(0)
        load_ed(0)
        load_egu(1)
        load_ed(1)
        wex = lambda e_i, base, r, lo, n, cw: (
            arena[e_i % 2][:, base + r * cw + lo: base + r * cw + lo + n])

        v_wrep_prev = self.c["ae"]
        for pair_i in range(16):
            e0, e1 = 2 * pair_i, 2 * pair_i + 1
            for e_idx, e_i in enumerate((e0, e1)):
                for tb in range(NTB):
                    def mk_w(e, e_i=e_i, tb=tb):
                        for q in range(4):
                            tc = tb * 4 + q
                            col = comb[:, tc * 32 + e_i: tc * 32 + e_i + 1]
                            lhsT = bass.AP(col.tensor, col.offset,
                                           [col.ap[0], [0, 128]])
                            inst = e.matmul(psW[:, q * 128:(q + 1) * 128], lhsT,
                                            self.ident[:, :], start=True, stop=True)
                        return inst
                    p.add("tensor", mk_w,
                          waits=[("ve", self.c["ve"]), ("ae", self.c["ae"])],
                          inc="pe")
                    v_wmm = self.c["pe"]
                    p.add("scalar", lambda e, tb=tb: e.activation(
                        self.scrB[:, WREP + tb * 512: WREP + tb * 512 + 512],
                        psW[:, :], AF.Identity), waits=[("pe", v_wmm)], inc="ae")
                v_wrep = self.c["ae"]
                for i, (zc, tb) in enumerate((zc, tb) for zc in range(4)
                                             for tb in range(NTB)):
                    psG = psG0 if i % 2 == 0 else psG1
                    psU = psU0 if i % 2 == 0 else psU1
                    def mk_g2(e, e_i=e_i, zc=zc, tb=tb, ps=psG, base=0):
                        for r in range(DCH):
                            inst = e.matmul(ps[:, :],
                                            wex(e_i, base, r, zc * 128, 128, 512),
                                            self.stv(self.xb, r, tb * 512, 512),
                                            start=(r == 0), stop=(r == DCH - 1))
                        return inst
                    waits = [("wd", v_wegu[e_i])]
                    if i >= 2:
                        waits.append(("ve", z_done[i - 2]))
                    else:
                        waits.append(("ve", self.c["ve"]))
                    p.add("tensor", lambda e, f=mk_g2, ps=psG: f(e, ps=ps, base=0),
                          waits=waits, inc="pe")
                    p.add("tensor", lambda e, f=mk_g2, ps=psU: f(e, ps=ps, base=4096),
                          inc="pe")
                    v_gu = self.c["pe"]
                    sil = self.scrB[:, SIL + (i % 2) * 512: SIL + (i % 2) * 512 + 512]
                    p.add("scalar", lambda e, ps=psG, sil=sil: e.activation(
                        sil, ps[:, :], AF.Silu), waits=[("pe", v_gu)], inc="ae")
                    v_sil = self.c["ae"]
                    zp = self.scrB[:, ZP:ZP + 512]
                    p.add("vector", lambda e, ps=psU, sil=sil, zp=zp: e.tensor_tensor(
                        zp, sil, ps[:, :], ALU.mult), waits=[("ae", v_sil)], inc="ve")
                    p.add("vector", lambda e, e_idx=e_idx, zc=zc, tb=tb, zp=zp:
                          e.tensor_tensor(
                              self.stv(self.A, e_idx * 4 + zc, tb * 512, 512), zp,
                              self.scrB[:, WREP + tb * 512: WREP + tb * 512 + 512],
                              ALU.mult), waits=[("ae", v_wrep)], inc="ve")
                    z_done[i] = self.c["ve"]
                if getattr(self, "dbg_moe", 0) == 2 and e_i == 0:
                    p.add("scalar", lambda e: e.activation(
                        self.stream[:, 0:2048],
                        self.scrB[:, WREP:WREP + 2048], AF.Identity),
                        waits=[("ve", self.c["ve"])], inc="ae")
                    for zc in range(4):
                        p.add("scalar", lambda e, zc=zc: e.activation(
                            self.stream[:, (zc + 1) * 2048:(zc + 2) * 2048],
                            self.A[:, zc * 2048:(zc + 1) * 2048], AF.Identity),
                            inc="ae")
                    self.v_stream = ("ae", self.c["ae"])
                    self.w_free_W = self.c["pe"]
                    self.w_free_Bx = self.c["pe"]
                    p.emit()
                    es2.close()
                    return
                egu_free[e_i % 2] = self.c["pe"]
                if e_i + 2 < 32:
                    load_egu(e_i + 2)
            v_zpair = self.c["ve"]
            dsd = {}
            for i, (co, tb) in enumerate((co, tb) for co in range(DCH)
                                         for tb in range(NTB)):
                ps = psG0 if i % 2 == 0 else psG1
                def mk_d2(e, e0=e0, e1=e1, co=co, tb=tb, ps=ps):
                    k = 0
                    for e_i in (e0, e1):
                        for r in range(4):
                            inst = e.matmul(ps[:, :],
                                            wex(e_i, 8192, r, co * 128, 128, 1024),
                                            self.stv(self.A, (e_i % 2) * 4 + r,
                                                     tb * 512, 512),
                                            start=(k == 0), stop=(k == 7))
                            k += 1
                    return inst
                waits = [("ve", v_zpair), ("wd", max(v_wed[e0], v_wed[e1]))]
                if i >= 2:
                    waits.append(("ve", dsd[i - 2]))
                elif pair_i > 0:
                    waits.append(("ve", dstream_done))
                p.add("tensor", mk_d2, waits=waits, inc="pe")
                v_d = self.c["pe"]
                p.add("vector", lambda e, co=co, tb=tb, ps=ps: e.tensor_tensor(
                    self.stv(self.stream, co, tb * 512, 512), ps[:, :],
                    self.stv(self.stream, co, tb * 512, 512), ALU.add),
                    waits=[("pe", v_d)], inc="ve")
                dsd[i] = self.c["ve"]
            dstream_done = self.c["ve"]
            ed_free[0] = self.c["pe"]
            ed_free[1] = self.c["pe"]
            if e0 + 2 < 32:
                load_ed(e0 + 2)
                load_ed(e1 + 2)
        self.v_stream = ("ve", self.c["ve"])
        self.w_free_W = self.c["pe"]
        self.w_free_Bx = self.c["pe"]
        p.emit()
        es2.close()

    def ph_l1_moe(self):
        nc = self.nc
        es2 = ExitStack()
        p = self.new_plan()
        fence_pe = self.fence["pe"]
        xb_sem, xb_val = self.v_xb
        st_sem, st_val = self.v_stream

        es_gate = ExitStack()
        psL = es_gate.enter_context(nc.psum_tensor("psL1g", [128, 128], F32))
        psGB = es_gate.enter_context(nc.psum_tensor("psGB1", [128, 8], F32))
        gw_sb = self.scrB[:, 0:64]
        v_gw = self.wload(p, gw_sb, self.chunked(self.d["l1_gw"].ap()),
                          wait_pe=fence_pe)
        p.add("sync", lambda e: e.dma_start(out=self.scrF[0:1, 1680:1688],
                                            in_=self.d["l1_gb"].ap()), inc="xd")
        v_gbr = self.c["xd"]
        p.add("tensor", lambda e: e.matmul(psGB[:, :], self.scrF[0:1, 1552:1680],
                                           self.scrF[0:1, 1680:1688],
                                           start=True, stop=True),
              waits=[("xd", v_gbr)], inc="pe")
        gbb = self.scrF[:, 1728:1736]
        p.add("scalar", lambda e: e.activation(gbb, psGB[:, :], AF.Identity),
              waits=[("pe", self.c["pe"])], inc="ae")
        v_gbb = self.c["ae"]
        for tc in range(NTC):
            def mk_lg(e, tc=tc):
                for r in range(DCH):
                    inst = e.matmul(psL[:, tc * 8:(tc + 1) * 8],
                                    self.stv(self.xb, r, tc * 128, 128),
                                    gw_sb[:, r * 8:(r + 1) * 8],
                                    start=(r == 0), stop=(r == DCH - 1))
                return inst
            p.add("tensor", mk_lg,
                  waits=[("wd", v_gw)] + ([(xb_sem, xb_val)] if tc == 0 else []),
                  inc="pe")
        v_lg = self.c["pe"]
        lg = self.scrF[:, 0:128]
        smx = self.scrF[:, 128:256]
        sel = self.scrF[:, 256:384]
        msk = self.scrF[:, 384:512]
        eq = self.scrB[:, 64:192]
        seg8 = lambda ap: bass.AP(ap.tensor, ap.offset, [ap.ap[0], [8, NTC], [1, 8]])
        mx = self.fap(self.scrF, 1968, 16, [[1, NTC]])
        mxb = self.fap(self.scrF, 1968, 16, [[1, NTC], [0, 8]])
        p.add("scalar", lambda e: e.activation(lg, psL[:, :], AF.Identity),
              waits=[("pe", v_lg)], inc="ae")
        v_lgs = self.c["ae"]
        p.add("vector", lambda e: e.reduce_max(mx, seg8(lg), axis=AX),
              waits=[("ae", v_lgs)], inc="ve")
        p.add("vector", lambda e: e.tensor_tensor(seg8(smx), seg8(lg), mxb,
                                                  ALU.subtract), inc="ve")
        v_sub = self.c["ve"]
        p.add("scalar", lambda e: e.activation(smx, smx, AF.Exp),
              waits=[("ve", v_sub)], inc="ae")
        v_exp = self.c["ae"]
        p.add("vector", lambda e: e.reduce_sum(mx, seg8(smx), axis=AX),
              waits=[("ae", v_exp)], inc="ve")
        p.add("vector", lambda e: e.reciprocal(mx, mx), inc="ve")
        p.add("vector", lambda e: e.tensor_tensor(seg8(smx), seg8(smx), mxb,
                                                  ALU.mult), inc="ve")
        def gb_b(_):
            return bass.AP(self.scrF.ap().tensor, gbb.offset,
                           [self.scrF.ap().ap[0], [0, NTC], [1, 8]])
        p.add("vector", lambda e: e.tensor_tensor(seg8(msk), seg8(smx), gb_b(None),
                                                  ALU.add),
              waits=[("ae", v_gbb)], inc="ve")
        for it in range(2):
            p.add("vector", lambda e: e.reduce_max(mx, seg8(msk), axis=AX), inc="ve")
            p.add("vector", lambda e: e.tensor_tensor(seg8(eq), seg8(msk), mxb,
                                                      ALU.is_ge), inc="ve")
            if it == 0:
                p.add("vector", lambda e: e.tensor_copy(sel, eq), inc="ve")
                p.add("vector", lambda e: e.scalar_tensor_tensor(
                    msk, eq, -1e9, msk, ALU.mult, ALU.add), inc="ve")
            else:
                p.add("vector", lambda e: e.tensor_tensor(sel, sel, eq, ALU.add),
                      inc="ve")
        p.add("vector", lambda e: e.tensor_tensor(msk, sel, smx, ALU.mult), inc="ve")
        comb = self.scrB[:, 64:192]
        p.add("vector", lambda e: e.tensor_copy(comb, msk), inc="ve")
        v_comb = self.c["ve"]
        es_gate.close()
        if getattr(self, "dbg_l1", 0) == 1:
            p.add("scalar", lambda e: e.activation(self.stream[:, 0:128], comb,
                                                   AF.Identity),
                  waits=[("ve", v_comb)], inc="ae")
            p.add("scalar", lambda e: e.activation(self.stream[:, 2048:2176],
                                                   smx, AF.Identity), inc="ae")
            p.add("scalar", lambda e: e.activation(self.stream[:, 4096:4224],
                                                   sel, AF.Identity), inc="ae")
            p.add("scalar", lambda e: e.activation(self.stream[:, 6144:6272],
                                                   lg, AF.Identity), inc="ae")
            self.v_stream = ("ae", self.c["ae"])
            self.w_free_W = self.c["pe"]
            self.w_free_Bx = self.c["pe"]
            p.emit()
            es2.close()
            return

        psY10 = es2.enter_context(nc.psum_tensor("psY10", [128, 512], F32))
        psY11 = es2.enter_context(nc.psum_tensor("psY11", [128, 512], F32))
        psY30 = es2.enter_context(nc.psum_tensor("psY30", [128, 512], F32))
        psY31 = es2.enter_context(nc.psum_tensor("psY31", [128, 512], F32))
        psD0 = es2.enter_context(nc.psum_tensor("psD0", [128, 512], F32))
        psD1 = es2.enter_context(nc.psum_tensor("psD1", [128, 512], F32))
        psW = es2.enter_context(nc.psum_tensor("psWr1", [128, 512], F32))
        E1, G, GM, LC, WREP, LS = 192, 704, 1216, 1728, 2240, 1216
        w1b = {0: 0, 1: 8192}
        w3b = {0: 4096, 1: 12288}
        slot_free = {0: self.w_free_W, 1: self.w_free_W}
        w2_free = self.w_free_Bx
        b2_free = 0
        B1C = {0: 1984, 1: 2016}
        B3C = {0: 2048, 1: 2080}
        dstream_done = 0
        self.yi = 0
        yz_done = {0: v_comb, 1: v_comb}
        v_wrep_guard = self.c["ae"]
        for e_i in range(9):
            sl = e_i % 2
            p.add("sync", lambda e, e_i=e_i, sl=sl: e.dma_start(
                out=self.scrF[:, B1C[sl]:B1C[sl] + 32],
                in_=self.d["l1_b1"].ap()[e_i]),
                waits=[("pe", fence_pe if e_i < 2 else b2_free)], inc="xd")
            p.add("sync", lambda e, e_i=e_i, sl=sl: e.dma_start(
                out=self.scrF[:, B3C[sl]:B3C[sl] + 32],
                in_=self.d["l1_b3"].ap()[e_i]), inc="xd")
            v_bias = self.c["xd"]
            v_b2 = self.wload(p, self.rowB[0:1, 1024:2048],
                              self.d["l1_b2"].ap()[e_i], wait_pe=b2_free)
            if e_i < 8:
                for tb in range(NTB):
                    def mk_w(e, e_i=e_i, tb=tb):
                        for q in range(4):
                            tc = tb * 4 + q
                            col = comb[:, tc * 8 + e_i: tc * 8 + e_i + 1]
                            lhsT = bass.AP(col.tensor, col.offset,
                                           [col.ap[0], [0, 128]])
                            inst = e.matmul(psW[:, q * 128:(q + 1) * 128], lhsT,
                                            self.ident[:, :], start=True, stop=True)
                        return inst
                    p.add("tensor", mk_w,
                          waits=[("ve", self.c["ve"]), ("ae", self.c["ae"])],
                          inc="pe")
                    v_wmm = self.c["pe"]
                    p.add("scalar", lambda e, tb=tb: e.activation(
                        self.scrB[:, WREP + tb * 512: WREP + tb * 512 + 512],
                        psW[:, :], AF.Identity), waits=[("pe", v_wmm)], inc="ae")
                v_wrep = self.c["ae"]
            for half in range(2):
                v_w2h = self.wload(
                    p, self.Bx[:, 0:8192],
                    self.chunked(self.d["l1_w2"].ap()
                                 [e_i, half * 1024:(half + 1) * 1024, :]),
                    wait_pe=w2_free)
                h_done = 0
                for cb in range(4):
                    B = half * 4 + cb
                    sb2 = B % 2
                    v_w1 = self.wload(
                        p, self.W[:, w1b[sb2]:w1b[sb2] + 4096],
                        self.chunked(self.d["l1_w1"].ap()
                                     [e_i][:, B * 512:(B + 1) * 512]),
                        wait_pe=slot_free[sb2])
                    v_w3 = self.wload(
                        p, self.W[:, w3b[sb2]:w3b[sb2] + 4096],
                        self.chunked(self.d["l1_w3"].ap()
                                     [e_i][:, B * 512:(B + 1) * 512]))
                    v_wblk = max(v_w1, v_w3)
                    for tb in range(NTB):
                        for j in range(2):
                            prod_v = {}
                            for gl in range(2):
                                fc = gl * 2 + j
                                i2 = self.yi % 2
                                self.yi += 1
                                psY1 = psY10 if i2 == 0 else psY11
                                psY3 = psY30 if i2 == 0 else psY31
                                def mk_y(e, sb2=sb2, fc=fc, tb=tb, ps=None, w13=0):
                                    base = (w1b if w13 == 0 else w3b)[sb2]
                                    for r in range(DCH):
                                        inst = e.matmul(
                                            ps[:, :],
                                            self.W[:, base + r * 512 + fc * 128:
                                                   base + r * 512 + fc * 128 + 128],
                                            self.stv(self.xb, r, tb * 512, 512),
                                            start=(r == 0), stop=(r == DCH - 1))
                                    return inst
                                waits = [("wd", v_wblk), ("ve", yz_done[i2])]
                                if xb_sem == "ae":
                                    waits.append((xb_sem, xb_val))
                                p.add("tensor",
                                      lambda e, f=mk_y, ps=psY1: f(e, ps=ps, w13=0),
                                      waits=waits, inc="pe")
                                p.add("tensor",
                                      lambda e, f=mk_y, ps=psY3: f(e, ps=ps, w13=1),
                                      inc="pe")
                                v_y = self.c["pe"]
                                bcol = B * 4 + fc
                                b1c = self.scrF[:, B1C[sl] + bcol: B1C[sl] + bcol + 1]
                                b3c = self.scrF[:, B3C[sl] + bcol: B3C[sl] + bcol + 1]
                                e1 = self.scrB[:, E1 + i2 * 256: E1 + i2 * 256 + 256] \
                                    if False else self.scrB[:, E1:E1 + 512]
                                ew = [("pe", v_y), ("xd", v_bias),
                                      ("ve", h_done)]
                                if gl == 1:
                                    ew.append(("ve", prod_v[0]))
                                p.add("scalar", lambda e, ps=psY1, e1=e1, b1c=b1c:
                                      e.activation(e1, ps[:, :], AF.Identity,
                                                   bias=b1c),
                                      waits=ew, inc="ae")
                                v_e1 = self.c["ae"]
                                dst = self.scrB[:, (G if gl == 0 else LC):
                                                (G if gl == 0 else LC) + 512]
                                p.add("vector", lambda e, ps=psY3, e1=e1, dst=dst,
                                      b3c=b3c: e.scalar_tensor_tensor(
                                          dst, ps[:, :], b3c, e1, ALU.add, ALU.mult),
                                      waits=[("ae", v_e1)], inc="ve")
                                prod_v[gl] = self.c["ve"]
                                yz_done[i2] = self.c["ve"]
                            if (getattr(self, "dbg_l1", 0) == 3 and e_i == 0
                                    and half == 0 and cb == 0 and tb == 0
                                    and j == 0):
                                p.add("scalar", lambda e: e.activation(
                                    self.stream[:, 0:512],
                                    self.scrB[:, G:G + 512], AF.Identity),
                                    waits=[("ve", prod_v[1])], inc="ae")
                                p.add("scalar", lambda e: e.activation(
                                    self.stream[:, 2048:2560],
                                    self.scrB[:, LC:LC + 512], AF.Identity),
                                    inc="ae")
                                p.add("scalar", lambda e: e.activation(
                                    self.stream[:, 4096:4608],
                                    self.scrB[:, WREP:WREP + 512], AF.Identity),
                                    inc="ae")
                                self.v_stream = ("ae", self.c["ae"])
                                self.w_free_W = self.c["pe"]
                                self.w_free_Bx = self.c["pe"]
                                p.emit()
                                es2.close()
                                return
                            gm = self.scrB[:, GM:GM + 512]
                            p.add("vector", lambda e, gm=gm: e.tensor_scalar(
                                gm, self.scrB[:, G:G + 512], 7.0, None, ALU.min),
                                waits=[("ve", prod_v[0])], inc="ve")
                            v_gm = self.c["ve"]
                            fg = self.scrB[:, E1:E1 + 512]
                            p.add("scalar", lambda e, gm=gm, fg=fg: e.activation(
                                fg, gm, AF.Silu, scale=1.702),
                                waits=[("ve", v_gm)], inc="ae")
                            v_fg = self.c["ae"]
                            ls = self.scrB[:, LS:LS + 512]
                            p.add("vector", lambda e, ls=ls: e.tensor_scalar(
                                ls, self.scrB[:, LC:LC + 512], 7.0, -7.0,
                                ALU.min, ALU.max),
                                waits=[("ve", prod_v[1]), ("ae", v_fg)],
                                inc="ve")
                            p.add("vector", lambda e, ls=ls: e.tensor_scalar(
                                ls, ls, 1.0, 1.0 / 1.702, ALU.add, ALU.mult),
                                inc="ve")
                            v_ls = self.c["ve"]
                            if e_i < 8:
                                p.add("vector", lambda e, ls=ls, tb=tb:
                                      e.tensor_tensor(
                                          ls, ls,
                                          self.scrB[:, WREP + tb * 512:
                                                    WREP + tb * 512 + 512],
                                          ALU.mult),
                                      waits=[("ae", v_wrep)], inc="ve")
                                v_ls = self.c["ve"]
                            hc = cb * 2 + j
                            p.add("vector", lambda e, hc=hc, tb=tb, fg=fg, ls=ls:
                                  e.tensor_tensor(
                                      self.stv(self.A, hc, tb * 512, 512), fg, ls,
                                      ALU.mult),
                                  waits=[("ae", v_fg)], inc="ve")
                            h_done = self.c["ve"]
                            if (getattr(self, "dbg_l1", 0) == 4 and e_i == 0
                                    and half == 0 and cb == 0 and tb == 0
                                    and j == 0):
                                p.add("scalar", lambda e, fg=fg: e.activation(
                                    self.stream[:, 0:512], fg, AF.Identity),
                                    waits=[("ae", v_fg), ("ve", h_done)], inc="ae")
                                p.add("scalar", lambda e, ls=ls: e.activation(
                                    self.stream[:, 2048:2560], ls, AF.Identity),
                                    inc="ae")
                                p.add("scalar", lambda e: e.activation(
                                    self.stream[:, 4096:4608],
                                    self.stv(self.A, 0, 0, 512), AF.Identity),
                                    inc="ae")
                                self.v_stream = ("ae", self.c["ae"])
                                self.w_free_W = self.c["pe"]
                                self.w_free_Bx = self.c["pe"]
                                p.emit()
                                es2.close()
                                return
                    slot_free[sb2] = self.c["pe"]
                v_h_half = h_done
                if getattr(self, "dbg_l1", 0) == 2 and e_i == 0 and half == 0:
                    p.add("scalar", lambda e: e.activation(
                        self.stream[:, 0:16384], self.A[:, 0:16384], AF.Identity),
                        waits=[("ve", v_h_half)], inc="ae")
                    self.v_stream = ("ae", self.c["ae"])
                    self.w_free_W = self.c["pe"]
                    self.w_free_Bx = self.c["pe"]
                    p.emit()
                    es2.close()
                    return
                dsd = {}
                for i, (co, tb) in enumerate((co, tb) for co in range(DCH)
                                             for tb in range(NTB)):
                    ps = psD0 if i % 2 == 0 else psD1
                    def mk_d(e, e_i=e_i, co=co, tb=tb, ps=ps, half=half):
                        for r in range(8):
                            inst = e.matmul(
                                ps[:, :],
                                self.Bx[:, r * 1024 + co * 128:
                                        r * 1024 + co * 128 + 128],
                                self.stv(self.A, r, tb * 512, 512),
                                start=(r == 0),
                                stop=(r == 7 and half == 1))
                        if half == 0:
                            rhs = (self.scrB[0:1, WREP + tb * 512:
                                             WREP + tb * 512 + 512]
                                   if e_i < 8 else self.ones_row[0:1, 256:768])
                            inst = e.matmul(ps[:, :],
                                            self.rowB[0:1, 1024 + co * 128:
                                                      1024 + co * 128 + 128],
                                            rhs, start=False, stop=True)
                        return inst
                    waits = [("ve", v_h_half), ("wd", max(v_w2h, v_b2))]
                    if i >= 2:
                        waits.append(("ve", dsd[i - 2]))
                    else:
                        waits.append(("ve", dstream_done))
                    p.add("tensor", mk_d, waits=waits, inc="pe")
                    v_d = self.c["pe"]
                    p.add("vector", lambda e, co=co, tb=tb, ps=ps: e.tensor_tensor(
                        self.stv(self.stream, co, tb * 512, 512), ps[:, :],
                        self.stv(self.stream, co, tb * 512, 512), ALU.add),
                        waits=[("pe", v_d), (st_sem, st_val)], inc="ve")
                    dsd[i] = self.c["ve"]
                    dstream_done = self.c["ve"]
                w2_free = self.c["pe"]
            b2_free = self.c["pe"]
        self.v_stream = ("ve", self.c["ve"])
        self.w_free_W = self.c["pe"]
        self.w_free_Bx = self.c["pe"]
        p.emit()
        es2.close()

    def ph_store(self):
        p = Plan(self)
        sem, val = self.v_stream
        p.add("sync", lambda e: e.dma_start(
            out=self.out.ap().rearrange("(c p) t -> p c t", p=128),
            in_=self.stream[:, :].rearrange("p (c t) -> p c t", t=T)),
            waits=[(sem, val)], inc="xd")
        p.wait("sync", [("xd", self.c["xd"])])
        p.emit()

    def build(self):
        self.ph_load()
        phases = [
            lambda: self.ph_attn(0),
            lambda: self.ph_ln(0),
            lambda: self.ph_l0_moe(),
            lambda: self.ph_ln(1),
            lambda: self.ph_attn(1),
            lambda: self.ph_ln(2),
            lambda: self.ph_l1_moe(),
            lambda: self.ph_ln(3),
        ]
        for i, ph in enumerate(phases):
            if i >= self.upto:
                break
            ph()
        self.ph_store()
        return self.nc


def host_prep(inputs):
    f = np.ascontiguousarray
    H = {}
    H["c_ident"] = np.eye(128, dtype=np.float32)
    sel = np.zeros((16, 1024), dtype=np.float32)
    for c in range(8):
        sel[2 * c, c * 128: c * 128 + 64] = 1.0
        sel[2 * c + 1, c * 128 + 64: c * 128 + 128] = 1.0
    H["c_sel16"] = sel
    H["l0_wq"] = f(inputs["L0_attn_Wq"].T)
    H["l0_wk"] = f(inputs["L0_attn_Wk"].T)
    H["l0_wv"] = f(inputs["L0_attn_Wv"].T)
    H["l0_wo"] = f(inputs["L0_attn_Wo"].T)
    H["l0_bq"] = f(inputs["L0_attn_bq"].reshape(8, 128).T)
    H["l0_bo"] = f(inputs["L0_attn_bo"].reshape(8, 128).T)
    H["l0_bk"] = f(inputs["L0_attn_bk"].reshape(1, 1024))
    H["l0_bv"] = f(inputs["L0_attn_bv"].reshape(1, 1024))
    H["l0_gw"] = f(inputs["L0_ffn_gate_w"].T)
    H["l0_gb"] = f(inputs["L0_ffn_gate_bias"].reshape(1, 32))
    H["l0_eg"] = f(inputs["L0_ffn_expert_gate"].transpose(0, 2, 1))
    H["l0_eu"] = f(inputs["L0_ffn_expert_up"].transpose(0, 2, 1))
    H["l0_ed"] = f(inputs["L0_ffn_expert_down"].transpose(0, 2, 1))
    H["l0_sg"] = f(inputs["L0_ffn_shared_gate"].T)
    H["l0_su"] = f(inputs["L0_ffn_shared_up"].T)
    H["l0_sd"] = f(inputs["L0_ffn_shared_down"].T)
    H["l1_wq"] = f(inputs["L1_attn_Wq"])
    H["l1_wk"] = f(inputs["L1_attn_Wk"])
    H["l1_wv"] = f(inputs["L1_attn_Wv"])
    H["l1_wo"] = f(inputs["L1_attn_Wo"])
    H["l1_gw"] = f(inputs["L1_ffn_gate_w"].T)
    H["l1_gb"] = f(inputs["L1_ffn_gate_b"].reshape(1, 8))
    perm = np.concatenate([512 * B + np.concatenate(
        [np.arange(0, 512, 2), np.arange(1, 512, 2)]) for B in range(8)])
    w1 = inputs["L1_ffn_w1"][:, :, perm]
    w3 = inputs["L1_ffn_w3"][:, :, perm]
    b1 = inputs["L1_ffn_b1"][:, perm]
    b3 = inputs["L1_ffn_b3"][:, perm]
    H["l1_w1"] = f(np.concatenate([w1, inputs["L1_ffn_sw1"][:, perm][None]], axis=0))
    H["l1_w3"] = f(np.concatenate([w3, inputs["L1_ffn_sw3"][:, perm][None]], axis=0))
    H["l1_w2"] = f(np.concatenate([inputs["L1_ffn_w2"],
                                   inputs["L1_ffn_sw2"][None]], axis=0))
    b1a = np.concatenate([b1, inputs["L1_ffn_sb1"][perm][None]], axis=0)
    b3a = np.concatenate([b3, inputs["L1_ffn_sb3"][perm][None]], axis=0)
    H["l1_b1"] = f(b1a.reshape(9, 32, 128).transpose(0, 2, 1))
    H["l1_b3"] = f(b3a.reshape(9, 32, 128).transpose(0, 2, 1))
    H["l1_b2"] = f(np.concatenate([inputs["L1_ffn_b2"],
                                   inputs["L1_ffn_sb2"][None]], axis=0)
                   .reshape(9, 1, 1024))
    return H


def build_kernel(upto=99):
    return MK(upto=upto).build()


def kernel(**inputs):
    H = host_prep(inputs)
    x = inputs["L0_attn_x"].reshape(16384, 1024)
    xT = np.ascontiguousarray(x.T)
    nc = build_kernel()
    in_maps = []
    for c in range(NCORES):
        m = dict(H)
        m["xT"] = np.ascontiguousarray(xT[:, c * T:(c + 1) * T])
        in_maps.append(m)
    res = run_bass_kernel_spmd(nc, in_maps, core_ids=list(range(NCORES)))
    outs = [np.asarray(r["out"]).T for r in res.results]
    return np.concatenate(outs, axis=0).reshape(4, 4096, 1024)


# version 03 (final)
# speedup vs baseline: 1.4388x; 1.0647x over previous
import numpy as np
from contextlib import ExitStack

import concourse.bass as bass
import concourse.mybir as mybir
from concourse.bass_utils import run_bass_kernel_spmd

F32 = mybir.dt.float32
BF16 = mybir.dt.bfloat16
AF = mybir.ActivationFunctionType
ALU = mybir.AluOpType
AX = mybir.AxisListType.X

NCORES = 8
T = 2048
D = 1024
DCH = 8
NTB = 4
NTC = 16
NH = 16

ENGS = ("tensor", "vector", "scalar", "gpsimd", "sync")


class Plan:

    def __init__(self, mk):
        self.mk = mk
        self.steps = {e: [] for e in ENGS}

    SELF_SEM = {"vector": "ve", "scalar": "ae"}

    def add(self, eng, fn, waits=None, inc=None, inc_amt=None):
        waits = list(waits or [])
        s = self.SELF_SEM.get(eng)
        if s is not None and self.mk.c[s] > 0:
            waits.append((s, self.mk.c[s]))
        if inc is not None and inc_amt is None:
            inc_amt = 16 if inc in ("wd", "xd") else 1
        if inc is not None:
            self.mk.c[inc] += inc_amt
        self.steps[eng].append((fn, waits, inc, inc_amt))
        return self.mk.c[inc] if inc else None

    def wait(self, eng, waits):
        self.steps[eng].append((None, list(waits), None, None))

    def emit(self):
        mk = self.mk
        with mk.nc.Block() as block:
            for eng in ENGS:
                if not self.steps[eng]:
                    continue

                def make(eng):
                    def run(proxy):
                        for fn, waits, inc, amt in self.steps[eng]:
                            for (sname, val) in waits:
                                if val and val > 0:
                                    proxy.wait_ge(mk.s[sname], val)
                            if fn is not None:
                                inst = fn(proxy)
                                if inc is not None:
                                    inst.then_inc(mk.s[inc], amt)
                    return run

                getattr(block, eng)(make(eng))


class MK:
    def __init__(self, upto=99):
        self.upto = upto
        nc = bass.Bass("TRN2", target_bir_lowering=False)
        self.nc = nc
        self.es = ExitStack()
        es = self.es

        dr = lambda name, shape: nc.dram_tensor(name, list(shape), F32,
                                                kind="ExternalInput")
        self.d = d = {}
        d["xT"] = dr("xT", (D, T))
        d["c_ident"] = dr("c_ident", (128, 128))
        d["c_sel16"] = dr("c_sel16", (16, 1024))
        for w in ("l0_wq", "l0_wk", "l0_wv", "l0_wo",
                  "l1_wq", "l1_wk", "l1_wv", "l1_wo",
                  "l0_sg", "l0_su", "l0_sd"):
            d[w] = dr(w, (D, D))
        d["l0_bq"] = dr("l0_bq", (128, 8))
        d["l0_bo"] = dr("l0_bo", (128, 8))
        d["l0_bk"] = dr("l0_bk", (1, D))
        d["l0_bv"] = dr("l0_bv", (1, D))
        d["l0_gw"] = dr("l0_gw", (D, 32))
        d["l0_gb"] = dr("l0_gb", (1, 32))
        d["l0_eg"] = dr("l0_eg", (32, D, 512))
        d["l0_eu"] = dr("l0_eu", (32, D, 512))
        d["l0_ed"] = dr("l0_ed", (32, 512, D))
        d["l1_gw"] = dr("l1_gw", (D, 8))
        d["l1_gb"] = dr("l1_gb", (1, 8))
        d["l1_w1"] = dr("l1_w1", (9, D, 4096))
        d["l1_w3"] = dr("l1_w3", (9, D, 4096))
        d["l1_w2"] = dr("l1_w2", (9, 2048, D))
        d["l1_b1"] = dr("l1_b1", (9, 128, 32))
        d["l1_b3"] = dr("l1_b3", (9, 128, 32))
        d["l1_b2"] = dr("l1_b2", (9, 1, D))
        self.out = nc.dram_tensor("out", [D, T], F32, kind="ExternalOutput")
        self.cc_in = [nc.dram_tensor(f"cc_in{i}", [128, 520], F32) for i in range(2)]
        self.cc_out = [nc.dram_tensor(f"cc_out{i}", [128, 520], F32) for i in range(2)]

        sb = lambda name, shape, dt: es.enter_context(
            nc.sbuf_tensor(name, list(shape), dt))
        self.stream = sb("stream", (128, DCH * T), F32)
        self.xb = sb("xb", (128, DCH * T), BF16)
        self.A = sb("A", (128, DCH * T), BF16)
        self.W = sb("W", (128, 16384), BF16)
        self.Bx = sb("Bx", (128, 12288), BF16)
        self.scrF = sb("scrF", (128, 2112), F32)
        self.scrB = sb("scrB", (128, 4416), BF16)
        self.rowB = sb("rowB", (1, 2048), BF16)
        self.ident = sb("ident", (128, 128), BF16)
        self.ones_col = sb("ones_col", (128, 1), BF16)
        self.ones_row = sb("ones_row", (1, 768), BF16)

        self.s = {}
        self.c = {}
        for name in ("pe", "ve", "ae", "ge", "wd", "xd", "cc"):
            self.s[name] = es.enter_context(nc.semaphore(name))
            self.c[name] = 0

        self.w_free_W = 0
        self.w_free_Bx = 0
        self.fence = {"pe": 0, "ve": 0, "ae": 0}
        self.v_stream = ("xd", 0)
        self.v_xb = ("wd", 0)

    def stv(self, buf, c, lo, n):
        return buf[:, c * T + lo: c * T + lo + n]

    def new_plan(self):
        p = Plan(self)
        f = {k: self.c[k] for k in ("pe", "ve", "ae")}
        p.wait("tensor", [("ve", f["ve"]), ("ae", f["ae"])])
        p.wait("vector", [("pe", f["pe"]), ("ae", f["ae"])])
        p.wait("scalar", [("pe", f["pe"]), ("ve", f["ve"])])
        self.fence = f
        return p

    def wload(self, p, dst_ap, src_ap, wait_pe=0, extra_waits=None):
        waits = list(extra_waits or [])
        if wait_pe:
            waits.append(("pe", wait_pe))
        if len(src_ap.ap) == 3 and len(dst_ap.ap) == 2:
            n = src_ap.ap[2][1]
            dst_ap = dst_ap.rearrange("p (r n) -> p r n", n=n)
        p.add("gpsimd", lambda e, dd=dst_ap, ss=src_ap: e.dma_start(out=dd, in_=ss),
              waits=waits, inc="wd")
        return self.c["wd"]

    def chunked(self, ap):
        return ap.rearrange("(r p) n -> p r n", p=128)

    def fap(self, buf, lo, n, dims):
        base = buf[:, lo: lo + n]
        return bass.AP(base.tensor, base.offset, [base.ap[0]] + list(dims))

    def ph_load(self):
        p = Plan(self)
        p.add("vector", lambda e: e.memset(self.ones_col[:, :], 1.0), inc="ve")
        p.add("vector", lambda e: e.memset(self.ones_row[0:1, 0:128], 1.0), inc="ve")
        p.add("vector", lambda e: e.memset(self.ones_row[0:1, 128:256], -1.0), inc="ve")
        p.add("vector", lambda e: e.memset(self.ones_row[0:1, 256:768], 1.0), inc="ve")
        p.add("vector", lambda e: e.memset(self.scrF[0:1, 1552:1680], 1.0), inc="ve")
        self.v_consts = self.c["ve"]
        xap = self.d["xT"].ap().rearrange("(c p) t -> p c t", p=128)
        st3 = self.stream[:, :].rearrange("p (c t) -> p c t", t=T)
        xb3 = self.xb[:, :].rearrange("p (c t) -> p c t", t=T)
        p.add("sync", lambda e: e.dma_start(out=st3, in_=xap), inc="xd")
        self.v_stream = ("xd", self.c["xd"])
        p.add("gpsimd", lambda e: e.dma_start(out=xb3, in_=xap), inc="wd")
        self.v_xb = ("wd", self.c["wd"])
        p.add("gpsimd", lambda e: e.dma_start(out=self.ident[:, :],
                                              in_=self.d["c_ident"].ap()), inc="wd")
        self.v_ident = ("wd", self.c["wd"])
        p.emit()

    def ph_attn(self, L):
        nc = self.nc
        has_bias = (L == 0)
        es2 = ExitStack()
        psK0 = es2.enter_context(nc.psum_tensor(f"psK0_{L}", [128, 512], F32))
        psK1 = es2.enter_context(nc.psum_tensor(f"psK1_{L}", [128, 512], F32))
        psV0 = es2.enter_context(nc.psum_tensor(f"psV0_{L}", [128, 512], F32))
        psV1 = es2.enter_context(nc.psum_tensor(f"psV1_{L}", [128, 512], F32))
        es_kv = ExitStack()
        kvA = [es_kv.enter_context(nc.psum_tensor(f"kvA{i}_{L}", [128, 260], F32))
               for i in range(2)]
        kvB = [es_kv.enter_context(nc.psum_tensor(f"kvB{i}_{L}", [128, 260], F32))
               for i in range(2)]

        pre = "l0_" if L == 0 else "l1_"
        KFM, VAUG, BDG, KVB = 0, 2048, 1024, 1152

        p = self.new_plan()
        fence_pe = self.fence["pe"]
        v_wk = self.wload(p, self.W[:, 0:8192],
                          self.chunked(self.d[pre + "wk"].ap()), wait_pe=self.w_free_W)
        v_wv = self.wload(p, self.W[:, 8192:16384],
                          self.chunked(self.d[pre + "wv"].ap()))
        v_wq = self.wload(p, self.Bx[:, 0:8192],
                          self.chunked(self.d[pre + "wq"].ap()), wait_pe=self.w_free_Bx)
        v_bk = v_bv = v_bq = v_bo = 0
        if has_bias:
            v_bk = self.wload(p, self.rowB[0:1, 0:1024], self.d["l0_bk"].ap())
            v_bv = self.wload(p, self.rowB[0:1, 1024:2048], self.d["l0_bv"].ap())
            p.add("sync", lambda e: e.dma_start(out=self.scrF[:, 1712:1720],
                                                in_=self.d["l0_bq"].ap()), inc="xd")
            v_bq = self.c["xd"]
            p.add("sync", lambda e: e.dma_start(out=self.scrF[:, 1720:1728],
                                                in_=self.d["l0_bo"].ap()), inc="xd")
            v_bo = self.c["xd"]
        sel_ap = self.scrF[0:16, 0:1024]

        p.add("vector", lambda e: e.memset(self.scrB[:, VAUG:VAUG + 1040], 1.0),
              inc="ve")
        v_vaug_ones = self.c["ve"]
        xb_sem, xb_val = self.v_xb

        fm_done = {}
        vaug_done = {}
        kvacc_done = {}
        for tc in range(NTC):
            kb = KFM + (tc % 2) * 1024
            kbuf = self.scrB[:, kb:kb + 1024]

            def mk_k(e, tc=tc, half=0, ps=psK0):
                for r in range(DCH):
                    inst = e.matmul(ps[:, :],
                                    self.stv(self.xb, r, tc * 128, 128),
                                    self.W[:, r * 1024 + half * 512:
                                           r * 1024 + half * 512 + 512],
                                    start=(r == 0),
                                    stop=(r == DCH - 1 and not has_bias))
                if has_bias:
                    inst = e.matmul(ps[:, :], self.ones_row[0:1, 0:128],
                                    self.rowB[0:1, half * 512: half * 512 + 512],
                                    start=False, stop=True)
                return inst

            waits = [("wd", max(v_wk, xb_val if xb_sem == "wd" else 0, v_bk)),
                     ("ve", self.v_consts)]
            if xb_sem == "ae":
                waits.append(("ae", xb_val))
            if tc >= 2:
                waits.append(("ve", fm_done[tc - 2][0]))
            p.add("tensor", lambda e, tc=tc: mk_k(e, tc, 0, psK0), waits=waits,
                  inc="pe")
            p.add("tensor", lambda e, tc=tc: mk_k(e, tc, 1, psK1),
                  waits=[("ve", fm_done[tc - 2][1])] if tc >= 2 else [], inc="pe")
            v_k_mm = self.c["pe"]

            def mk_v(e, tc=tc, half=0, ps=psV0):
                for r in range(DCH):
                    inst = e.matmul(ps[:, :],
                                    self.stv(self.xb, r, tc * 128, 128),
                                    self.W[:, 8192 + r * 1024 + half * 512:
                                           8192 + r * 1024 + half * 512 + 512],
                                    start=(r == 0),
                                    stop=(r == DCH - 1 and not has_bias))
                if has_bias:
                    inst = e.matmul(ps[:, :], self.ones_row[0:1, 0:128],
                                    self.rowB[0:1, 1024 + half * 512:
                                              1024 + half * 512 + 512],
                                    start=False, stop=True)
                return inst

            vwaits = [("wd", max(v_wv, v_bv))]
            if tc >= 1:
                vwaits.append(("ae", vaug_done[tc - 1]))
            p.add("tensor", lambda e, tc=tc: mk_v(e, tc, 0, psV0), waits=vwaits,
                  inc="pe")
            p.add("tensor", lambda e, tc=tc: mk_v(e, tc, 1, psV1), inc="pe")
            v_v_mm = self.c["pe"]

            fd = []
            for half, ps in ((0, psK0), (1, psK1)):
                s0 = self.scrF[:, 0:512]
                s1 = self.scrF[:, 512:1024]
                kout = kbuf[:, half * 512: half * 512 + 512]
                p.add("vector", lambda e, ps=ps, s0=s0: e.tensor_scalar(
                    s0, ps[:, :], 0.0, None, ALU.min),
                    waits=[("pe", v_k_mm)], inc="ve")
                p.add("scalar", lambda e, s0=s0, s1=s1: e.activation(s1, s0, AF.Exp),
                      waits=[("ve", self.c["ve"])], inc="ae")
                p.add("vector", lambda e, ps=ps, s1=s1, kout=kout:
                      e.scalar_tensor_tensor(kout, ps[:, :], 0.0, s1,
                                             ALU.max, ALU.add),
                      waits=[("ae", self.c["ae"])], inc="ve")
                fd.append(self.c["ve"])
            fm_done[tc] = fd

            for half, ps in ((0, psV0), (1, psV1)):
                def mk_vc(e, half=half, ps=ps):
                    src = bass.AP(ps.ap().tensor, ps.ap().offset,
                                  [ps.ap().ap[0], [64, 8], [1, 64]])
                    dst = self.fap(self.scrB, VAUG + half * 520, 520,
                                   [[65, 8], [1, 64]])
                    return e.activation(dst, src, AF.Identity)
                p.add("scalar", mk_vc,
                      waits=[("pe", v_v_mm), ("ve", v_vaug_ones)], inc="ae")
            vaug_done[tc] = self.c["ae"]

            def mk_kv(e, tc=tc, kbuf=kbuf):
                for h in range(NH):
                    pair, base = h // 2, (h % 2) * 64
                    ps = (kvA if pair < 4 else kvB)[tc % 2]
                    col = (pair % 4) * 65
                    inst = e.matmul(ps[base:base + 64, col:col + 65],
                                    kbuf[:, h * 64: h * 64 + 64],
                                    self.scrB[:, VAUG + h * 65: VAUG + h * 65 + 65],
                                    start=(h in (0, 1, 8, 9)),
                                    stop=(h in (6, 7, 14, 15)),
                                    skip_group_check=True)
                return inst
            kvw = [("ve", fd[1]), ("ae", vaug_done[tc])]
            if tc >= 2:
                kvw.append(("ve", kvacc_done[tc - 2]))
            p.add("tensor", mk_kv, waits=kvw, inc="pe")
            v_kvmm_tc = self.c["pe"]
            kvsb_t = self.scrF[:, 1024:1544]
            if tc == 0:
                p.add("vector", lambda e, tc=tc: e.tensor_copy(
                    kvsb_t[:, 0:260], kvA[tc % 2][:, :]),
                    waits=[("pe", v_kvmm_tc)], inc="ve")
                p.add("vector", lambda e, tc=tc: e.tensor_copy(
                    kvsb_t[:, 260:520], kvB[tc % 2][:, :]), inc="ve")
            else:
                p.add("vector", lambda e, tc=tc: e.tensor_tensor(
                    kvsb_t[:, 0:260], kvA[tc % 2][:, :], kvsb_t[:, 0:260],
                    ALU.add), waits=[("pe", v_kvmm_tc)], inc="ve")
                p.add("vector", lambda e, tc=tc: e.tensor_tensor(
                    kvsb_t[:, 260:520], kvB[tc % 2][:, :], kvsb_t[:, 260:520],
                    ALU.add), inc="ve")
            kvacc_done[tc] = self.c["ve"]
        v_kv_mm = self.c["pe"]

        kvsb = self.scrF[:, 1024:1544]
        v_kv_sb = kvacc_done[NTC - 1]
        if getattr(self, "dbg_attn", 0) == 2:
            p.add("vector", lambda e: e.tensor_copy(self.stream[:, 0:520], kvsb),
                  waits=[("ve", v_kv_sb)], inc="ve")
            v_dloc = self.c["ve"]
        p.add("sync", lambda e: e.dma_start(out=self.cc_in[L][:, :], in_=kvsb),
              waits=[("ve", v_kv_sb)], inc="xd")
        v_bounce = self.c["xd"]
        p.add("gpsimd", lambda e: e.collective_compute(
            "AllReduce", ALU.add, replica_groups=[[0, 1], [2, 3], [4, 5], [6, 7]],
            ins=[self.cc_in[L].ap().opt()], outs=[self.cc_out[L].ap().opt()]),
            waits=[("xd", v_bounce)], inc="cc")
        v_cc = self.c["cc"]
        p.add("sync", lambda e: e.dma_start(out=kvsb, in_=self.cc_out[L][:, :]),
              waits=[("cc", v_cc)], inc="xd")
        v_kv_back = self.c["xd"]

        es_kv.close()
        psQ0 = es2.enter_context(nc.psum_tensor(f"psQ0_{L}", [128, 512], F32))
        psQ1 = es2.enter_context(nc.psum_tensor(f"psQ1_{L}", [128, 512], F32))
        q_evac = {}
        for i, (c, tb) in enumerate((c, tb) for c in range(DCH) for tb in range(NTB)):
            ps = psQ0 if i % 2 == 0 else psQ1

            def mk_q(e, c=c, tb=tb, ps=ps):
                for r in range(DCH):
                    inst = e.matmul(ps[:, :],
                                    self.Bx[:, r * 1024 + c * 128:
                                            r * 1024 + c * 128 + 128],
                                    self.stv(self.xb, r, tb * 512, 512),
                                    start=(r == 0), stop=(r == DCH - 1))
                return inst
            waits = [("wd", v_wq)]
            if i < 2:
                waits.append(("ve", v_kv_sb))
            if i >= 2:
                waits.append(("ve", q_evac[i - 2]))
            p.add("tensor", mk_q, waits=waits, inc="pe")
            v_q_mm = self.c["pe"]
            s0, s1 = self.scrF[:, 0:512], self.scrF[:, 512:1024]
            qout = self.stv(self.A, c, tb * 512, 512)
            if has_bias:
                bq = self.scrF[:, 1712 + c: 1713 + c]
                p.add("vector", lambda e, ps=ps, s0=s0, bq=bq: e.tensor_scalar(
                    s0, ps[:, :], bq, 0.0, ALU.add, ALU.min),
                    waits=[("pe", v_q_mm), ("xd", v_bq)], inc="ve")
                p.add("scalar", lambda e, s0=s0, s1=s1: e.activation(s1, s0, AF.Exp),
                      waits=[("ve", self.c["ve"])], inc="ae")
                p.add("scalar", lambda e, ps=ps, s0=s0, bq=bq: e.activation(
                    s0, ps[:, :], AF.Relu, bias=bq), inc="ae")
                p.add("vector", lambda e, s1=s1, s0=s0, qout=qout: e.tensor_tensor(
                    qout, s0, s1, ALU.add), waits=[("ae", self.c["ae"])], inc="ve")
            else:
                p.add("vector", lambda e, ps=ps, s0=s0: e.tensor_scalar(
                    s0, ps[:, :], 0.0, None, ALU.min),
                    waits=[("pe", v_q_mm)], inc="ve")
                p.add("scalar", lambda e, s0=s0, s1=s1: e.activation(s1, s0, AF.Exp),
                      waits=[("ve", self.c["ve"])], inc="ae")
                p.add("vector", lambda e, ps=ps, s1=s1, qout=qout:
                      e.scalar_tensor_tensor(qout, ps[:, :], 0.0, s1,
                                             ALU.max, ALU.add),
                      waits=[("ae", self.c["ae"])], inc="ve")
            q_evac[i] = self.c["ve"]
        v_q_done = self.c["ve"]
        p.add("sync", lambda e: e.dma_start(out=sel_ap, in_=self.d["c_sel16"].ap()),
              waits=[("ve", v_q_done)], inc="xd")
        v_sel = self.c["xd"]

        kvbf = self.scrB[:, KVB:KVB + 520]
        p.add("scalar", lambda e: e.activation(kvbf, kvsb, AF.Identity),
              waits=[("xd", v_kv_back)], inc="ae")
        v_kvbf = self.c["ae"]
        if getattr(self, "dbg_attn", 0) == 2:
            p.add("vector", lambda e: e.tensor_copy(self.stream[:, 2048:2568], kvsb),
                  waits=[("xd", v_kv_back)], inc="ve")
            self.v_stream = ("ve", self.c["ve"])
            self.w_free_W = self.c["pe"]
            self.w_free_Bx = self.c["pe"]
            p.emit()
            es2.close()
            return
        p.add("vector", lambda e: e.memset(self.scrB[:, BDG:BDG + 128], 0.0),
              waits=[("pe", v_kv_mm)], inc="ve")
        v_bz = self.c["ve"]
        for h in range(NH):
            pair, base = h // 2, (h % 2) * 64
            src = self.scrB[base:base + 64,
                            KVB + pair * 65 + 64: KVB + pair * 65 + 65]
            dst = self.scrB[base:base + 64,
                            BDG + (h // 2) * 16 + h: BDG + (h // 2) * 16 + h + 1]
            p.add("scalar", lambda e, src=src, dst=dst: e.activation(
                dst, src, AF.Identity),
                waits=[("ae", v_kvbf), ("ve", v_bz)], inc="ae")
        v_bdiag = self.c["ae"]

        psN = es2.enter_context(nc.psum_tensor(f"psN_{L}", [16, 512], F32))
        normF = self.scrF[0:16, 1024:1536]
        op = ALU.max if L == 0 else ALU.add
        qs_done = {}
        v_prev_bc = 0
        for tb in range(NTB):
            def mk_n(e, tb=tb):
                for r in range(DCH):
                    inst = e.matmul(psN[:, :],
                                    self.scrB[:, BDG + r * 16: BDG + r * 16 + 16],
                                    self.stv(self.A, r, tb * 512, 512),
                                    start=(r == 0), stop=(r == DCH - 1))
                return inst
            waits = [("ae", v_bdiag), ("ve", v_q_done)]
            p.add("tensor", mk_n, waits=waits, inc="pe")
            p.add("scalar", lambda e: e.activation(normF, psN[:, :], AF.Identity),
                  waits=[("pe", self.c["pe"]), ("pe", v_prev_bc)], inc="ae")
            v_ne = self.c["ae"]
            p.add("vector", lambda e: e.tensor_scalar(normF, normF, 1e-6, None, op),
                  waits=[("ae", v_ne)], inc="ve")
            p.add("vector", lambda e: e.reciprocal(normF, normF), inc="ve")
            v_recf = self.c["ve"]
            for c in range(DCH):
                i = c * NTB + tb
                ps = psK0 if c % 2 == 0 else psK1
                def mk_b(e, c=c, ps=ps):
                    return e.matmul(ps[:, :], sel_ap[:, c * 128:(c + 1) * 128],
                                    normF, start=True, stop=True)
                waits = [("ve", v_recf), ("xd", v_sel)]
                if c >= 2:
                    waits.append(("ve", qs_done[(c - 2) * NTB + tb]))
                p.add("tensor", mk_b, waits=waits, inc="pe")
                v_b_mm = self.c["pe"]
                p.add("vector", lambda e, c=c, tb=tb, ps=ps: e.tensor_tensor(
                    self.stv(self.xb, c, tb * 512, 512),
                    self.stv(self.A, c, tb * 512, 512), ps[:, :], ALU.mult),
                    waits=[("pe", v_b_mm)], inc="ve")
                qs_done[i] = self.c["ve"]
            v_prev_bc = self.c["pe"]

        at_done = {}
        for i, (pair, tb) in enumerate((pr, tb) for pr in range(DCH)
                                       for tb in range(NTB)):
            ps = psV0 if i % 2 == 0 else psV1
            def mk_o(e, pair=pair, tb=tb, ps=ps):
                for hh in range(2):
                    base = hh * 64
                    inst = e.matmul(
                        ps[base:base + 64, :],
                        self.scrB[base:base + 64,
                                  KVB + pair * 65: KVB + pair * 65 + 64],
                        self.xb[base:base + 64,
                                pair * T + tb * 512: pair * T + tb * 512 + 512],
                        start=True, stop=True)
                return inst
            waits = [("ae", v_kvbf), ("ve", qs_done[pair * NTB + tb])]
            if i >= 2:
                waits.append(("ae", at_done[i - 2]))
            p.add("tensor", mk_o, waits=waits, inc="pe")
            v_o_mm = self.c["pe"]
            p.add("scalar", lambda e, pair=pair, tb=tb, ps=ps: e.activation(
                self.stv(self.A, pair, tb * 512, 512), ps[:, :], AF.Identity),
                waits=[("pe", v_o_mm)], inc="ae")
            at_done[i] = self.c["ae"]
        v_at_done = self.c["ae"]
        if getattr(self, "dbg_attn", 0) == 1:
            p.add("scalar", lambda e: e.activation(self.stream[:, :],
                                                   self.A[:, :], AF.Identity),
                  waits=[("ae", v_at_done)], inc="ae")
            self.v_stream = ("ae", self.c["ae"])
            self.w_free_W = self.c["pe"]
            self.w_free_Bx = self.c["pe"]
            p.emit()
            es2.close()
            return

        v_wo = self.wload(p, self.W[:, 0:8192],
                          self.chunked(self.d[pre + "wo"].ap()), wait_pe=v_kv_mm)
        st_sem, st_val = self.v_stream
        res_done = {}
        for i, (c, tb) in enumerate((c, tb) for c in range(DCH) for tb in range(NTB)):
            ps = psQ0 if i % 2 == 0 else psQ1
            def mk_p(e, c=c, tb=tb, ps=ps):
                for r in range(DCH):
                    inst = e.matmul(ps[:, :],
                                    self.W[:, r * 1024 + c * 128:
                                           r * 1024 + c * 128 + 128],
                                    self.stv(self.A, r, tb * 512, 512),
                                    start=(r == 0), stop=(r == DCH - 1))
                return inst
            waits = [("wd", v_wo), ("ae", v_at_done)]
            if i >= 2:
                waits.append(("ve", res_done[i - 2]))
            p.add("tensor", mk_p, waits=waits, inc="pe")
            v_p_mm = self.c["pe"]
            ewaits = [("pe", v_p_mm), (st_sem, st_val)]
            if has_bias:
                bo = self.scrF[:, 1720 + c: 1721 + c]
                ewaits.append(("xd", v_bo))
                p.add("vector", lambda e, c=c, tb=tb, ps=ps, bo=bo:
                      e.scalar_tensor_tensor(
                          self.stv(self.stream, c, tb * 512, 512), ps[:, :], bo,
                          self.stv(self.stream, c, tb * 512, 512),
                          ALU.add, ALU.add), waits=ewaits, inc="ve")
            else:
                p.add("vector", lambda e, c=c, tb=tb, ps=ps: e.tensor_tensor(
                    self.stv(self.stream, c, tb * 512, 512), ps[:, :],
                    self.stv(self.stream, c, tb * 512, 512), ALU.add),
                    waits=ewaits, inc="ve")
            res_done[i] = self.c["ve"]
        self.v_stream = ("ve", self.c["ve"])
        self.w_free_W = self.c["pe"]
        self.w_free_Bx = self.c["pe"]
        p.emit()
        es2.close()

    def ph_ln(self, idx):
        nc = self.nc
        es2 = ExitStack()
        psS = es2.enter_context(nc.psum_tensor(f"psS_{idx}", [1, 512], F32))
        psQ = es2.enter_context(nc.psum_tensor(f"psQg_{idx}", [1, 512], F32))
        psA = es2.enter_context(nc.psum_tensor(f"psA_{idx}", [128, 512], F32))
        psB = es2.enter_context(nc.psum_tensor(f"psB_{idx}", [128, 512], F32))
        p = self.new_plan()
        st_sem, st_val = self.v_stream
        SQ = 3072
        v_apply_prev = 0
        for tb in range(NTB):
            sq_done = {}
            for c in range(DCH):
                p.add("scalar", lambda e, c=c, tb=tb: e.activation(
                    self.stv(self.A, c, tb * 512, 512),
                    self.stv(self.stream, c, tb * 512, 512), AF.Identity),
                    waits=[(st_sem, st_val)], inc="ae")
                v_cast = self.c["ae"]
                sq = self.scrB[:, SQ + (c % 2) * 512: SQ + (c % 2) * 512 + 512]
                waits = [("ae", v_cast)]
                if c >= 2:
                    waits.append(("pe", sq_done[c - 2]))
                p.add("scalar", lambda e, c=c, tb=tb, sq=sq: e.activation(
                    sq, self.stv(self.A, c, tb * 512, 512), AF.Square),
                    waits=waits, inc="ae")
                v_sq = self.c["ae"]
                p.add("tensor", lambda e, c=c, tb=tb: e.matmul(
                    psS[:, :], self.ones_col[:, :],
                    self.stv(self.A, c, tb * 512, 512),
                    start=(c == 0), stop=(c == DCH - 1)),
                    waits=[("ae", v_cast)] + ([("ve", v_apply_prev)] if c == 0 else []),
                    inc="pe")
                p.add("tensor", lambda e, c=c, sq=sq: e.matmul(
                    psQ[:, :], self.ones_col[:, :], sq,
                    start=(c == 0), stop=(c == DCH - 1)),
                    waits=[("ae", v_sq)], inc="pe")
                sq_done[c] = self.c["pe"]
            v_stats = self.c["pe"]
            m_row = self.scrF[0:1, 1024:1536]
            e2_row = self.scrF[0:1, 0:512]
            r_row = self.scrF[0:1, 512:1024]
            p.add("scalar", lambda e: e.activation(m_row, psS[0:1, :], AF.Identity,
                                                   scale=1.0 / D),
                  waits=[("pe", v_stats)], inc="ae")
            p.add("scalar", lambda e: e.activation(e2_row, psQ[0:1, :], AF.Identity,
                                                   scale=1.0 / D), inc="ae")
            v_rows = self.c["ae"]
            p.add("vector", lambda e: e.tensor_tensor(psS[0:1, :], m_row, m_row,
                                                      ALU.mult),
                  waits=[("ae", v_rows)], inc="ve")
            p.add("vector", lambda e: e.scalar_tensor_tensor(
                psQ[0:1, :], e2_row, 1e-5, psS[0:1, :], ALU.add, ALU.subtract),
                inc="ve")
            v_var = self.c["ve"]
            p.add("scalar", lambda e: e.activation(e2_row, psQ[0:1, :], AF.Ln),
                  waits=[("ve", v_var)], inc="ae")
            p.add("scalar", lambda e: e.activation(r_row, e2_row, AF.Exp,
                                                   scale=-0.5), inc="ae")
            v_r = self.c["ae"]
            p.add("vector", lambda e: e.scalar_tensor_tensor(
                m_row, m_row, -1.0, r_row, ALU.mult, ALU.mult),
                waits=[("ae", v_r)], inc="ve")
            v_b = self.c["ve"]
            onesF = self.scrF[0:1, 1552:1680]
            p.add("tensor", lambda e: e.matmul(psA[:, :], onesF[0:1, 0:128],
                                               r_row, start=True, stop=True),
                  waits=[("ae", v_r), ("ve", v_apply_prev),
                         ("ve", self.v_consts)], inc="pe")
            p.add("tensor", lambda e: e.matmul(psB[:, :], onesF[0:1, 0:128],
                                               m_row, start=True, stop=True),
                  waits=[("ve", v_b)], inc="pe")
            v_bc = self.c["pe"]
            for c in range(DCH):
                t0 = self.scrF[:, (c % 2) * 512: (c % 2) * 512 + 512]
                p.add("vector", lambda e, c=c, tb=tb, t0=t0: e.tensor_tensor(
                    t0, self.stv(self.stream, c, tb * 512, 512), psA[:, :],
                    ALU.mult), waits=[("pe", v_bc)], inc="ve")
                p.add("vector", lambda e, c=c, tb=tb, t0=t0: e.tensor_tensor(
                    self.stv(self.stream, c, tb * 512, 512), t0, psB[:, :],
                    ALU.add), inc="ve")
                v_app = self.c["ve"]
                p.add("scalar", lambda e, c=c, tb=tb: e.activation(
                    self.stv(self.xb, c, tb * 512, 512),
                    self.stv(self.stream, c, tb * 512, 512), AF.Identity),
                    waits=[("ve", v_app)], inc="ae")
            v_apply_prev = self.c["ve"]
        self.v_stream = ("ve", self.c["ve"])
        self.v_xb = ("ae", self.c["ae"])
        self.ln_pe_done = self.c["pe"]
        p.emit()
        es2.close()

    def ph_l0_moe(self):
        nc = self.nc
        es2 = ExitStack()
        p = self.new_plan()
        fence_pe = self.fence["pe"]
        xb_sem, xb_val = self.v_xb
        st_sem, st_val = self.v_stream

        es_gate = ExitStack()
        psL = es_gate.enter_context(nc.psum_tensor("psL0g", [128, 512], F32))
        psGB = es_gate.enter_context(nc.psum_tensor("psGB0", [128, 32], F32))
        gw_sb = self.scrB[:, 0:256]
        v_gw = self.wload(p, gw_sb, self.chunked(self.d["l0_gw"].ap()),
                          wait_pe=fence_pe)
        p.add("sync", lambda e: e.dma_start(out=self.scrF[0:1, 1680:1712],
                                            in_=self.d["l0_gb"].ap()), inc="xd")
        v_gbr = self.c["xd"]
        p.add("tensor", lambda e: e.matmul(psGB[:, :], self.scrF[0:1, 1552:1680],
                                           self.scrF[0:1, 1680:1712],
                                           start=True, stop=True),
              waits=[("xd", v_gbr), ("ve", self.v_consts)], inc="pe")
        gbb = self.scrF[:, 1728:1760]
        p.add("scalar", lambda e: e.activation(gbb, psGB[:, :], AF.Identity),
              waits=[("pe", self.c["pe"])], inc="ae")
        v_gbb = self.c["ae"]
        for tc in range(NTC):
            def mk_lg(e, tc=tc):
                for r in range(DCH):
                    inst = e.matmul(psL[:, tc * 32:(tc + 1) * 32],
                                    self.stv(self.xb, r, tc * 128, 128),
                                    gw_sb[:, r * 32:(r + 1) * 32],
                                    start=(r == 0), stop=(r == DCH - 1))
                return inst
            p.add("tensor", mk_lg,
                  waits=[("wd", v_gw)] + ([(xb_sem, xb_val)] if tc == 0 else []),
                  inc="pe")
        v_lg = self.c["pe"]

        scores = self.scrF[:, 0:512]
        sc = self.scrF[:, 512:1024]
        msk = self.scrF[:, 1024:1536]
        sel = self.scrB[:, 768:1280]
        eq = self.scrB[:, 256:768]
        p.add("scalar", lambda e: e.activation(scores, psL[:, :], AF.Sigmoid),
              waits=[("pe", v_lg)], inc="ae")
        v_scores = self.c["ae"]

        seg32 = lambda ap: bass.AP(ap.tensor, ap.offset,
                                   [ap.ap[0], [32, NTC], [1, 32]])
        g48 = lambda ap: bass.AP(ap.tensor, ap.offset,
                                 [ap.ap[0], [32, NTC], [8, 4], [1, 8]])
        sm = lambda lo, n, dims: self.fap(self.scrF, lo, n, dims)
        gm1 = sm(1760, 64, [[4, NTC], [1, 4]])
        gm1b = sm(1760, 64, [[4, NTC], [1, 4], [0, 8]])
        gm2 = sm(1824, 64, [[4, NTC], [1, 4]])
        gs = gm1
        gh = sm(1888, 16, [[1, NTC]])
        ghb = sm(1888, 16, [[1, NTC], [0, 4]])
        gmask = sm(1904, 64, [[4, NTC], [1, 4]])
        gmaskb = sm(1904, 64, [[4, NTC], [1, 4], [0, 8]])
        geq = self.fap(self.scrB, 1280, 64, [[4, NTC], [1, 4]])
        mx = sm(1968, 16, [[1, NTC]])
        mxb = sm(1968, 16, [[1, NTC], [0, 32]])

        def gb_b(ap):
            return bass.AP(self.scrF.ap().tensor, gbb.offset,
                           [self.scrF.ap().ap[0], [0, NTC], [1, 32]])
        p.add("vector", lambda e: e.tensor_tensor(seg32(sc), seg32(scores),
                                                  gb_b(None), ALU.add),
              waits=[("ae", max(v_scores, v_gbb))], inc="ve")
        p.add("vector", lambda e: e.reduce_max(gm1, g48(sc), axis=AX), inc="ve")
        p.add("vector", lambda e: e.tensor_tensor(g48(eq), g48(sc), gm1b, ALU.is_ge),
              inc="ve")
        p.add("vector", lambda e: e.scalar_tensor_tensor(msk, eq, -1e9, sc,
                                                         ALU.mult, ALU.add), inc="ve")
        p.add("vector", lambda e: e.reduce_max(gm2, g48(msk), axis=AX), inc="ve")
        p.add("vector", lambda e: e.tensor_tensor(gs, gm1, gm2, ALU.add), inc="ve")
        p.add("vector", lambda e: e.reduce_max(gh, gs, axis=AX), inc="ve")
        p.add("vector", lambda e: e.tensor_tensor(gmask, gs, ghb, ALU.is_ge),
              inc="ve")
        p.add("vector", lambda e: e.scalar_tensor_tensor(gs, gmask, -1e9, gs,
                                                         ALU.mult, ALU.add), inc="ve")
        p.add("vector", lambda e: e.reduce_max(gh, gs, axis=AX), inc="ve")
        p.add("vector", lambda e: e.tensor_tensor(geq, gs, ghb, ALU.is_ge), inc="ve")
        p.add("vector", lambda e: e.tensor_tensor(gmask, gmask, geq, ALU.add),
              inc="ve")
        p.add("vector", lambda e: e.tensor_tensor(g48(msk), g48(sc), gmaskb,
                                                  ALU.mult), inc="ve")
        for it in range(8):
            p.add("vector", lambda e: e.reduce_max(mx, seg32(msk), axis=AX),
                  inc="ve")
            p.add("vector", lambda e: e.tensor_tensor(seg32(eq), seg32(msk), mxb,
                                                      ALU.is_ge), inc="ve")
            if it == 0:
                p.add("vector", lambda e: e.tensor_copy(sel, eq), inc="ve")
            else:
                p.add("vector", lambda e: e.tensor_tensor(sel, sel, eq, ALU.add),
                      inc="ve")
            if it < 7:
                p.add("vector", lambda e: e.scalar_tensor_tensor(
                    msk, eq, -1e9, msk, ALU.mult, ALU.add), inc="ve")
        p.add("vector", lambda e: e.tensor_tensor(msk, sel, scores, ALU.mult),
              inc="ve")
        p.add("vector", lambda e: e.reduce_sum(mx, seg32(msk), axis=AX), inc="ve")
        p.add("vector", lambda e: e.tensor_scalar(mx, mx, 1e-20, None, ALU.add),
              inc="ve")
        p.add("vector", lambda e: e.reciprocal(mx, mx), inc="ve")
        p.add("vector", lambda e: e.tensor_scalar(mx, mx, 2.5, None, ALU.mult),
              inc="ve")
        p.add("vector", lambda e: e.tensor_tensor(seg32(msk), seg32(msk), mxb,
                                                  ALU.mult), inc="ve")
        comb = self.scrB[:, 256:768]
        p.add("vector", lambda e: e.tensor_copy(comb, msk), inc="ve")
        v_comb = self.c["ve"]
        es_gate.close()
        if getattr(self, "dbg_moe", 0) == 1:
            p.add("scalar", lambda e: e.activation(self.stream[:, 0:512], comb,
                                                   AF.Identity),
                  waits=[("ve", v_comb)], inc="ae")
            p.add("scalar", lambda e: e.activation(self.stream[:, 512:1024],
                                                   scores, AF.Identity), inc="ae")
            p.add("scalar", lambda e: e.activation(self.stream[:, 1024:1536],
                                                   sel, AF.Identity), inc="ae")
            v_dump = self.c["ae"]
            self.v_stream = ("ae", v_dump)
            self.w_free_W = self.c["pe"]
            self.w_free_Bx = self.c["pe"]
            p.emit()
            es2.close()
            return

        psG0 = es2.enter_context(nc.psum_tensor("l0psG0", [128, 512], F32))
        psG1 = es2.enter_context(nc.psum_tensor("l0psG1", [128, 512], F32))
        psU0 = es2.enter_context(nc.psum_tensor("l0psU0", [128, 512], F32))
        psU1 = es2.enter_context(nc.psum_tensor("l0psU1", [128, 512], F32))
        psW = es2.enter_context(nc.psum_tensor("l0psW", [128, 512], F32))
        v_sg = self.wload(p, self.W[:, 0:8192],
                          self.chunked(self.d["l0_sg"].ap()), wait_pe=self.w_free_W)
        v_su = self.wload(p, self.W[:, 8192:16384],
                          self.chunked(self.d["l0_su"].ap()))
        v_sd = self.wload(p, self.Bx[:, 0:8192],
                          self.chunked(self.d["l0_sd"].ap()), wait_pe=self.w_free_Bx)
        SIL = 1344
        ZP = 832
        WREP = 2368
        z_done = {}
        for i, (zc, tb) in enumerate((zc, tb) for zc in range(DCH)
                                     for tb in range(NTB)):
            psG = psG0 if i % 2 == 0 else psG1
            psU = psU0 if i % 2 == 0 else psU1
            def mk_g(e, zc=zc, tb=tb, ps=psG, base=0, vv=v_sg):
                for r in range(DCH):
                    inst = e.matmul(ps[:, :],
                                    self.W[:, base + r * 1024 + zc * 128:
                                           base + r * 1024 + zc * 128 + 128],
                                    self.stv(self.xb, r, tb * 512, 512),
                                    start=(r == 0), stop=(r == DCH - 1))
                return inst
            waits = [("wd", max(v_sg, v_su)), ("ve", v_comb)]
            if i >= 2:
                waits.append(("ve", z_done[i - 2]))
            p.add("tensor", lambda e, f=mk_g, ps=psG: f(e, ps=ps, base=0),
                  waits=waits, inc="pe")
            p.add("tensor", lambda e, f=mk_g, ps=psU: f(e, ps=ps, base=8192),
                  inc="pe")
            v_gu = self.c["pe"]
            sil = self.scrB[:, SIL + (i % 2) * 512: SIL + (i % 2) * 512 + 512]
            p.add("scalar", lambda e, ps=psG, sil=sil: e.activation(
                sil, ps[:, :], AF.Silu), waits=[("pe", v_gu)], inc="ae")
            p.add("vector", lambda e, zc=zc, tb=tb, ps=psU, sil=sil: e.tensor_tensor(
                self.stv(self.A, zc, tb * 512, 512), sil, ps[:, :], ALU.mult),
                waits=[("ae", self.c["ae"])], inc="ve")
            z_done[i] = self.c["ve"]
        v_zsh = self.c["ve"]
        v_zsh_pe = self.c["pe"]

        dsd = {}
        for i, (co, tb) in enumerate((co, tb) for co in range(DCH)
                                     for tb in range(NTB)):
            ps = psG0 if i % 2 == 0 else psG1
            def mk_d(e, co=co, tb=tb, ps=ps):
                for r in range(DCH):
                    inst = e.matmul(ps[:, :],
                                    self.Bx[:, r * 1024 + co * 128:
                                            r * 1024 + co * 128 + 128],
                                    self.stv(self.A, r, tb * 512, 512),
                                    start=(r == 0), stop=(r == DCH - 1))
                return inst
            waits = [("wd", v_sd), ("ve", v_zsh)]
            if i >= 2:
                waits.append(("ve", dsd[i - 2]))
            p.add("tensor", mk_d, waits=waits, inc="pe")
            v_d = self.c["pe"]
            p.add("vector", lambda e, co=co, tb=tb, ps=ps: e.tensor_tensor(
                self.stv(self.stream, co, tb * 512, 512), ps[:, :],
                self.stv(self.stream, co, tb * 512, 512), ALU.add),
                waits=[("pe", v_d), (st_sem, st_val)], inc="ve")
            dsd[i] = self.c["ve"]
        dstream_done = self.c["ve"]
        v_shared_pe = self.c["pe"]

        arena = {0: self.W, 1: self.Bx}
        egu_free = {0: v_zsh_pe, 1: v_zsh_pe}
        ed_free = {0: v_shared_pe, 1: v_shared_pe}
        v_wegu = {}
        v_wed = {}

        def load_egu(e_i):
            buf = arena[e_i % 2]
            v1 = self.wload(p, buf[:, 0:4096],
                            self.chunked(self.d["l0_eg"].ap()[e_i]),
                            wait_pe=egu_free[e_i % 2])
            v2 = self.wload(p, buf[:, 4096:8192],
                            self.chunked(self.d["l0_eu"].ap()[e_i]))
            v_wegu[e_i] = max(v1, v2)

        def load_ed(e_i):
            buf = arena[e_i % 2]
            v_wed[e_i] = self.wload(p, buf[:, 8192:12288],
                                    self.chunked(self.d["l0_ed"].ap()[e_i]),
                                    wait_pe=ed_free[e_i % 2])

        load_egu(0)
        load_ed(0)
        load_egu(1)
        load_ed(1)
        wex = lambda e_i, base, r, lo, n, cw: (
            arena[e_i % 2][:, base + r * cw + lo: base + r * cw + lo + n])

        v_wrep_prev = self.c["ae"]
        for pair_i in range(16):
            e0, e1 = 2 * pair_i, 2 * pair_i + 1
            for e_idx, e_i in enumerate((e0, e1)):
                for tb in range(NTB):
                    def mk_w(e, e_i=e_i, tb=tb):
                        for q in range(4):
                            tc = tb * 4 + q
                            col = comb[:, tc * 32 + e_i: tc * 32 + e_i + 1]
                            lhsT = bass.AP(col.tensor, col.offset,
                                           [col.ap[0], [0, 128]])
                            inst = e.matmul(psW[:, q * 128:(q + 1) * 128], lhsT,
                                            self.ident[:, :], start=True, stop=True)
                        return inst
                    p.add("tensor", mk_w,
                          waits=[("ve", self.c["ve"]), ("ae", self.c["ae"])],
                          inc="pe")
                    v_wmm = self.c["pe"]
                    p.add("scalar", lambda e, tb=tb: e.activation(
                        self.scrB[:, WREP + tb * 512: WREP + tb * 512 + 512],
                        psW[:, :], AF.Identity), waits=[("pe", v_wmm)], inc="ae")
                v_wrep = self.c["ae"]
                for i, (zc, tb) in enumerate((zc, tb) for zc in range(4)
                                             for tb in range(NTB)):
                    psG = psG0 if i % 2 == 0 else psG1
                    psU = psU0 if i % 2 == 0 else psU1
                    def mk_g2(e, e_i=e_i, zc=zc, tb=tb, ps=psG, base=0):
                        for r in range(DCH):
                            inst = e.matmul(ps[:, :],
                                            wex(e_i, base, r, zc * 128, 128, 512),
                                            self.stv(self.xb, r, tb * 512, 512),
                                            start=(r == 0), stop=(r == DCH - 1))
                        return inst
                    waits = [("wd", v_wegu[e_i])]
                    if i >= 2:
                        waits.append(("ve", z_done[i - 2]))
                    else:
                        waits.append(("ve", self.c["ve"]))
                    p.add("tensor", lambda e, f=mk_g2, ps=psG: f(e, ps=ps, base=0),
                          waits=waits, inc="pe")
                    p.add("tensor", lambda e, f=mk_g2, ps=psU: f(e, ps=ps, base=4096),
                          inc="pe")
                    v_gu = self.c["pe"]
                    sil = self.scrB[:, SIL + (i % 2) * 512: SIL + (i % 2) * 512 + 512]
                    p.add("scalar", lambda e, ps=psG, sil=sil: e.activation(
                        sil, ps[:, :], AF.Silu), waits=[("pe", v_gu)], inc="ae")
                    v_sil = self.c["ae"]
                    zp = self.scrB[:, ZP:ZP + 512]
                    p.add("vector", lambda e, ps=psU, sil=sil, zp=zp: e.tensor_tensor(
                        zp, sil, ps[:, :], ALU.mult), waits=[("ae", v_sil)], inc="ve")
                    p.add("vector", lambda e, e_idx=e_idx, zc=zc, tb=tb, zp=zp:
                          e.tensor_tensor(
                              self.stv(self.A, e_idx * 4 + zc, tb * 512, 512), zp,
                              self.scrB[:, WREP + tb * 512: WREP + tb * 512 + 512],
                              ALU.mult), waits=[("ae", v_wrep)], inc="ve")
                    z_done[i] = self.c["ve"]
                if getattr(self, "dbg_moe", 0) == 2 and e_i == 0:
                    p.add("scalar", lambda e: e.activation(
                        self.stream[:, 0:2048],
                        self.scrB[:, WREP:WREP + 2048], AF.Identity),
                        waits=[("ve", self.c["ve"])], inc="ae")
                    for zc in range(4):
                        p.add("scalar", lambda e, zc=zc: e.activation(
                            self.stream[:, (zc + 1) * 2048:(zc + 2) * 2048],
                            self.A[:, zc * 2048:(zc + 1) * 2048], AF.Identity),
                            inc="ae")
                    self.v_stream = ("ae", self.c["ae"])
                    self.w_free_W = self.c["pe"]
                    self.w_free_Bx = self.c["pe"]
                    p.emit()
                    es2.close()
                    return
                egu_free[e_i % 2] = self.c["pe"]
                if e_i + 2 < 32:
                    load_egu(e_i + 2)
            v_zpair = self.c["ve"]
            dsd = {}
            for i, (co, tb) in enumerate((co, tb) for co in range(DCH)
                                         for tb in range(NTB)):
                ps = psG0 if i % 2 == 0 else psG1
                def mk_d2(e, e0=e0, e1=e1, co=co, tb=tb, ps=ps):
                    k = 0
                    for e_i in (e0, e1):
                        for r in range(4):
                            inst = e.matmul(ps[:, :],
                                            wex(e_i, 8192, r, co * 128, 128, 1024),
                                            self.stv(self.A, (e_i % 2) * 4 + r,
                                                     tb * 512, 512),
                                            start=(k == 0), stop=(k == 7))
                            k += 1
                    return inst
                waits = [("ve", v_zpair), ("wd", max(v_wed[e0], v_wed[e1]))]
                if i >= 2:
                    waits.append(("ve", dsd[i - 2]))
                elif pair_i > 0:
                    waits.append(("ve", dstream_done))
                p.add("tensor", mk_d2, waits=waits, inc="pe")
                v_d = self.c["pe"]
                p.add("vector", lambda e, co=co, tb=tb, ps=ps: e.tensor_tensor(
                    self.stv(self.stream, co, tb * 512, 512), ps[:, :],
                    self.stv(self.stream, co, tb * 512, 512), ALU.add),
                    waits=[("pe", v_d)], inc="ve")
                dsd[i] = self.c["ve"]
            dstream_done = self.c["ve"]
            ed_free[0] = self.c["pe"]
            ed_free[1] = self.c["pe"]
            if e0 + 2 < 32:
                load_ed(e0 + 2)
                load_ed(e1 + 2)
        self.v_stream = ("ve", self.c["ve"])
        self.w_free_W = self.c["pe"]
        self.w_free_Bx = self.c["pe"]
        p.emit()
        es2.close()

    def ph_l1_moe(self):
        nc = self.nc
        es2 = ExitStack()
        p = self.new_plan()
        fence_pe = self.fence["pe"]
        xb_sem, xb_val = self.v_xb
        st_sem, st_val = self.v_stream

        es_gate = ExitStack()
        psL = es_gate.enter_context(nc.psum_tensor("psL1g", [128, 128], F32))
        psGB = es_gate.enter_context(nc.psum_tensor("psGB1", [128, 8], F32))
        gw_sb = self.scrB[:, 0:64]
        v_gw = self.wload(p, gw_sb, self.chunked(self.d["l1_gw"].ap()),
                          wait_pe=fence_pe)
        p.add("sync", lambda e: e.dma_start(out=self.scrF[0:1, 1680:1688],
                                            in_=self.d["l1_gb"].ap()), inc="xd")
        v_gbr = self.c["xd"]
        p.add("tensor", lambda e: e.matmul(psGB[:, :], self.scrF[0:1, 1552:1680],
                                           self.scrF[0:1, 1680:1688],
                                           start=True, stop=True),
              waits=[("xd", v_gbr)], inc="pe")
        gbb = self.scrF[:, 1728:1736]
        p.add("scalar", lambda e: e.activation(gbb, psGB[:, :], AF.Identity),
              waits=[("pe", self.c["pe"])], inc="ae")
        v_gbb = self.c["ae"]
        for tc in range(NTC):
            def mk_lg(e, tc=tc):
                for r in range(DCH):
                    inst = e.matmul(psL[:, tc * 8:(tc + 1) * 8],
                                    self.stv(self.xb, r, tc * 128, 128),
                                    gw_sb[:, r * 8:(r + 1) * 8],
                                    start=(r == 0), stop=(r == DCH - 1))
                return inst
            p.add("tensor", mk_lg,
                  waits=[("wd", v_gw)] + ([(xb_sem, xb_val)] if tc == 0 else []),
                  inc="pe")
        v_lg = self.c["pe"]
        lg = self.scrF[:, 0:128]
        smx = self.scrF[:, 128:256]
        sel = self.scrF[:, 256:384]
        msk = self.scrF[:, 384:512]
        eq = self.scrB[:, 64:192]
        seg8 = lambda ap: bass.AP(ap.tensor, ap.offset, [ap.ap[0], [8, NTC], [1, 8]])
        mx = self.fap(self.scrF, 1968, 16, [[1, NTC]])
        mxb = self.fap(self.scrF, 1968, 16, [[1, NTC], [0, 8]])
        p.add("scalar", lambda e: e.activation(lg, psL[:, :], AF.Identity),
              waits=[("pe", v_lg)], inc="ae")
        v_lgs = self.c["ae"]
        p.add("vector", lambda e: e.reduce_max(mx, seg8(lg), axis=AX),
              waits=[("ae", v_lgs)], inc="ve")
        p.add("vector", lambda e: e.tensor_tensor(seg8(smx), seg8(lg), mxb,
                                                  ALU.subtract), inc="ve")
        v_sub = self.c["ve"]
        p.add("scalar", lambda e: e.activation(smx, smx, AF.Exp),
              waits=[("ve", v_sub)], inc="ae")
        v_exp = self.c["ae"]
        p.add("vector", lambda e: e.reduce_sum(mx, seg8(smx), axis=AX),
              waits=[("ae", v_exp)], inc="ve")
        p.add("vector", lambda e: e.reciprocal(mx, mx), inc="ve")
        p.add("vector", lambda e: e.tensor_tensor(seg8(smx), seg8(smx), mxb,
                                                  ALU.mult), inc="ve")
        def gb_b(_):
            return bass.AP(self.scrF.ap().tensor, gbb.offset,
                           [self.scrF.ap().ap[0], [0, NTC], [1, 8]])
        p.add("vector", lambda e: e.tensor_tensor(seg8(msk), seg8(smx), gb_b(None),
                                                  ALU.add),
              waits=[("ae", v_gbb)], inc="ve")
        for it in range(2):
            p.add("vector", lambda e: e.reduce_max(mx, seg8(msk), axis=AX), inc="ve")
            p.add("vector", lambda e: e.tensor_tensor(seg8(eq), seg8(msk), mxb,
                                                      ALU.is_ge), inc="ve")
            if it == 0:
                p.add("vector", lambda e: e.tensor_copy(sel, eq), inc="ve")
                p.add("vector", lambda e: e.scalar_tensor_tensor(
                    msk, eq, -1e9, msk, ALU.mult, ALU.add), inc="ve")
            else:
                p.add("vector", lambda e: e.tensor_tensor(sel, sel, eq, ALU.add),
                      inc="ve")
        p.add("vector", lambda e: e.tensor_tensor(msk, sel, smx, ALU.mult), inc="ve")
        comb = self.scrB[:, 64:192]
        p.add("vector", lambda e: e.tensor_copy(comb, msk), inc="ve")
        v_comb = self.c["ve"]
        es_gate.close()
        if getattr(self, "dbg_l1", 0) == 1:
            p.add("scalar", lambda e: e.activation(self.stream[:, 0:128], comb,
                                                   AF.Identity),
                  waits=[("ve", v_comb)], inc="ae")
            p.add("scalar", lambda e: e.activation(self.stream[:, 2048:2176],
                                                   smx, AF.Identity), inc="ae")
            p.add("scalar", lambda e: e.activation(self.stream[:, 4096:4224],
                                                   sel, AF.Identity), inc="ae")
            p.add("scalar", lambda e: e.activation(self.stream[:, 6144:6272],
                                                   lg, AF.Identity), inc="ae")
            self.v_stream = ("ae", self.c["ae"])
            self.w_free_W = self.c["pe"]
            self.w_free_Bx = self.c["pe"]
            p.emit()
            es2.close()
            return

        psY10 = es2.enter_context(nc.psum_tensor("psY10", [128, 512], F32))
        psY11 = es2.enter_context(nc.psum_tensor("psY11", [128, 512], F32))
        psY30 = es2.enter_context(nc.psum_tensor("psY30", [128, 512], F32))
        psY31 = es2.enter_context(nc.psum_tensor("psY31", [128, 512], F32))
        psD0 = es2.enter_context(nc.psum_tensor("psD0", [128, 512], F32))
        psD1 = es2.enter_context(nc.psum_tensor("psD1", [128, 512], F32))
        psW = es2.enter_context(nc.psum_tensor("psWr1", [128, 512], F32))
        E1, G, GM, LC, WREP, LS = 192, 704, 1216, 1728, 2240, 1216
        w1b = {s: s * 2048 for s in range(4)}
        w3b = {s: 8192 + s * 2048 for s in range(4)}
        slot_free = {s: self.w_free_W for s in range(4)}
        w2_free = self.w_free_Bx
        b2_free = 0
        B1C = {0: 1984, 1: 2016}
        B3C = {0: 2048, 1: 2080}
        dstream_done = 0
        self.yi = 0
        yz_done = {0: v_comb, 1: v_comb}
        v_wrep_guard = self.c["ae"]
        for e_i in range(9):
            sl = e_i % 2
            p.add("sync", lambda e, e_i=e_i, sl=sl: e.dma_start(
                out=self.scrF[:, B1C[sl]:B1C[sl] + 32],
                in_=self.d["l1_b1"].ap()[e_i]),
                waits=[("pe", fence_pe if e_i < 2 else b2_free)], inc="xd")
            p.add("sync", lambda e, e_i=e_i, sl=sl: e.dma_start(
                out=self.scrF[:, B3C[sl]:B3C[sl] + 32],
                in_=self.d["l1_b3"].ap()[e_i]), inc="xd")
            v_bias = self.c["xd"]
            v_b2 = self.wload(p, self.rowB[0:1, 1024:2048],
                              self.d["l1_b2"].ap()[e_i], wait_pe=b2_free)
            if e_i < 8:
                for tb in range(NTB):
                    def mk_w(e, e_i=e_i, tb=tb):
                        for q in range(4):
                            tc = tb * 4 + q
                            col = comb[:, tc * 8 + e_i: tc * 8 + e_i + 1]
                            lhsT = bass.AP(col.tensor, col.offset,
                                           [col.ap[0], [0, 128]])
                            inst = e.matmul(psW[:, q * 128:(q + 1) * 128], lhsT,
                                            self.ident[:, :], start=True, stop=True)
                        return inst
                    p.add("tensor", mk_w,
                          waits=[("ve", self.c["ve"]), ("ae", self.c["ae"])],
                          inc="pe")
                    v_wmm = self.c["pe"]
                    p.add("scalar", lambda e, tb=tb: e.activation(
                        self.scrB[:, WREP + tb * 512: WREP + tb * 512 + 512],
                        psW[:, :], AF.Identity), waits=[("pe", v_wmm)], inc="ae")
                v_wrep = self.c["ae"]
            for half in range(2):
                v_w2h = self.wload(
                    p, self.Bx[:, 0:8192],
                    self.chunked(self.d["l1_w2"].ap()
                                 [e_i, half * 1024:(half + 1) * 1024, :]),
                    wait_pe=w2_free)
                h_done = 0
                for bj in range(8):
                    ofs = half * 2048 + bj * 256
                    sb2 = (half * 8 + bj) % 4
                    v_w1 = self.wload(
                        p, self.W[:, w1b[sb2]:w1b[sb2] + 2048],
                        self.chunked(self.d["l1_w1"].ap()[e_i][:, ofs:ofs + 256]),
                        wait_pe=slot_free[sb2])
                    v_w3 = self.wload(
                        p, self.W[:, w3b[sb2]:w3b[sb2] + 2048],
                        self.chunked(self.d["l1_w3"].ap()[e_i][:, ofs:ofs + 256]))
                    v_wblk = max(v_w1, v_w3)
                    for tb in range(NTB):
                        if True:
                            j = bj % 2
                            prod_v = {}
                            for gl in range(2):
                                i2 = self.yi % 2
                                self.yi += 1
                                psY1 = psY10 if i2 == 0 else psY11
                                psY3 = psY30 if i2 == 0 else psY31
                                def mk_y(e, sb2=sb2, gl=gl, tb=tb, ps=None, w13=0):
                                    base = (w1b if w13 == 0 else w3b)[sb2]
                                    for r in range(DCH):
                                        inst = e.matmul(
                                            ps[:, :],
                                            self.W[:, base + r * 256 + gl * 128:
                                                   base + r * 256 + gl * 128 + 128],
                                            self.stv(self.xb, r, tb * 512, 512),
                                            start=(r == 0), stop=(r == DCH - 1))
                                    return inst
                                waits = [("wd", v_wblk), ("ve", yz_done[i2])]
                                if xb_sem == "ae":
                                    waits.append((xb_sem, xb_val))
                                p.add("tensor",
                                      lambda e, f=mk_y, ps=psY1: f(e, ps=ps, w13=0),
                                      waits=waits, inc="pe")
                                p.add("tensor",
                                      lambda e, f=mk_y, ps=psY3: f(e, ps=ps, w13=1),
                                      inc="pe")
                                v_y = self.c["pe"]
                                bcol = half * 16 + bj * 2 + gl
                                b1c = self.scrF[:, B1C[sl] + bcol: B1C[sl] + bcol + 1]
                                b3c = self.scrF[:, B3C[sl] + bcol: B3C[sl] + bcol + 1]
                                e1 = self.scrB[:, E1 + i2 * 256: E1 + i2 * 256 + 256] \
                                    if False else self.scrB[:, E1:E1 + 512]
                                ew = [("pe", v_y), ("xd", v_bias),
                                      ("ve", h_done)]
                                if gl == 1:
                                    ew.append(("ve", prod_v[0]))
                                p.add("scalar", lambda e, ps=psY1, e1=e1, b1c=b1c:
                                      e.activation(e1, ps[:, :], AF.Identity,
                                                   bias=b1c),
                                      waits=ew, inc="ae")
                                v_e1 = self.c["ae"]
                                dst = self.scrB[:, (G if gl == 0 else LC):
                                                (G if gl == 0 else LC) + 512]
                                p.add("vector", lambda e, ps=psY3, e1=e1, dst=dst,
                                      b3c=b3c: e.scalar_tensor_tensor(
                                          dst, ps[:, :], b3c, e1, ALU.add, ALU.mult),
                                      waits=[("ae", v_e1)], inc="ve")
                                prod_v[gl] = self.c["ve"]
                                yz_done[i2] = self.c["ve"]
                            if (getattr(self, "dbg_l1", 0) == 3 and e_i == 0
                                    and half == 0 and bj == 0 and tb == 0):
                                p.add("scalar", lambda e: e.activation(
                                    self.stream[:, 0:512],
                                    self.scrB[:, G:G + 512], AF.Identity),
                                    waits=[("ve", prod_v[1])], inc="ae")
                                p.add("scalar", lambda e: e.activation(
                                    self.stream[:, 2048:2560],
                                    self.scrB[:, LC:LC + 512], AF.Identity),
                                    inc="ae")
                                p.add("scalar", lambda e: e.activation(
                                    self.stream[:, 4096:4608],
                                    self.scrB[:, WREP:WREP + 512], AF.Identity),
                                    inc="ae")
                                self.v_stream = ("ae", self.c["ae"])
                                self.w_free_W = self.c["pe"]
                                self.w_free_Bx = self.c["pe"]
                                p.emit()
                                es2.close()
                                return
                            gm = self.scrB[:, GM:GM + 512]
                            p.add("vector", lambda e, gm=gm: e.tensor_scalar(
                                gm, self.scrB[:, G:G + 512], 7.0, None, ALU.min),
                                waits=[("ve", prod_v[0])], inc="ve")
                            v_gm = self.c["ve"]
                            fg = self.scrB[:, E1:E1 + 512]
                            p.add("scalar", lambda e, gm=gm, fg=fg: e.activation(
                                fg, gm, AF.Silu, scale=1.702),
                                waits=[("ve", v_gm)], inc="ae")
                            v_fg = self.c["ae"]
                            ls = self.scrB[:, LS:LS + 512]
                            p.add("vector", lambda e, ls=ls: e.tensor_scalar(
                                ls, self.scrB[:, LC:LC + 512], 7.0, -7.0,
                                ALU.min, ALU.max),
                                waits=[("ve", prod_v[1]), ("ae", v_fg)],
                                inc="ve")
                            p.add("vector", lambda e, ls=ls: e.tensor_scalar(
                                ls, ls, 1.0, 1.0 / 1.702, ALU.add, ALU.mult),
                                inc="ve")
                            v_ls = self.c["ve"]
                            if e_i < 8:
                                p.add("vector", lambda e, ls=ls, tb=tb:
                                      e.tensor_tensor(
                                          ls, ls,
                                          self.scrB[:, WREP + tb * 512:
                                                    WREP + tb * 512 + 512],
                                          ALU.mult),
                                      waits=[("ae", v_wrep)], inc="ve")
                                v_ls = self.c["ve"]
                            hc = bj
                            p.add("vector", lambda e, hc=hc, tb=tb, fg=fg, ls=ls:
                                  e.tensor_tensor(
                                      self.stv(self.A, hc, tb * 512, 512), fg, ls,
                                      ALU.mult),
                                  waits=[("ae", v_fg)], inc="ve")
                            h_done = self.c["ve"]
                            if (getattr(self, "dbg_l1", 0) == 4 and e_i == 0
                                    and half == 0 and bj == 0 and tb == 0):
                                p.add("scalar", lambda e, fg=fg: e.activation(
                                    self.stream[:, 0:512], fg, AF.Identity),
                                    waits=[("ae", v_fg), ("ve", h_done)], inc="ae")
                                p.add("scalar", lambda e, ls=ls: e.activation(
                                    self.stream[:, 2048:2560], ls, AF.Identity),
                                    inc="ae")
                                p.add("scalar", lambda e: e.activation(
                                    self.stream[:, 4096:4608],
                                    self.stv(self.A, 0, 0, 512), AF.Identity),
                                    inc="ae")
                                self.v_stream = ("ae", self.c["ae"])
                                self.w_free_W = self.c["pe"]
                                self.w_free_Bx = self.c["pe"]
                                p.emit()
                                es2.close()
                                return
                    slot_free[sb2] = self.c["pe"]
                v_h_half = h_done
                if getattr(self, "dbg_l1", 0) == 2 and e_i == 0 and half == 0:
                    p.add("scalar", lambda e: e.activation(
                        self.stream[:, 0:16384], self.A[:, 0:16384], AF.Identity),
                        waits=[("ve", v_h_half)], inc="ae")
                    self.v_stream = ("ae", self.c["ae"])
                    self.w_free_W = self.c["pe"]
                    self.w_free_Bx = self.c["pe"]
                    p.emit()
                    es2.close()
                    return
                dsd = {}
                for i, (co, tb) in enumerate((co, tb) for co in range(DCH)
                                             for tb in range(NTB)):
                    ps = psD0 if i % 2 == 0 else psD1
                    def mk_d(e, e_i=e_i, co=co, tb=tb, ps=ps, half=half):
                        for r in range(8):
                            inst = e.matmul(
                                ps[:, :],
                                self.Bx[:, r * 1024 + co * 128:
                                        r * 1024 + co * 128 + 128],
                                self.stv(self.A, r, tb * 512, 512),
                                start=(r == 0),
                                stop=(r == 7 and half == 1))
                        if half == 0:
                            rhs = (self.scrB[0:1, WREP + tb * 512:
                                             WREP + tb * 512 + 512]
                                   if e_i < 8 else self.ones_row[0:1, 256:768])
                            inst = e.matmul(ps[:, :],
                                            self.rowB[0:1, 1024 + co * 128:
                                                      1024 + co * 128 + 128],
                                            rhs, start=False, stop=True)
                        return inst
                    waits = [("ve", v_h_half), ("wd", max(v_w2h, v_b2))]
                    if i >= 2:
                        waits.append(("ve", dsd[i - 2]))
                    else:
                        waits.append(("ve", dstream_done))
                    p.add("tensor", mk_d, waits=waits, inc="pe")
                    v_d = self.c["pe"]
                    p.add("vector", lambda e, co=co, tb=tb, ps=ps: e.tensor_tensor(
                        self.stv(self.stream, co, tb * 512, 512), ps[:, :],
                        self.stv(self.stream, co, tb * 512, 512), ALU.add),
                        waits=[("pe", v_d), (st_sem, st_val)], inc="ve")
                    dsd[i] = self.c["ve"]
                    dstream_done = self.c["ve"]
                w2_free = self.c["pe"]
            b2_free = self.c["pe"]
        self.v_stream = ("ve", self.c["ve"])
        self.w_free_W = self.c["pe"]
        self.w_free_Bx = self.c["pe"]
        p.emit()
        es2.close()

    def ph_store(self):
        p = Plan(self)
        sem, val = self.v_stream
        p.add("sync", lambda e: e.dma_start(
            out=self.out.ap().rearrange("(c p) t -> p c t", p=128),
            in_=self.stream[:, :].rearrange("p (c t) -> p c t", t=T)),
            waits=[(sem, val)], inc="xd")
        p.wait("sync", [("xd", self.c["xd"])])
        p.emit()

    def build(self):
        self.ph_load()
        phases = [
            lambda: self.ph_attn(0),
            lambda: self.ph_ln(0),
            lambda: self.ph_l0_moe(),
            lambda: self.ph_ln(1),
            lambda: self.ph_attn(1),
            lambda: self.ph_ln(2),
            lambda: self.ph_l1_moe(),
            lambda: self.ph_ln(3),
        ]
        for i, ph in enumerate(phases):
            if i >= self.upto:
                break
            ph()
        self.ph_store()
        return self.nc


def host_prep(inputs):
    f = np.ascontiguousarray
    H = {}
    H["c_ident"] = np.eye(128, dtype=np.float32)
    sel = np.zeros((16, 1024), dtype=np.float32)
    for c in range(8):
        sel[2 * c, c * 128: c * 128 + 64] = 1.0
        sel[2 * c + 1, c * 128 + 64: c * 128 + 128] = 1.0
    H["c_sel16"] = sel
    H["l0_wq"] = f(inputs["L0_attn_Wq"].T)
    H["l0_wk"] = f(inputs["L0_attn_Wk"].T)
    H["l0_wv"] = f(inputs["L0_attn_Wv"].T)
    H["l0_wo"] = f(inputs["L0_attn_Wo"].T)
    H["l0_bq"] = f(inputs["L0_attn_bq"].reshape(8, 128).T)
    H["l0_bo"] = f(inputs["L0_attn_bo"].reshape(8, 128).T)
    H["l0_bk"] = f(inputs["L0_attn_bk"].reshape(1, 1024))
    H["l0_bv"] = f(inputs["L0_attn_bv"].reshape(1, 1024))
    H["l0_gw"] = f(inputs["L0_ffn_gate_w"].T)
    H["l0_gb"] = f(inputs["L0_ffn_gate_bias"].reshape(1, 32))
    H["l0_eg"] = f(inputs["L0_ffn_expert_gate"].transpose(0, 2, 1))
    H["l0_eu"] = f(inputs["L0_ffn_expert_up"].transpose(0, 2, 1))
    H["l0_ed"] = f(inputs["L0_ffn_expert_down"].transpose(0, 2, 1))
    H["l0_sg"] = f(inputs["L0_ffn_shared_gate"].T)
    H["l0_su"] = f(inputs["L0_ffn_shared_up"].T)
    H["l0_sd"] = f(inputs["L0_ffn_shared_down"].T)
    H["l1_wq"] = f(inputs["L1_attn_Wq"])
    H["l1_wk"] = f(inputs["L1_attn_Wk"])
    H["l1_wv"] = f(inputs["L1_attn_Wv"])
    H["l1_wo"] = f(inputs["L1_attn_Wo"])
    H["l1_gw"] = f(inputs["L1_ffn_gate_w"].T)
    H["l1_gb"] = f(inputs["L1_ffn_gate_b"].reshape(1, 8))
    perm = np.concatenate([512 * B + np.concatenate(
        [np.arange(0, 256, 2), np.arange(1, 256, 2),
         np.arange(256, 512, 2), np.arange(257, 512, 2)]) for B in range(8)])
    w1 = inputs["L1_ffn_w1"][:, :, perm]
    w3 = inputs["L1_ffn_w3"][:, :, perm]
    b1 = inputs["L1_ffn_b1"][:, perm]
    b3 = inputs["L1_ffn_b3"][:, perm]
    H["l1_w1"] = f(np.concatenate([w1, inputs["L1_ffn_sw1"][:, perm][None]], axis=0))
    H["l1_w3"] = f(np.concatenate([w3, inputs["L1_ffn_sw3"][:, perm][None]], axis=0))
    H["l1_w2"] = f(np.concatenate([inputs["L1_ffn_w2"],
                                   inputs["L1_ffn_sw2"][None]], axis=0))
    b1a = np.concatenate([b1, inputs["L1_ffn_sb1"][perm][None]], axis=0)
    b3a = np.concatenate([b3, inputs["L1_ffn_sb3"][perm][None]], axis=0)
    H["l1_b1"] = f(b1a.reshape(9, 32, 128).transpose(0, 2, 1))
    H["l1_b3"] = f(b3a.reshape(9, 32, 128).transpose(0, 2, 1))
    H["l1_b2"] = f(np.concatenate([inputs["L1_ffn_b2"],
                                   inputs["L1_ffn_sb2"][None]], axis=0)
                   .reshape(9, 1, 1024))
    return H


def build_kernel(upto=99):
    return MK(upto=upto).build()


def kernel(**inputs):
    H = host_prep(inputs)
    x = inputs["L0_attn_x"].reshape(16384, 1024)
    xT = np.ascontiguousarray(x.T)
    nc = build_kernel()
    in_maps = []
    for c in range(NCORES):
        m = dict(H)
        m["xT"] = np.ascontiguousarray(xT[:, c * T:(c + 1) * T])
        in_maps.append(m)
    res = run_bass_kernel_spmd(nc, in_maps, core_ids=list(range(NCORES)))
    outs = [np.asarray(r["out"]).T for r in res.results]
    return np.concatenate(outs, axis=0).reshape(4, 4096, 1024)
